# version 17
# baseline (speedup 1.0000x reference)
import numpy as np
import contextlib
import concourse.bass as bass
import concourse.mybir as mybir
from concourse.bass_utils import run_bass_kernel_spmd

F32 = mybir.dt.float32
BF16 = mybir.dt.bfloat16
I32 = mybir.dt.int32
AF = mybir.ActivationFunctionType
ALU = mybir.AluOpType
AX = mybir.AxisListType

D = 1024
TC = 2048
TB = 4096
NH, HD = 8, 256
DQ = NH * HD
LR = 64
IF = 2816
KC = D // 128
SCALE = 1.0 / 16.0


class Sem:
    def __init__(self, nc, es, name):
        self.h = es.enter_context(nc.semaphore(name))
        self.n = 0


class DmaStream:

    def __init__(self, nc, es, name, nslots=2):
        self.sems = [Sem(nc, es, f"{name}{i}") for i in range(nslots)]
        self.cnt = 0
        self.nslots = nslots

    def issue(self, eb, qeng, out, in_):
        s = self.sems[self.cnt % self.nslots]
        s.n += 16
        eb.em(qeng, lambda e, out=out, in_=in_, s=s: e.dma_start(out=out, in_=in_).then_inc(s.h, 16))
        self.cnt += 1

    def wait_for(self, eb, eng, i):
        s = self.sems[i % self.nslots]
        eb.wait(eng, s, 16 * (i // self.nslots + 1))

    def wait_all(self, eb, eng):
        for s in self.sems:
            eb.wait(eng, s, s.n)


class EB:
    ENGS = ("sync", "tensor", "vector", "scalar", "gpsimd")

    def __init__(self, nc):
        self.nc = nc
        self.q = {e: [] for e in self.ENGS}

    def em(self, eng, fn):
        self.q[eng].append(fn)

    def wait(self, eng, sem, v):
        if v > 0:
            self.em(eng, lambda e: e.wait_ge(sem.h, v))

    def flush(self, blk):
        for eng in self.ENGS:
            lst = self.q[eng]
            if not lst:
                continue
            dec = getattr(blk, eng)

            def make(lst):
                def _(e):
                    for f in lst:
                        f(e)
                return _
            dec(make(lst))
        self.q = {e: [] for e in self.ENGS}


def build_program(debug=False, stop_after=9):
    import os
    B1_NT = int(os.environ.get('B1_NT', '32'))
    B1_QUANT = int(os.environ.get('B1_QUANT', '1'))
    B1_TRANS = int(os.environ.get('B1_TRANS', '1'))
    B1_WQ = int(os.environ.get('B1_WQ', '1'))
    B1_SPILL = int(os.environ.get('B1_SPILL', '1'))
    nc = bass.Bass("TRN2", target_bir_lowering=False)
    x_tok = nc.dram_tensor("x_tok", [TB, D], F32, kind="ExternalInput").ap()
    xT_own = nc.dram_tensor("xT_own", [D, TC], F32, kind="ExternalInput").ap()
    WqT = nc.dram_tensor("WqT", [D, DQ], F32, kind="ExternalInput").ap()
    WkvT = nc.dram_tensor("WkvT", [D, LR], F32, kind="ExternalInput").ap()
    WkuT = nc.dram_tensor("WkuT", [LR, DQ], F32, kind="ExternalInput").ap()
    WvuT = nc.dram_tensor("WvuT", [LR, DQ], F32, kind="ExternalInput").ap()
    WoT = nc.dram_tensor("WoT", [DQ, D], F32, kind="ExternalInput").ap()
    Wg = nc.dram_tensor("Wg", [IF, D], F32, kind="ExternalInput").ap()
    Wu = nc.dram_tensor("Wu", [IF, D], F32, kind="ExternalInput").ap()
    Wd = nc.dram_tensor("Wd", [D, IF], F32, kind="ExternalInput").ap()
    identD = nc.dram_tensor("identD", [128, 128], F32, kind="ExternalInput").ap()
    o_out = nc.dram_tensor("o_out", [D, TC], F32, kind="ExternalOutput").ap()
    WgqN = nc.dram_tensor("WgqN", [IF, D], BF16, kind="Internal").ap()
    WuqN = nc.dram_tensor("WuqN", [IF, D], BF16, kind="Internal").ap()
    WdqN = nc.dram_tensor("WdqN", [D, IF], BF16, kind="Internal").ap()
    xqT_scr = nc.dram_tensor("xqT_scr", [D, TC], BF16, kind="Internal").ap()
    resid_scr = nc.dram_tensor("resid_scr", [D, TC], F32, kind="Internal").ap()
    dbg = {}
    if debug:
        dbg['xnT'] = nc.dram_tensor("d_xnT", [128, KC, TB], BF16, kind="ExternalOutput").ap()
        dbg['kvT'] = nc.dram_tensor("d_kvT", [LR, TB], BF16, kind="ExternalOutput").ap()
        dbg['qT'] = nc.dram_tensor("d_qT", [128, 16, TC], BF16, kind="ExternalOutput").ap()
        dbg['kT0'] = nc.dram_tensor("d_kT0", [128, 2, TB], BF16, kind="ExternalOutput").ap()
        dbg['oall'] = nc.dram_tensor("d_oall", [128, 16, TC], BF16, kind="ExternalOutput").ap()
        dbg['sums'] = nc.dram_tensor("d_sums", [NH, TC], F32, kind="ExternalOutput").ap()
        dbg['resid'] = nc.dram_tensor("d_resid", [D, TC], F32, kind="ExternalOutput").ap()
        dbg['xq'] = nc.dram_tensor("d_xq", [D, TC], BF16, kind="ExternalOutput").ap()
        dbg['hq0'] = nc.dram_tensor("d_hq0", [128, 11, 1024], BF16, kind="ExternalOutput").ap()
        dbg['sml'] = nc.dram_tensor("d_sml", [128, 16], F32, kind="ExternalOutput").ap()
        dbg['bnd'] = nc.dram_tensor("d_bnd", [128, 12], F32, kind="ExternalOutput").ap()

    es_sh = contextlib.ExitStack()

    def sb(name, shape, dt, es):
        return es.enter_context(nc.sbuf_tensor(name, shape, dt))

    def ps_(shape, dt, es, name):
        return es.enter_context(nc.psum_tensor(name, shape, dt))

    S = {}
    for s in ["x_in", "lnv", "lna", "lnd", "lnap", "xq", "tr", "trev",
              "xqs", "win", "wqd", "wqs", "wql", "kvev", "qpj", "wqt", "sco", "exp",
              "avq", "oev", "bnd", "b12", "b23", "b34", "b45", "wo16",
              "xt16", "rs", "rr", "out", "fw", "xqr", "kv_pe", "q_pe", "prep", "pbev",
              "ffh", "sil", "hbld", "hq", "fftr", "fftrev", "ffdn", "ssq", "srw", "wkl", "lad", "aab", "fw2"]:
        S[s] = Sem(nc, es_sh, "s_" + s)

    DS = {}
    for nm, nsl in [("x_in", 2), ("win", 2), ("wqs", 2), ("wql", 2), ("wkl", 4),
                    ("xt", 2), ("rr", 2), ("rsd", 2), ("ost", 2), ("ssq", 1)]:
        DS[nm] = DmaStream(nc, es_sh, "d_" + nm, nsl)

    pbm = sb("pbm", [128, 96], F32, es_sh)
    pscale = sb("pscale", [128, 96], F32, es_sh)
    prcp = sb("prcp", [128, 96], F32, es_sh)
    ident_t = sb("ident_t", [128, 128], F32, es_sh)
    ones_f1 = sb("ones_f1", [1, 128], F32, es_sh)
    kvT_b = sb("kvT_b", [LR, TB], BF16, es_sh)
    sums8 = sb("sums8", [NH, TC], F32, es_sh)
    sumc = sb("sumc", [128, 2], F32, es_sh)
    sqc = sb("sqc", [128, 2], F32, es_sh)
    mc = sb("mc", [128, 2], F32, es_sh)
    msqc = sb("msqc", [128, 2], F32, es_sh)
    varc = sb("varc", [128, 2], F32, es_sh)
    sdc = sb("sdc", [128, 2], F32, es_sh)
    istdc = sb("istdc", [128, 2], F32, es_sh)
    negmsc = sb("negmsc", [128, 2], F32, es_sh)
    bt1 = sb("bt1", [1, 8], F32, es_sh)
    bq2 = sb("bq2", [1, 8], F32, es_sh)
    ones_b128 = sb("ones_b128", [128, 1], BF16, es_sh)
    eps_col = sb("eps_col", [128, 1], F32, es_sh)
    negB_col = sb("negB_col", [128, 1], F32, es_sh)

    es_wq = contextlib.ExitStack()
    wbuf = sb("wbuf", [128, 2, 512], F32, es_wq)
    wqout = sb("wqout", [128, 2, 512], BF16, es_wq)
    srowq = sb("srowq", [1, 1, 512], F32, es_wq)

    es_ps = contextlib.ExitStack()
    psA = ps_([128, 512], F32, es_ps, "psA")
    psB = ps_([128, 512], F32, es_ps, "psB")
    psC0 = ps_([128, 512], F32, es_ps, "psC0")
    psC1 = ps_([128, 512], F32, es_ps, "psC1")
    psD0 = ps_([128, 512], F32, es_ps, "psD0")
    psD1 = ps_([128, 512], F32, es_ps, "psD1")
    psE = ps_([128, 512], F32, es_ps, "psE")
    psF = ps_([128, 512], F32, es_ps, "psF")
    SBK = [psA, psB]
    OAB = [(psC0, psC1, psE), (psD0, psD1, psF)]

    def emit_quant(eb, F, src, dst, PL, act_abs=False):
        NB = F // 16
        pA, pC, pD, pK = PL['pA'], PL['pC'], PL['pD'], PL['pK']
        v = 'vector'
        L = S['lad']
        if act_abs:
            AB = S['aab']
            AB.n += 1
            eb.em('scalar', lambda e: e.activation(out=pA[:, 0:F], in_=src, func=AF.Abs).then_inc(AB.h, 1))
            a_abs = AB.n
            AB.n += 1
            eb.em('scalar', lambda e: e.activation(out=pK[:, 0:F], in_=src, func=AF.Sign).then_inc(AB.h, 1))
            a_sg = AB.n

        def op(fn, inc=True):
            if inc:
                L.n += 1
                eb.em(v, lambda e, fn=fn: fn(e).then_inc(L.h, 1))
            else:
                eb.em(v, fn)
            return L.n

        def wt(tgt):
            eb.wait(v, L, tgt)

        eb.em(v, lambda e: e.drain())
        src3 = src.rearrange("p (b w) -> p b w", w=16)
        A3 = pA[:, 0:F].rearrange("p (b w) -> p b w", w=16)
        K3 = pK[:, 0:F].rearrange("p (b w) -> p b w", w=16)
        sb_ = pscale[:, 0:NB].unsqueeze(2).broadcast_to([128, NB, 16])
        rb = prcp[:, 0:NB].unsqueeze(2).broadcast_to([128, NB, 16])
        t_bm = op(lambda e: e.tensor_reduce(out=pbm[:, 0:NB], in_=src3, axis=AX.X, op=ALU.max, apply_absolute_value=True))
        if not act_abs:
            t_abs = op(lambda e: e.scalar_tensor_tensor(out=pA[:, 0:F], in0=src, scalar=-1.0, in1=src, op0=ALU.mult, op1=ALU.max))
            t_sg = op(lambda e: e.tensor_scalar(out=pK[:, 0:F], in0=src, scalar1=0.0, scalar2=2.0, op0=ALU.is_ge, op1=ALU.mult))
        wt(t_bm)
        t_sc = op(lambda e: e.tensor_scalar(out=pscale[:, 0:NB], in0=pbm[:, 0:NB], scalar1=6e-12, scalar2=1.0 / 6.0, op0=ALU.max, op1=ALU.mult))
        if not act_abs:
            wt(t_sg)
            t_sg1 = op(lambda e: e.tensor_single_scalar(out=pK[:, 0:F], in_=pK[:, 0:F], scalar=1.0, op=ALU.subtract))
        wt(t_sc)
        t_rcp = op(lambda e: e.reciprocal(prcp[:, 0:NB], pscale[:, 0:NB]))
        wt(t_rcp)
        if act_abs:
            eb.wait(v, S['aab'], a_abs)
        else:
            wt(t_abs)
        t_A = op(lambda e: e.tensor_tensor(out=A3, in0=A3, in1=rb, op=ALU.mult))
        wt(t_A)
        tc = op(lambda e: e.tensor_single_scalar(out=pC[:, 0:F], in_=pA[:, 0:F], scalar=0.25, op=ALU.is_gt))
        td = op(lambda e: e.tensor_scalar(out=pD[:, 0:F], in0=pA[:, 0:F], scalar1=5.0, scalar2=2.0, op0=ALU.is_gt, op1=ALU.mult))
        wt(tc)
        tc = op(lambda e: e.scalar_tensor_tensor(out=pC[:, 0:F], in0=pA[:, 0:F], scalar=0.75, in1=pC[:, 0:F], op0=ALU.is_gt, op1=ALU.add))
        wt(td)
        td = op(lambda e: e.scalar_tensor_tensor(out=pD[:, 0:F], in0=pA[:, 0:F], scalar=3.5, in1=pD[:, 0:F], op0=ALU.is_gt, op1=ALU.add))
        wt(tc)
        tc = op(lambda e: e.scalar_tensor_tensor(out=pC[:, 0:F], in0=pA[:, 0:F], scalar=1.25, in1=pC[:, 0:F], op0=ALU.is_gt, op1=ALU.add))
        wt(td)
        td = op(lambda e: e.scalar_tensor_tensor(out=pD[:, 0:F], in0=pA[:, 0:F], scalar=2.5, in1=pD[:, 0:F], op0=ALU.is_gt, op1=ALU.add))
        wt(tc)
        tc = op(lambda e: e.scalar_tensor_tensor(out=pC[:, 0:F], in0=pA[:, 0:F], scalar=1.75, in1=pC[:, 0:F], op0=ALU.is_gt, op1=ALU.add))
        wt(tc)
        wt(td)
        t_q = op(lambda e: e.scalar_tensor_tensor(out=pC[:, 0:F], in0=pC[:, 0:F], scalar=0.5, in1=pD[:, 0:F], op0=ALU.mult, op1=ALU.add))
        wt(t_q)
        if act_abs:
            eb.wait(v, S['aab'], a_sg)
        else:
            wt(t_sg1)
        t_f1 = op(lambda e: e.tensor_tensor(out=pC[:, 0:F], in0=pC[:, 0:F], in1=pK[:, 0:F], op=ALU.mult))
        wt(t_f1)
        C3 = pC[:, 0:F].rearrange("p (b w) -> p b w", w=16)
        dst3 = dst.rearrange("p (b w) -> p b w", w=16)
        return lambda e: e.tensor_tensor(out=dst3, in0=C3, in1=sb_, op=ALU.mult)

    wq_units = []
    for W_, WN_ in ((Wg, WgqN), (Wu, WuqN)):
        for i in range(22):
            for j in range(2):
                wq_units.append((W_[i * 128:(i + 1) * 128, j * 512:(j + 1) * 512],
                                 WN_[i * 128:(i + 1) * 128, j * 512:(j + 1) * 512], 512))
    for i in range(8):
        for j in range(6):
            c0 = j * 512
            cl = 512 if j < 5 else 256
            wq_units.append((Wd[i * 128:(i + 1) * 128, c0:c0 + cl],
                             WdqN[i * 128:(i + 1) * 128, c0:c0 + cl], cl))
    NW_UNITS = len(wq_units)

    def emit_wq_units(eb, us, PL):
        for u in us:
            if u >= NW_UNITS:
                continue
            src, dst, F = wq_units[u]
            if u >= 2:
                eb.wait('sync', S['wqd'], u - 1)
            DS['win'].issue(eb, 'sync', wbuf[:, u % 2, 0:F], src)
            DS['win'].wait_for(eb, 'vector', u)
            if u >= 2:
                DS['wqs'].wait_for(eb, 'vector', u - 2)
            last = emit_quant(eb, F, wbuf[:, u % 2, 0:F], wqout[:, u % 2, 0:F], PL)
            S['wqd'].n += 1
            eb.em('vector', lambda e, last=last: last(e).then_inc(S['wqd'].h, 1))
            eb.wait('gpsimd', S['wqd'], S['wqd'].n)
            DS['wqs'].issue(eb, 'gpsimd', dst, wqout[:, u % 2, 0:F])

    def barrier(eb, name):
        eb.wait('sync', S['out'], S['out'].n)
        for eng in EB.ENGS:
            S[name].n += 1
            if eng == 'sync':
                eb.em(eng, lambda e: e.nop().then_inc(S[name].h, 1))
            else:
                eb.em(eng, lambda e: e.drain().then_inc(S[name].h, 1))

    def barrier_wait(eb, name):
        for eng in EB.ENGS:
            eb.wait(eng, S[name], S[name].n)

    es_b12 = contextlib.ExitStack()
    xnT = sb("xnT", [128, KC, TB], BF16, es_b12)
    es_b1 = contextlib.ExitStack()
    PL1 = {k: sb("PL1" + k, [128, 1024], F32, es_b1) for k in ('pA', 'pC', 'pD', 'pK')}
    xbuf = sb("xbuf", [128, 2, 2, 1024], F32, es_b1)
    xn_scr = sb("xn_scr", [128, 2, 1024], F32, es_b1)
    xq_f = sb("xq_f", [128, 2, 1024], F32, es_b1)
    stage_q = sb("stage_q", [128, KC, 512], BF16, es_b1)

    eb = EB(nc)
    eb.em('scalar', lambda e: e.preload_act_table(AF.Sqrt))
    S['xt16'].n += 16
    eb.em('sync', lambda e: e.dma_start(out=ident_t[:], in_=identD).then_inc(S['xt16'].h, 16))
    S['prep'].n += 4
    eb.em('gpsimd', lambda e: e.memset(bt1[:], 0.0).then_inc(S['prep'].h, 1))
    eb.em('gpsimd', lambda e: e.memset(ones_f1[:], 1.0).then_inc(S['prep'].h, 1))
    eb.em('gpsimd', lambda e: e.memset(eps_col[:], 1e-5).then_inc(S['prep'].h, 1))
    eb.em('gpsimd', lambda e: e.memset(ones_b128[:], 1.0).then_inc(S['prep'].h, 1))

    eb.wait('scalar', S['prep'], S['prep'].n)
    NCH_X = (B1_NT + 1) // 2

    def issue_x_chunk(ch):
        if ch >= 2:
            eb.wait('sync', S['lnap'], 2 * (ch - 1))
        DS['x_in'].issue(eb, 'sync', xbuf[:, ch % 2, :, :],
                         x_tok.rearrange("(c t p) d -> p c t d", p=128, t=2)[:, ch, :, :])

    for ch in range(min(2, NCH_X)):
        issue_x_chunk(ch)

    tr_cnt = 0
    eb.wait('tensor', S['xt16'], 16)

    def emit_tile_transposes(eb, src_tile, dst_fn):
        nonlocal tr_cnt
        for b_ in range(2):
            pbank = SBK[tr_cnt % 2]
            if tr_cnt >= 2:
                eb.wait('tensor', S['trev'], tr_cnt - 1)
            for j in range(4):
                c = b_ * 4 + j
                ins = (lambda e, c=c, j=j, pbank=pbank, src_tile=src_tile: e.transpose(
                    pbank[:, j * 128:(j + 1) * 128], src_tile[:, c * 128:(c + 1) * 128], ident_t[:]))
                if j == 3:
                    S['tr'].n += 1
                    eb.em('tensor', lambda e, ins=ins: ins(e).then_inc(S['tr'].h, 1))
                else:
                    eb.em('tensor', ins)
            eb.wait('scalar', S['tr'], S['tr'].n)
            dst = dst_fn(b_)
            src = pbank[:].rearrange("p (c n) -> p c n", c=4)
            S['trev'].n += 1
            eb.em('scalar', lambda e, dst=dst, src=src: e.copy(dst, src).then_inc(S['trev'].h, 1))
            tr_cnt += 1

    for t in range(B1_NT):
        ch = t // 2
        if t % 2 == 0 and 2 <= ch + 1 < NCH_X:
            issue_x_chunk(ch + 1)
        own = t < 16
        xt = xbuf[:, ch % 2, t % 2, :]
        c2 = slice(t % 2, t % 2 + 1)
        DS['x_in'].wait_for(eb, 'vector', ch)
        eb.em('vector', lambda e, xt=xt, c2=c2: e.tensor_reduce(out=sumc[:, c2], in_=xt, axis=AX.X, op=ALU.add))
        eb.em('vector', lambda e, xt=xt, c2=c2: e.scalar_tensor_tensor(
            out=PL1['pA'][:, 0:1024], in0=xt, scalar=1.0, in1=xt, op0=ALU.mult, op1=ALU.mult, accum_out=sqc[:, c2]))
        eb.em('vector', lambda e: e.drain())
        eb.em('vector', lambda e, c2=c2: e.tensor_single_scalar(out=mc[:, c2], in_=sumc[:, c2], scalar=1.0 / D, op=ALU.mult))
        eb.em('vector', lambda e: e.drain())
        eb.em('vector', lambda e, c2=c2: e.tensor_single_scalar(out=msqc[:, c2], in_=mc[:, c2], scalar=mc[:, c2], op=ALU.mult))
        eb.em('vector', lambda e, c2=c2: e.tensor_single_scalar(out=sqc[:, c2], in_=sqc[:, c2], scalar=1.0 / D, op=ALU.mult))
        eb.em('vector', lambda e: e.drain())
        eb.em('vector', lambda e, c2=c2: e.tensor_single_scalar(out=varc[:, c2], in_=sqc[:, c2], scalar=msqc[:, c2], op=ALU.subtract))
        S['lnv'].n += 1
        eb.em('vector', lambda e: e.drain().then_inc(S['lnv'].h, 1))
        eb.wait('scalar', S['lnv'], S['lnv'].n)
        S['lna'].n += 1
        eb.em('scalar', lambda e, c2=c2: e.activation(out=sdc[:, c2], in_=varc[:, c2], func=AF.Sqrt, bias=eps_col[:, 0:1], scale=1.0).then_inc(S['lna'].h, 1))
        eb.wait('vector', S['lna'], S['lna'].n)
        eb.em('vector', lambda e, c2=c2: e.reciprocal(istdc[:, c2], sdc[:, c2]))
        eb.em('vector', lambda e: e.drain())
        if t >= 2:
            eb.wait('vector', S['tr'], tr_cnt)
        S['lnap'].n += 1
        eb.em('vector', lambda e, xt=xt, c2=c2, t=t: e.tensor_scalar(
            out=xn_scr[:, t % 2, :], in0=xt, scalar1=mc[:, c2], scalar2=istdc[:, c2],
            op0=ALU.subtract, op1=ALU.mult).then_inc(S['lnap'].h, 1))
        eb.wait('tensor', S['lnap'], S['lnap'].n)
        if B1_TRANS:
            emit_tile_transposes(eb, xn_scr[:, t % 2, :],
                                 lambda b_, t=t: xnT[:, b_ * 4:(b_ + 1) * 4, t * 128:(t + 1) * 128])
        if own and B1_QUANT:
            eb.wait('vector', S['lnap'], S['lnap'].n)
            last = emit_quant(eb, 1024, xn_scr[:, t % 2, :], xq_f[:, t % 2, :], PL1)
            S['xq'].n += 1
            eb.em('vector', lambda e, last=last: last(e).then_inc(S['xq'].h, 1))
            eb.wait('tensor', S['xq'], S['xq'].n)
            g = t // 4
            if B1_TRANS:
                emit_tile_transposes(eb, xq_f[:, t % 2, :],
                                     lambda b_, t=t: stage_q[:, b_ * 4:(b_ + 1) * 4, (t % 4) * 128:(t % 4 + 1) * 128])
            if B1_TRANS and B1_SPILL and t % 4 == 3:
                eb.wait('sync', S['trev'], S['trev'].n)
                S['xqs'].n += 16
                eb.em('sync', lambda e, g=g: e.dma_start(
                    out=xqT_scr.rearrange("(c p) q -> p c q", p=128)[:, :, g * 512:(g + 1) * 512],
                    in_=stage_q[:]).then_inc(S['xqs'].h, 16))
                if g < 3:
                    eb.wait('vector', S['xqs'], 16 * (g + 1))
    if debug:
        eb.wait('sync', S['trev'], S['trev'].n)
        eb.wait('sync', S['lnap'], S['lnap'].n)
        S['out'].n += 16
        eb.em('sync', lambda e: e.dma_start(out=dbg['xnT'], in_=xnT[:]).then_inc(S['out'].h, 16))
        for i_, tns in enumerate([sumc, sqc, mc, msqc, varc, sdc, istdc, istdc]):
            S['out'].n += 16
            eb.em('sync', lambda e, i_=i_, tns=tns: e.dma_start(out=dbg['sml'][:, 2*i_:2*i_+2], in_=tns[:]).then_inc(S['out'].h, 16))
        S['out'].n += 16
        eb.em('sync', lambda e: e.dma_start(out=dbg['resid'][0:128, 0:2048], in_=xn_scr[:].rearrange("p c d -> p (c d)")).then_inc(S['out'].h, 16))


    if B1_WQ:
        pass
    eb.wait('sync', S['xqs'], S['xqs'].n)
    barrier(eb, 'b12')
    with nc.Block() as blk:
        eb.flush(blk)

    if stop_after <= 1:
        eb2 = EB(nc)
        eb2.wait('sync', S['out'], S['out'].n)
        with nc.Block() as blk:
            eb2.flush(blk)
        return nc
    es_b1.close()

    es_b2 = contextlib.ExitStack()
    PL2 = {k: sb("PL2" + k, [128, 512], F32, es_b2) for k in ('pA', 'pC', 'pD', 'pK')}
    wkv_b = sb("wkv_b", [128, KC, LR], BF16, es_b2)
    sqb2 = sb("sqb2", [128, 2, 512], BF16, es_b2)
    wqchunk = sb("wqchunk", [128, 2, KC, 512], BF16, es_b2)
    es_b23 = contextlib.ExitStack()
    qT_all = es_b23.enter_context(nc.sbuf_tensor("qT_all", [128, 16, TC], BF16, side="right"))

    eb = EB(nc)
    barrier_wait(eb, 'b12')
    eb.em('scalar', lambda e: e.preload_act_table(AF.Exp))
    S['wqt'].n += 16
    eb.em('gpsimd', lambda e: e.dma_start(out=wkv_b[:], in_=WkvT.rearrange("(c p) l -> p c l", p=128)).then_inc(S['wqt'].h, 16))
    eb.wait('tensor', S['wqt'], S['wqt'].n)
    for n in range(TB // 512):
        pb = SBK[n % 2]
        if n >= 2:
            eb.wait('tensor', S['kvev'], n - 1)
        for kc in range(KC):
            ins = (lambda e, n=n, kc=kc, pb=pb: e.matmul(pb[0:LR, 0:512], wkv_b[:, kc, :], xnT[:, kc, n * 512:(n + 1) * 512],
                                                         start=(kc == 0), stop=(kc == KC - 1)))
            if kc == KC - 1:
                S['kv_pe'].n += 1
                eb.em('tensor', lambda e, ins=ins: ins(e).then_inc(S['kv_pe'].h, 1))
            else:
                eb.em('tensor', ins)
        eb.wait('scalar', S['kv_pe'], S['kv_pe'].n)
        S['kvev'].n += 1
        eb.em('scalar', lambda e, n=n, pb=pb: e.copy(kvT_b[:, n * 512:(n + 1) * 512], pb[0:LR, 0:512]).then_inc(S['kvev'].h, 1))
    if debug:
        eb.wait('sync', S['kvev'], S['kvev'].n)
        S['out'].n += 16
        eb.em('sync', lambda e: e.dma_start(out=dbg['kvT'], in_=kvT_b[:]).then_inc(S['out'].h, 16))

    for j in range(4):
        if j >= 2:
            eb.wait('gpsimd', S['q_pe'], 16 * (j - 1))
        DS['wql'].issue(eb, 'gpsimd', wqchunk[:, j % 2, :, :],
                        WqT.rearrange("(c p) m -> p c m", p=128)[:, :, j * 512:(j + 1) * 512])
    eb.wait('tensor', S['kvev'], S['kvev'].n)
    qpj_base = S['qpj'].n
    for j in range(4):
        DS['wql'].wait_for(eb, 'tensor', j)
        for mrel in range(4):
            m = j * 4 + mrel
            for n in range(TC // 512):
                idx = m * 4 + n
                pb = SBK[idx % 2]
                if idx >= 2:
                    eb.wait('tensor', S['qpj'], qpj_base + idx - 1)
                for kc in range(KC):
                    ins = (lambda e, j=j, mrel=mrel, n=n, kc=kc, pb=pb: e.matmul(
                        pb[:, :], wqchunk[:, j % 2, kc, mrel * 128:(mrel + 1) * 128],
                        xnT[:, kc, n * 512:(n + 1) * 512], start=(kc == 0), stop=(kc == KC - 1)))
                    if kc == KC - 1:
                        S['q_pe'].n += 1
                        eb.em('tensor', lambda e, ins=ins: ins(e).then_inc(S['q_pe'].h, 1))
                    else:
                        eb.em('tensor', ins)
                eb.wait('scalar', S['q_pe'], S['q_pe'].n)
                S['qpj'].n += 1
                eb.em('scalar', lambda e, m=m, n=n, pb=pb: e.copy(
                    qT_all[:, m, n * 512:(n + 1) * 512], pb[:, :]).then_inc(S['qpj'].h, 1))
    if debug:
        eb.wait('sync', S['qpj'], S['qpj'].n)
        S['out'].n += 16
        eb.em('sync', lambda e: e.dma_start(out=dbg['qT'], in_=qT_all[:]).then_inc(S['out'].h, 16))

    eb.wait('vector', S['kvev'], S['kvev'].n)
    eb.wait('vector', S['qpj'], S['qpj'].n)
    eb.wait('tensor', S['prep'], S['prep'].n)
    PSR = [psE, psF]
    for sl in range(8):
        S['bnd'].n += 1
        eb.em('vector', lambda e, sl=sl: e.tensor_tensor(
            out=sqb2[0:LR, 0, :], in0=kvT_b[:, sl * 512:(sl + 1) * 512],
            in1=kvT_b[:, sl * 512:(sl + 1) * 512], op=ALU.mult))
        eb.em('vector', lambda e: e.drain().then_inc(S['bnd'].h, 1))
        eb.wait('tensor', S['bnd'], S['bnd'].n)
        if sl >= 2:
            eb.wait('tensor', S['pbev'], S['pbev'].n - 1)
        pse = PSR[sl % 2]
        S['bnd'].n += 1
        eb.em('tensor', lambda e, pse=pse: e.matmul(pse[0:1, 0:512], ones_b128[0:LR, 0:1], sqb2[0:LR, 0, :],
                                                    start=True, stop=True).then_inc(S['bnd'].h, 1))
        eb.wait('vector', S['bnd'], S['bnd'].n)
        eb.em('vector', lambda e, pse=pse: e.tensor_reduce(out=bt1[:, 0:1], in_=pse[0:1, 0:512], axis=AX.X, op=ALU.max))
        eb.em('vector', lambda e: e.drain())
        S['pbev'].n += 1
        if sl == 0:
            eb.em('vector', lambda e: e.tensor_single_scalar(out=bt1[:, 7:8], in_=bt1[:, 0:1], scalar=1.0, op=ALU.mult))
        else:
            eb.em('vector', lambda e: e.tensor_single_scalar(out=bt1[:, 7:8], in_=bt1[:, 0:1], scalar=bt1[:, 7:8], op=ALU.max))
        eb.em('vector', lambda e: e.drain().then_inc(S['pbev'].h, 1))
    for h2 in range(NH):
        for n in range(4):
            for m in range(2):
                qsl = qT_all[:, 2 * h2 + m, n * 512:(n + 1) * 512]
                if m == 1:
                    S['bnd'].n += 1
                    eb.em('vector', lambda e, qsl=qsl, m=m: e.tensor_tensor(out=sqb2[:, m, :], in0=qsl, in1=qsl, op=ALU.mult))
                    eb.em('vector', lambda e: e.drain().then_inc(S['bnd'].h, 1))
                else:
                    eb.em('vector', lambda e, qsl=qsl, m=m: e.tensor_tensor(out=sqb2[:, m, :], in0=qsl, in1=qsl, op=ALU.mult))
            eb.wait('tensor', S['bnd'], S['bnd'].n)
            eb.wait('tensor', S['pbev'], S['pbev'].n - 1)
            pse = PSR[n % 2]
            for m in range(2):
                ins = (lambda e, pse=pse, m=m: e.matmul(pse[0:1, 0:512], ones_b128[:, 0:1], sqb2[:, m, :],
                                                        start=(m == 0), stop=(m == 1)))
                if m == 1:
                    S['bnd'].n += 1
                    eb.em('tensor', lambda e, ins=ins: ins(e).then_inc(S['bnd'].h, 1))
                else:
                    eb.em('tensor', ins)
            eb.wait('vector', S['bnd'], S['bnd'].n)
            eb.em('vector', lambda e, pse=pse: e.tensor_reduce(out=bt1[:, 0:1], in_=pse[0:1, 0:512], axis=AX.X, op=ALU.max))
            eb.em('vector', lambda e: e.drain())
            S['pbev'].n += 1
            if n == 0:
                eb.em('vector', lambda e, h2=h2: e.tensor_single_scalar(out=bq2[:, h2:h2 + 1], in_=bt1[:, 0:1], scalar=1.0, op=ALU.mult))
            else:
                eb.em('vector', lambda e, h2=h2: e.tensor_single_scalar(out=bq2[:, h2:h2 + 1], in_=bt1[:, 0:1], scalar=bq2[:, h2:h2 + 1], op=ALU.max))
            eb.em('vector', lambda e: e.drain().then_inc(S['pbev'].h, 1))
    barrier(eb, 'b23')
    with nc.Block() as blk:
        eb.flush(blk)

    if stop_after <= 2:
        eb2 = EB(nc)
        eb2.wait('sync', S['out'], S['out'].n)
        with nc.Block() as blk:
            eb2.flush(blk)
        return nc
    es_b2.close()
    es_b12.close()

    es_b34 = contextlib.ExitStack()
    o_all = sb("o_all", [128, 16, TC], BF16, es_b34)
    es_b3 = contextlib.ExitStack()
    PL3 = {k: sb("PL3" + k, [128, 512], F32, es_b3) for k in ('pA', 'pC', 'pD', 'pK')}
    sqv_t = sb("sqv_t", [128, 1024], BF16, es_b3)
    wkuh = sb("wkuh", [LR, 2, 256], BF16, es_b3)
    wvuh = sb("wvuh", [LR, 2, 256], BF16, es_b3)
    kT_h = sb("kT_h", [128, 2, TB], BF16, es_b3)
    V_h = sb("V_h", [128, 32, 258], BF16, es_b3)
    pbuf = sb("pbuf", [128, 2, 512], BF16, es_b3)

    eb = EB(nc)
    barrier_wait(eb, 'b23')
    S['prep'].n += 1
    eb.em('gpsimd', lambda e: e.memset(V_h[:, :, 256:257], 1.0).then_inc(S['prep'].h, 1))
    prep_end = [None] * NH

    def emit_head_wload(h):
        if h >= 2 and prep_end[h - 2] is not None:
            eb.wait('gpsimd', S['prep'], prep_end[h - 2])
        DS['wkl'].issue(eb, 'gpsimd', wkuh[:, h % 2, :], WkuT[:, h * 256:(h + 1) * 256])
        DS['wkl'].issue(eb, 'gpsimd', wvuh[:, h % 2, :], WvuT[:, h * 256:(h + 1) * 256])

    emit_head_wload(0)
    emit_head_wload(1)

    wq_next = 0
    qglob = 0
    cglob = 0
    kt0_dump_tgt = [0]
    for h in range(NH):
        if debug and h == 1:
            eb.wait('vector', S['out'], kt0_dump_tgt[0])
        DS['wkl'].wait_for(eb, 'tensor', 2 * h)
        DS['wkl'].wait_for(eb, 'tensor', 2 * h + 1)
        eb.wait('tensor', S['prep'], S['prep'].n)
        if h > 0:
            eb.wait('tensor', S['exp'], cglob)
        kv_base_a = S['kvev'].n
        kv_base_d = S['qpj'].n
        if h > 0:
            eb.wait('tensor', S['oev'], 4 * h)
        PR4 = [psA, psB, psE, psF]

        def prep_done_wait(eng, upto):
            eb.wait(eng, S['kvev'], kv_base_a + (upto + 2) // 2)
            eb.wait(eng, S['qpj'], kv_base_d + (upto + 1) // 2)

        def emit_prep(rng):
            for idx in rng:
                pb = PR4[idx % 4]
                if idx >= 4:
                    pidx = idx - 4
                    if pidx % 2 == 0:
                        eb.wait('tensor', S['kvev'], kv_base_a + pidx // 2 + 1)
                    else:
                        eb.wait('tensor', S['qpj'], kv_base_d + pidx // 2 + 1)
                if idx < 16:
                    m, n = idx // 8, idx % 8
                    ins2 = [(lambda e, h=h, m=m, n=n, pb=pb: e.matmul(
                        pb[:, :], wkuh[:, h % 2, m * 128:(m + 1) * 128],
                        kvT_b[:, n * 512:(n + 1) * 512], start=True, stop=True))]
                    eva = (lambda e, m=m, n=n, pb=pb: e.copy(
                        kT_h[:, m, n * 512:(n + 1) * 512], pb[:, :]))
                    evd = (lambda e, m=m, n=n, pb=pb: e.tensor_copy(
                        kT_h[:, m, n * 512:(n + 1) * 512], pb[:, :]))
                else:
                    mt = (idx - 16) * 2
                    ins2 = [(lambda e, h=h, mt=mt, pb=pb: e.matmul(
                        pb[0:128, 0:256], kvT_b[:, mt * 128:(mt + 1) * 128],
                        wvuh[:, h % 2, :], start=True, stop=True)),
                            (lambda e, h=h, mt=mt, pb=pb: e.matmul(
                        pb[0:128, 256:512], kvT_b[:, (mt + 1) * 128:(mt + 2) * 128],
                        wvuh[:, h % 2, :], start=True, stop=True))]
                    dst = V_h[:, mt:mt + 2, 0:256]
                    srcp = pb[0:128, 0:512].rearrange("p (c n) -> p c n", c=2)
                    eva = (lambda e, dst=dst, srcp=srcp: e.copy(dst, srcp))
                    evd = (lambda e, dst=dst, srcp=srcp: e.tensor_copy(dst, srcp))
                for k2, ins in enumerate(ins2):
                    S['prep'].n += 1
                    eb.em('tensor', lambda e, ins=ins: ins(e).then_inc(S['prep'].h, 1))
                eb.wait('scalar', S['prep'], S['prep'].n)
                if idx % 2 == 0:
                    S['kvev'].n += 1
                    eb.em('scalar', lambda e, eva=eva: eva(e).then_inc(S['kvev'].h, 1))
                else:
                    S['qpj'].n += 1
                    eb.em('scalar', lambda e, eva=eva: eva(e).then_inc(S['qpj'].h, 1))

        emit_prep(range(16))
        NB4 = [psC0, psC1, psD0, psD1]
        eb.wait('scalar', S['qpj'], kv_base_d + 8)
        mmp_tgt = [0] * 8
        for grp in range(8):
            emit_prep(range(16 + 2 * grp, 16 + 2 * grp + 2))
            if grp >= 1:
                eb.wait('scalar', S['bnd'], mmp_tgt[grp - 1])
            sq_base = S['sco'].n
            for m in range(2):
                S['sco'].n += 1
                eb.em('scalar', lambda e, grp=grp, m=m: e.activation(
                    out=sqv_t[:, m * 512:(m + 1) * 512], in_=kT_h[:, m, grp * 512:(grp + 1) * 512],
                    func=AF.Square).then_inc(S['sco'].h, 1))
            pse = NB4[grp % 4]
            if grp >= 4:
                eb.wait('tensor', S['pbev'], S['pbev'].n - 3)
            eb.wait('tensor', S['sco'], S['sco'].n)
            for m in range(2):
                ins = (lambda e, pse=pse, m=m: e.matmul(
                    pse[0:1, 0:512], ones_b128[:, 0:1], sqv_t[:, m * 512:(m + 1) * 512],
                    start=(m == 0), stop=(m == 1)))
                if m == 1:
                    S['bnd'].n += 1
                    eb.em('tensor', lambda e, ins=ins: ins(e).then_inc(S['bnd'].h, 1))
                else:
                    eb.em('tensor', ins)
            mmp_tgt[grp] = S['bnd'].n
            eb.wait('vector', S['bnd'], S['bnd'].n)
            eb.em('vector', lambda e, pse=pse: e.tensor_reduce(out=bt1[:, 0:1], in_=pse[0:1, 0:512], axis=AX.X, op=ALU.max))
            eb.em('vector', lambda e: e.drain())
            S['pbev'].n += 1
            if grp == 0:
                eb.em('vector', lambda e: e.tensor_single_scalar(out=bt1[:, 2:3], in_=bt1[:, 0:1], scalar=1.0, op=ALU.mult))
            else:
                eb.em('vector', lambda e: e.tensor_single_scalar(out=bt1[:, 2:3], in_=bt1[:, 0:1], scalar=bt1[:, 2:3], op=ALU.max))
            eb.em('vector', lambda e: e.drain().then_inc(S['pbev'].h, 1))
        prep_end[h] = S['prep'].n
        if h + 2 < NH:
            emit_head_wload(h + 2)
        if debug and h == 0:
            eb.wait('sync', S['kvev'], S['kvev'].n)
            S['out'].n += 16
            eb.em('sync', lambda e: e.dma_start(out=dbg['kT0'], in_=kT_h[:]).then_inc(S['out'].h, 16))

        eb.wait('vector', S['exp'], 128 * h)
        eb.em('vector', lambda e, h=h: e.tensor_single_scalar(out=bt1[:, 3:4], in_=bq2[:, h:h + 1], scalar=bt1[:, 2:3], op=ALU.add))
        eb.em('vector', lambda e: e.drain())
        S['bnd'].n += 1
        eb.em('vector', lambda e: e.tensor_single_scalar(out=bt1[:, 6:7], in_=bt1[:, 3:4], scalar=-1.02 / 32.0, op=ALU.mult))
        eb.em('vector', lambda e: e.drain().then_inc(S['bnd'].h, 1))
        eb.wait('tensor', S['bnd'], S['bnd'].n)
        eb.wait('tensor', S['pbev'], S['pbev'].n)
        S['bnd'].n += 1
        eb.em('tensor', lambda e: e.matmul(psE[0:128, 0:1], ones_f1[:, :], bt1[0:1, 6:7], start=True, stop=True).then_inc(S['bnd'].h, 1))
        eb.wait('vector', S['bnd'], S['bnd'].n)
        S['pbev'].n += 1
        eb.em('vector', lambda e: e.tensor_single_scalar(out=negB_col[:], in_=psE[0:128, 0:1], scalar=1.0, op=ALU.mult))
        eb.em('vector', lambda e: e.drain().then_inc(S['pbev'].h, 1))
        S['bnd'].n += 1
        eb.em('vector', lambda e: e.nop().then_inc(S['bnd'].h, 1))
        bnd_ready = S['bnd'].n
        if debug and h == 0:
            eb.wait('sync', S['bnd'], bnd_ready)
            S['out'].n += 32
            eb.em('sync', lambda e: e.dma_start(out=dbg['bnd'][0:1, 0:8], in_=bt1[:]).then_inc(S['out'].h, 16))
            eb.em('sync', lambda e: e.dma_start(out=dbg['bnd'][:, 8:9], in_=negB_col[:], allow_slow_non_contiguous=True).then_inc(S['out'].h, 16))
            eb.wait('scalar', S['out'], S['out'].n)
            kt0_dump_tgt[0] = S['out'].n

        eb.wait('tensor', S['kvev'], S['kvev'].n)
        eb.wait('tensor', S['qpj'], S['qpj'].n)
        pbev_ready = S['pbev'].n
        for qq in range(4):
            par = qq % 2
            o0, o1, osum = OAB[par]
            exp_tgt = [0] * 32

            def emit_scores(kc):
                nonlocal cglob
                pb = SBK[cglob % 2]
                if cglob >= 2:
                    eb.wait('tensor', S['exp'], cglob - 1)
                if h == 0 and qq == 0 and kc == 0:
                    eb.wait('tensor', S['pbev'], S['pbev'].n)
                for m in range(2):
                    ins = (lambda e, h=h, qq=qq, kc=kc, m=m, pb=pb: e.matmul(
                        pb[:, :], kT_h[:, m, kc * 128:(kc + 1) * 128],
                        qT_all[:, 2 * h + m, qq * 512:(qq + 1) * 512],
                        start=(m == 0), stop=(m == 1)))
                    if m == 1:
                        S['sco'].n += 1
                        eb.em('tensor', lambda e, ins=ins: ins(e).then_inc(S['sco'].h, 1))
                    else:
                        eb.em('tensor', ins)
                eb.wait('scalar', S['sco'], S['sco'].n)
                if qq == 0 and kc == 0:
                    eb.wait('scalar', S['bnd'], bnd_ready)
                S['exp'].n += 1
                exp_tgt[kc] = S['exp'].n
                eb.em('scalar', lambda e, cglob=cglob, pb=pb: e.activation(
                    out=pbuf[:, cglob % 2, :], in_=pb[:, :], func=AF.Exp,
                    bias=negB_col[:], scale=SCALE).then_inc(S['exp'].h, 1))
                cglob += 1

            def emit_av(kc, cg):
                eb.wait('tensor', S['exp'], exp_tgt[kc])
                if kc == 0 and qglob >= 2:
                    eb.wait('tensor', S['oev'], qglob - 1)
                if kc == 0 and qq == 0:
                    eb.wait('tensor', S['pbev'], pbev_ready)
                for mt2 in range(3):
                    lo = [0, 128, 256][mt2]
                    hi = [128, 256, 257][mt2]
                    dstp = [o0, o1, osum][mt2]
                    dsl = (slice(0, 128), slice(0, 128), slice(0, 1))[mt2]
                    ins = (lambda e, kc=kc, cg=cg, lo=lo, hi=hi, dstp=dstp, dsl=dsl: e.matmul(
                        dstp[dsl, 0:512], V_h[:, kc, lo:hi], pbuf[:, cg % 2, :],
                        start=(kc == 0), stop=(kc == 31)))
                    if mt2 == 2 and kc == 31:
                        S['avq'].n += 1
                        eb.em('tensor', lambda e, ins=ins: ins(e).then_inc(S['avq'].h, 1))
                    else:
                        eb.em('tensor', ins)

            cg0 = cglob
            for kc in range(32):
                emit_scores(kc)
                if kc >= 1:
                    emit_av(kc - 1, cg0 + kc - 1)
            emit_av(31, cg0 + 31)
            eb.wait('scalar', S['avq'], S['avq'].n)
            if qglob >= 1:
                DS['ssq'].wait_for(eb, 'scalar', qglob - 1)
            for m in range(2):
                op = [o0, o1][m]
                eb.em('scalar', lambda e, h=h, qq=qq, m=m, op=op: e.copy(
                    o_all[:, 2 * h + m, qq * 512:(qq + 1) * 512], op[:, :]))
            S['oev'].n += 1
            eb.em('scalar', lambda e, qglob=qglob, osum=osum: e.copy(
                srowq[:, 0, :], osum[0:1, 0:512]).then_inc(S['oev'].h, 1))
            eb.wait('gpsimd', S['oev'], S['oev'].n)
            DS['ssq'].issue(eb, 'gpsimd', sums8[h:h + 1, qq * 512:(qq + 1) * 512], srowq[:, 0, :])
            qglob += 1
            kq = [6, 6, 5, 0][(qglob - 1) % 4]
            emit_wq_units(eb, range(wq_next, wq_next + kq), PL3)
            wq_next = min(wq_next + kq, NW_UNITS)

    emit_wq_units(eb, range(wq_next, NW_UNITS), PL3)
    wq_next = NW_UNITS
    if debug:
        eb.wait('sync', S['oev'], S['oev'].n)
        S['out'].n += 16
        eb.em('sync', lambda e: e.dma_start(out=dbg['oall'], in_=o_all[:]).then_inc(S['out'].h, 16))

    barrier(eb, 'b34')
    with nc.Block() as blk:
        eb.flush(blk)

    if stop_after <= 3:
        eb2 = EB(nc)
        eb2.wait('sync', S['out'], S['out'].n)
        with nc.Block() as blk:
            eb2.flush(blk)
        return nc
    es_b3.close()
    es_b23.close()

    es_w5 = contextlib.ExitStack()
    wgq_b = es_w5.enter_context(nc.sbuf_tensor("wgq_b", [128, KC, 1408], BF16, side="right"))
    wuq_b = es_w5.enter_context(nc.sbuf_tensor("wuq_b", [128, KC, 1408], BF16, side="right"))
    es_b4 = contextlib.ExitStack()
    wo_b = sb("wo_b", [128, 16, D], BF16, es_b4)
    xt_buf = sb("xt_buf", [128, 2, TC], F32, es_b4)
    rstage = sb("rstage", [128, 2, TC], F32, es_b4)
    srow = sb("srow", [1, TC], F32, es_b4)

    eb = EB(nc)
    barrier_wait(eb, 'b34')
    S['wo16'].n += 16
    eb.em('gpsimd', lambda e: e.dma_start(out=wo_b[:], in_=WoT.rearrange("(c p) d -> p c d", p=128)).then_inc(S['wo16'].h, 16))
    DS['wqs'].wait_all(eb, 'scalar')
    for kc2 in range(KC):
        S['fw'].n += 16
        eb.em('scalar', lambda e, kc2=kc2: e.dma_start(
            out=wgq_b[:, kc2, :], in_=WgqN[0:1408, kc2 * 128:(kc2 + 1) * 128], transpose=True).then_inc(S['fw'].h, 16))
    for kc2 in range(KC):
        S['fw'].n += 16
        eb.em('scalar', lambda e, kc2=kc2: e.dma_start(
            out=wuq_b[:, kc2, :], in_=WuqN[0:1408, kc2 * 128:(kc2 + 1) * 128], transpose=True).then_inc(S['fw'].h, 16))

    DS['ssq'].wait_all(eb, 'vector')
    rec_done = S['oev'].n + 1
    S['oev'].n += 1
    eb.em('vector', lambda e: e.reciprocal(sums8[:, :], sums8[:, :]))
    eb.em('vector', lambda e: e.drain().then_inc(S['oev'].h, 1))
    if debug:
        eb.wait('sync', S['oev'], rec_done)
        S['out'].n += 16
        eb.em('sync', lambda e: e.dma_start(out=dbg['sums'], in_=sums8[:, :]).then_inc(S['out'].h, 16))
    norm_base = S['oev'].n
    NBNK = [psC0, psC1, psD0, psD1]
    for h in range(NH):
        eb.wait('sync', S['oev'], rec_done if h == 0 else norm_base + 4 * h)
        S['srw'].n += 16
        eb.em('sync', lambda e, h=h: e.dma_start(out=srow[0:1, :], in_=sums8[h:h + 1, :]).then_inc(S['srw'].h, 16))
        eb.wait('tensor', S['srw'], S['srw'].n)
        for n in range(4):
            idx = h * 4 + n
            pb = NBNK[idx % 4]
            if idx >= 4:
                eb.wait('tensor', S['oev'], norm_base + idx - 3)
            S['avq'].n += 1
            eb.em('tensor', lambda e, n=n, pb=pb: e.matmul(
                pb[:, :], ones_f1[:, :], srow[0:1, n * 512:(n + 1) * 512],
                start=True, stop=True).then_inc(S['avq'].h, 1))
            eb.wait('vector', S['avq'], S['avq'].n)
            for m in range(2):
                sl = o_all[:, 2 * h + m, n * 512:(n + 1) * 512]
                ins = (lambda e, sl=sl, pb=pb: e.tensor_tensor(out=sl, in0=sl, in1=pb[:, :], op=ALU.mult))
                if m == 1:
                    S['oev'].n += 1
                    eb.em('vector', lambda e, ins=ins: ins(e).then_inc(S['oev'].h, 1))
                else:
                    eb.em('vector', ins)
    eb.wait('tensor', S['wo16'], S['wo16'].n)
    eb.wait('tensor', S['oev'], S['oev'].n)
    wo_base = S['oev'].n
    for m in range(2):
        DS['xt'].issue(eb, 'sync', xt_buf[:, m % 2, :], xT_own[m * 128:(m + 1) * 128, :])
    for m in range(8):
        for n in range(4):
            idx = 4 * m + n
            pb = [psC0, psC1][idx % 2]
            if idx >= 2:
                eb.wait('tensor', S['oev'], wo_base + idx - 1)
            for kc in range(16):
                ins = (lambda e, m=m, n=n, kc=kc, pb=pb: e.matmul(
                    pb[:, :], wo_b[:, kc, m * 128:(m + 1) * 128], o_all[:, kc, n * 512:(n + 1) * 512],
                    start=(kc == 0), stop=(kc == 15)))
                if kc == 15:
                    S['avq'].n += 1
                    eb.em('tensor', lambda e, ins=ins: ins(e).then_inc(S['avq'].h, 1))
                else:
                    eb.em('tensor', ins)
            eb.wait('vector', S['avq'], S['avq'].n)
            DS['xt'].wait_for(eb, 'vector', m)
            if m >= 2 and n == 0:
                DS['rsd'].wait_for(eb, 'vector', m - 2)
            S['oev'].n += 1
            eb.em('vector', lambda e, m=m, n=n, pb=pb: e.tensor_tensor(
                out=rstage[:, m % 2, n * 512:(n + 1) * 512], in0=xt_buf[:, m % 2, n * 512:(n + 1) * 512],
                in1=pb[:, :], op=ALU.add).then_inc(S['oev'].h, 1))
        eb.wait('sync', S['oev'], wo_base + 4 * (m + 1))
        DS['rsd'].issue(eb, 'sync', resid_scr[m * 128:(m + 1) * 128, :], rstage[:, m % 2, :])
        if m + 2 < 8:
            DS['xt'].issue(eb, 'sync', xt_buf[:, m % 2, :], xT_own[(m + 2) * 128:(m + 3) * 128, :])
    if debug:
        DS['rsd'].wait_all(eb, 'sync')

    DS['wqs'].wait_all(eb, 'sync')
    DS['rsd'].wait_all(eb, 'sync')
    barrier(eb, 'b45')
    with nc.Block() as blk:
        eb.flush(blk)

    if stop_after <= 4:
        eb2 = EB(nc)
        eb2.wait('sync', S['out'], S['out'].n)
        with nc.Block() as blk:
            eb2.flush(blk)
        return nc
    es_b4.close()
    es_b34.close()
    es_wq.close()

    es_b5 = contextlib.ExitStack()
    PL5 = {k: sb("PL5" + k, [128, 1408], F32, es_b5) for k in ('pA', 'pC', 'pD', 'pK')}
    xqT_sb = sb("xqT_sb", [128, KC, TC], BF16, es_b5)
    wdq_b = sb("wdq_b", [128, 11, 1024], BF16, es_b5)
    gs_c = sb("gs_c", [128, 512], F32, es_b5)
    h_t = sb("h_t", [128, 1408], F32, es_b5)
    hq_t = sb("hq_t", [128, 2, 1408], F32, es_b5)
    hqT_b = sb("hqT_b", [128, 11, 1024], BF16, es_b5)
    rbuf = sb("rbuf", [128, 2, 1024], F32, es_b5)
    obuf = sb("obuf", [128, 2, 1024], F32, es_b5)

    eb = EB(nc)
    barrier_wait(eb, 'b45')
    eb.em('scalar', lambda e: e.preload_act_table(AF.Silu))
    S['xqr'].n += 16
    eb.em('sync', lambda e: e.dma_start(out=xqT_sb[:], in_=xqT_scr.rearrange("(c p) q -> p c q", p=128)).then_inc(S['xqr'].h, 16))
    for it in range(11):
        S['fw2'].n += 16
        eb.em('scalar', lambda e, it=it: e.dma_start(
            out=wdq_b[:, it, :], in_=WdqN[0:1024, it * 128:(it + 1) * 128], transpose=True).then_inc(S['fw2'].h, 16))

    IG = [0, 1408]
    NCH = [(0, 512), (512, 512), (1024, 384)]
    ftr_glob = 0
    lgq = 0
    pend_tr = None

    def emit_tr(eb, lg, tt, hq_tgt):
        nonlocal ftr_glob
        if ftr_glob % 24 == 0 and ftr_glob > 0:
            eb.wait('scalar', S['avq'], S['avq'].n)
        eb.wait('tensor', S['hq'], hq_tgt)
        for b_ in range(3):
            js = [4, 4, 3][b_]
            pbk = SBK[ftr_glob % 2]
            if ftr_glob >= 2:
                eb.wait('tensor', S['fftrev'], ftr_glob - 1)
            for j in range(js):
                c = b_ * 4 + j
                ins = (lambda e, c=c, j=j, pbk=pbk, lg=lg: e.transpose(
                    pbk[:, j * 128:(j + 1) * 128], hq_t[:, lg % 2, c * 128:(c + 1) * 128], ident_t[:]))
                if j == js - 1:
                    S['fftr'].n += 1
                    eb.em('tensor', lambda e, ins=ins: ins(e).then_inc(S['fftr'].h, 1))
                else:
                    eb.em('tensor', ins)
            eb.wait('scalar', S['fftr'], S['fftr'].n)
            S['fftrev'].n += 1
            dst = hqT_b[:, b_ * 4:b_ * 4 + js, tt * 128:(tt + 1) * 128]
            src = pbk[:, 0:js * 128].rearrange("p (c n) -> p c n", c=js)
            eb.em('scalar', lambda e, dst=dst, src=src: e.copy(dst, src).then_inc(S['fftrev'].h, 1))
            ftr_glob += 1
    PAIR3 = [(psD0, psD1), (psE, psF), (psC0, psC1)]
    need_ffdn = [False]
    pend_down = None

    def mk_down(ig, half, toff, hbld_tgt):
        def go():
            dn_base = S['ffdn'].n
            eb.wait('tensor', S['fw2'], (11 if ig == 0 else 22) * 16)
            st_ds = DS['rsd'] if ig == 0 else DS['ost']
            rr_base_i = DS['rr'].cnt
            st_base_i = st_ds.cnt
            for m in range(2):
                DS['rr'].issue(eb, 'sync', rbuf[:, m % 2, :], resid_scr[m * 128:(m + 1) * 128, toff:toff + 1024])
            eb.wait('tensor', S['fftrev'], 24 * (2 * ig + half + 1))
            eb.wait('tensor', S['hbld'], hbld_tgt)
            DS['rsd'].wait_all(eb, 'vector')
            DS['ost'].wait_all(eb, 'vector')
            for m in range(8):
                for n2 in range(2):
                    idx = m * 2 + n2
                    pb = [psC0, psC1][idx % 2]
                    if idx >= 2:
                        eb.wait('tensor', S['ffdn'], dn_base + idx - 1)
                    for it in range(11):
                        ins = (lambda e, m=m, n2=n2, it=it, pb=pb: e.matmul(
                            pb[:, :], wdq_b[:, it, m * 128:(m + 1) * 128], hqT_b[:, it, n2 * 512:(n2 + 1) * 512],
                            start=(it == 0), stop=(it == 10)))
                        if it == 10:
                            S['avq'].n += 1
                            eb.em('tensor', lambda e, ins=ins: ins(e).then_inc(S['avq'].h, 1))
                        else:
                            eb.em('tensor', ins)
                    eb.wait('vector', S['avq'], S['avq'].n)
                    DS['rr'].wait_for(eb, 'vector', rr_base_i + m)
                    if m >= 2 and n2 == 0:
                        st_ds.wait_for(eb, 'vector', st_base_i + m - 2)
                    S['ffdn'].n += 1
                    eb.em('vector', lambda e, m=m, n2=n2, pb=pb: e.tensor_tensor(
                        out=obuf[:, m % 2, n2 * 512:(n2 + 1) * 512], in0=rbuf[:, m % 2, n2 * 512:(n2 + 1) * 512],
                        in1=pb[:, :], op=ALU.add).then_inc(S['ffdn'].h, 1))
                eb.wait('sync', S['ffdn'], dn_base + 2 * (m + 1))
                if ig == 0:
                    st_ds.issue(eb, 'sync', resid_scr[m * 128:(m + 1) * 128, toff:toff + 1024], obuf[:, m % 2, :])
                else:
                    st_ds.issue(eb, 'sync', o_out[m * 128:(m + 1) * 128, toff:toff + 1024], obuf[:, m % 2, :])
                if m + 2 < 8:
                    DS['rr'].issue(eb, 'sync', rbuf[:, m % 2, :], resid_scr[(m + 2) * 128:(m + 3) * 128, toff:toff + 1024])
            if ig == 0:
                DS['rsd'].wait_all(eb, 'sync')
                if half == 1:
                    eb.wait('scalar', S['avq'], S['avq'].n)
                    for it in range(11):
                        S['fw2'].n += 16
                        eb.em('scalar', lambda e, it=it: e.dma_start(
                            out=wdq_b[:, it, :], in_=WdqN[0:1024, 1408 + it * 128:1408 + (it + 1) * 128], transpose=True).then_inc(S['fw2'].h, 16))
            need_ffdn[0] = True
        return go

    for ig in range(2):
        i0 = IG[ig]
        eb.wait('tensor', S['fw'], S['fw'].n if ig == 1 else 16 * 16)
        eb.wait('tensor', S['xqr'], S['xqr'].n)
        for half in range(2):
            toff = half * 1024
            hoisted = (half == 1)
            for tt in range(8):
                eb.wait('vector', S['aab'], S['aab'].n)
                for ci, (c0, clen) in enumerate(NCH):
                    cg = S['hbld'].n
                    if hoisted and tt == 0:
                        pg, py = [PAIR3[0], PAIR3[1], PAIR3[0]][ci]
                        tgt = cg - 1
                    else:
                        pg, py = PAIR3[cg % 3]
                        tgt = cg - 1 if (hoisted and tt == 1) else cg - 2
                    if pg is psC0 and need_ffdn[0]:
                        eb.wait('tensor', S['ffdn'], S['ffdn'].n)
                        need_ffdn[0] = False
                    if tgt > 0:
                        eb.wait('tensor', S['hbld'], tgt)
                    for kc2 in range(KC):
                        eb.em('tensor', lambda e, tt=tt, c0=c0, clen=clen, kc2=kc2, pg=pg, toff=toff: e.matmul(
                            pg[:, 0:clen], xqT_sb[:, kc2, toff + tt * 128:toff + (tt + 1) * 128],
                            wgq_b[:, kc2, c0:c0 + clen], start=(kc2 == 0), stop=(kc2 == KC - 1)))
                    for kc2 in range(KC):
                        ins = (lambda e, tt=tt, c0=c0, clen=clen, kc2=kc2, py=py, toff=toff: e.matmul(
                            py[:, 0:clen], xqT_sb[:, kc2, toff + tt * 128:toff + (tt + 1) * 128],
                            wuq_b[:, kc2, c0:c0 + clen], start=(kc2 == 0), stop=(kc2 == KC - 1)))
                        if kc2 == KC - 1:
                            S['ffh'].n += 1
                            eb.em('tensor', lambda e, ins=ins: ins(e).then_inc(S['ffh'].h, 1))
                        else:
                            eb.em('tensor', ins)
                    eb.wait('scalar', S['ffh'], S['ffh'].n)
                    if S['hbld'].n > 0:
                        eb.wait('scalar', S['hbld'], S['hbld'].n)
                    S['sil'].n += 1
                    eb.em('scalar', lambda e, clen=clen, pg=pg: e.activation(
                        out=gs_c[:, 0:clen], in_=pg[:, 0:clen], func=AF.Silu).then_inc(S['sil'].h, 1))
                    eb.wait('vector', S['sil'], S['sil'].n)
                    S['hbld'].n += 1
                    eb.em('vector', lambda e, c0=c0, clen=clen, py=py: e.tensor_tensor(
                        out=h_t[:, c0:c0 + clen], in0=gs_c[:, 0:clen], in1=py[:, 0:clen],
                        op=ALU.mult).then_inc(S['hbld'].h, 1))
                lg = lgq
                eb.wait('scalar', S['hq'], S['hq'].n)
                eb.wait('scalar', S['hbld'], S['hbld'].n)
                eb.wait('vector', S['hbld'], S['hbld'].n)
                if lg >= 2:
                    eb.wait('vector', S['fftr'], 3 * (lg - 1))
                if pend_tr is not None:
                    pend_tr()
                    pend_tr = None
                last = emit_quant(eb, 1408, h_t[:, 0:1408], hq_t[:, lg % 2, 0:1408], PL5, act_abs=True)
                S['hq'].n += 1
                eb.em('vector', lambda e, last=last: last(e).then_inc(S['hq'].h, 1))
                pend_tr = (lambda lg=lg, tt=tt, hq_tgt=S['hq'].n: emit_tr(eb, lg, tt, hq_tgt))
                lgq += 1
                if hoisted and tt == 0 and pend_down is not None:
                    pend_down()
                    pend_down = None
            if not hoisted:
                pend_down = mk_down(ig, half, toff, S['hbld'].n)
                continue
            if pend_tr is not None:
                pend_tr()
                pend_tr = None
            if ig == 0:
                eb.wait('scalar', S['ffh'], S['ffh'].n)
                for kc2 in range(KC):
                    S['fw'].n += 16
                    eb.em('scalar', lambda e, kc2=kc2: e.dma_start(
                        out=wgq_b[:, kc2, :], in_=WgqN[1408:2816, kc2 * 128:(kc2 + 1) * 128], transpose=True).then_inc(S['fw'].h, 16))
                for kc2 in range(KC):
                    S['fw'].n += 16
                    eb.em('scalar', lambda e, kc2=kc2: e.dma_start(
                        out=wuq_b[:, kc2, :], in_=WuqN[1408:2816, kc2 * 128:(kc2 + 1) * 128], transpose=True).then_inc(S['fw'].h, 16))
            mk_down(ig, half, toff, S['hbld'].n)()
    DS['ost'].wait_all(eb, 'sync')
    eb.wait('sync', S['out'], S['out'].n)
    with nc.Block() as blk:
        eb.flush(blk)
    es_b5.close()
    es_w5.close()
    es_ps.close()
    es_sh.close()
    return nc


def make_inputs(inputs):
    x = np.asarray(inputs["L0_attn_x"], np.float32)
    Wq = np.asarray(inputs["L0_attn_Wq"], np.float32)
    Wkv = np.asarray(inputs["L0_attn_Wkv_down"], np.float32)
    Wku = np.asarray(inputs["L0_attn_Wk_up"], np.float32)
    Wvu = np.asarray(inputs["L0_attn_Wv_up"], np.float32)
    Wo = np.asarray(inputs["L0_attn_Wo"], np.float32)
    Wg = np.ascontiguousarray(np.asarray(inputs["L0_ffn_w_gate"], np.float32))
    Wu = np.ascontiguousarray(np.asarray(inputs["L0_ffn_w_up"], np.float32))
    Wd = np.ascontiguousarray(np.asarray(inputs["L0_ffn_w_down"], np.float32))
    shared = dict(
        WqT=np.ascontiguousarray(Wq.T), WkvT=np.ascontiguousarray(Wkv.T),
        WkuT=np.ascontiguousarray(Wku.T), WvuT=np.ascontiguousarray(Wvu.T),
        WoT=np.ascontiguousarray(Wo.T), Wg=Wg, Wu=Wu, Wd=Wd,
        identD=np.eye(128, dtype=np.float32),
    )
    in_maps = []
    for c in range(8):
        b, hf = c // 2, c % 2
        own = x[b, hf * TC:(hf + 1) * TC]
        oth = x[b, (1 - hf) * TC:(2 - hf) * TC]
        m = dict(shared)
        m["x_tok"] = np.ascontiguousarray(np.concatenate([own, oth], 0))
        m["xT_own"] = np.ascontiguousarray(own.T)
        in_maps.append(m)
    return in_maps


def kernel(**inputs):
    in_maps = make_inputs(inputs)
    nc = build_program(debug=False)
    res = run_bass_kernel_spmd(nc, in_maps, core_ids=list(range(8)))
    out = np.empty((4, TB, D), np.float32)
    for c in range(8):
        b, hf = c // 2, c % 2
        out[b, hf * TC:(hf + 1) * TC] = np.asarray(res.results[c]["o_out"], np.float32).T
    return out


# version 18
# speedup vs baseline: 1.0274x; 1.0274x over previous
import numpy as np
import contextlib
import concourse.bass as bass
import concourse.mybir as mybir
from concourse.bass_utils import run_bass_kernel_spmd

F32 = mybir.dt.float32
BF16 = mybir.dt.bfloat16
I32 = mybir.dt.int32
AF = mybir.ActivationFunctionType
ALU = mybir.AluOpType
AX = mybir.AxisListType

D = 1024
TC = 2048
TB = 4096
NH, HD = 8, 256
DQ = NH * HD
LR = 64
IF = 2816
KC = D // 128
SCALE = 1.0 / 16.0


class Sem:
    def __init__(self, nc, es, name):
        self.h = es.enter_context(nc.semaphore(name))
        self.n = 0


class DmaStream:

    def __init__(self, nc, es, name, nslots=2):
        self.sems = [Sem(nc, es, f"{name}{i}") for i in range(nslots)]
        self.cnt = 0
        self.nslots = nslots

    def issue(self, eb, qeng, out, in_):
        s = self.sems[self.cnt % self.nslots]
        s.n += 16
        eb.em(qeng, lambda e, out=out, in_=in_, s=s: e.dma_start(out=out, in_=in_).then_inc(s.h, 16))
        self.cnt += 1

    def wait_for(self, eb, eng, i):
        s = self.sems[i % self.nslots]
        eb.wait(eng, s, 16 * (i // self.nslots + 1))

    def wait_all(self, eb, eng):
        for s in self.sems:
            eb.wait(eng, s, s.n)


class EB:
    ENGS = ("sync", "tensor", "vector", "scalar", "gpsimd")

    def __init__(self, nc):
        self.nc = nc
        self.q = {e: [] for e in self.ENGS}

    def em(self, eng, fn):
        self.q[eng].append(fn)

    def wait(self, eng, sem, v):
        if v > 0:
            self.em(eng, lambda e: e.wait_ge(sem.h, v))

    def flush(self, blk):
        for eng in self.ENGS:
            lst = self.q[eng]
            if not lst:
                continue
            dec = getattr(blk, eng)

            def make(lst):
                def _(e):
                    for f in lst:
                        f(e)
                return _
            dec(make(lst))
        self.q = {e: [] for e in self.ENGS}


def build_program(debug=False, stop_after=9):
    import os
    B1_NT = int(os.environ.get('B1_NT', '32'))
    B1_QUANT = int(os.environ.get('B1_QUANT', '1'))
    B1_TRANS = int(os.environ.get('B1_TRANS', '1'))
    B1_WQ = int(os.environ.get('B1_WQ', '1'))
    B1_SPILL = int(os.environ.get('B1_SPILL', '1'))
    nc = bass.Bass("TRN2", target_bir_lowering=False)
    x_tok = nc.dram_tensor("x_tok", [TB, D], F32, kind="ExternalInput").ap()
    xT_own = nc.dram_tensor("xT_own", [D, TC], F32, kind="ExternalInput").ap()
    WqT = nc.dram_tensor("WqT", [D, DQ], F32, kind="ExternalInput").ap()
    WkvT = nc.dram_tensor("WkvT", [D, LR], F32, kind="ExternalInput").ap()
    WkuT = nc.dram_tensor("WkuT", [LR, DQ], F32, kind="ExternalInput").ap()
    WvuT = nc.dram_tensor("WvuT", [LR, DQ], F32, kind="ExternalInput").ap()
    WoT = nc.dram_tensor("WoT", [DQ, D], F32, kind="ExternalInput").ap()
    Wg = nc.dram_tensor("Wg", [IF, D], F32, kind="ExternalInput").ap()
    Wu = nc.dram_tensor("Wu", [IF, D], F32, kind="ExternalInput").ap()
    Wd = nc.dram_tensor("Wd", [D, IF], F32, kind="ExternalInput").ap()
    identD = nc.dram_tensor("identD", [128, 128], F32, kind="ExternalInput").ap()
    o_out = nc.dram_tensor("o_out", [D, TC], F32, kind="ExternalOutput").ap()
    WgqN = nc.dram_tensor("WgqN", [IF, D], BF16, kind="Internal").ap()
    WuqN = nc.dram_tensor("WuqN", [IF, D], BF16, kind="Internal").ap()
    WdqN = nc.dram_tensor("WdqN", [D, IF], BF16, kind="Internal").ap()
    xqT_scr = nc.dram_tensor("xqT_scr", [D, TC], BF16, kind="Internal").ap()
    resid_scr = nc.dram_tensor("resid_scr", [D, TC], F32, kind="Internal").ap()
    dbg = {}
    if debug:
        dbg['xnT'] = nc.dram_tensor("d_xnT", [128, KC, TB], BF16, kind="ExternalOutput").ap()
        dbg['kvT'] = nc.dram_tensor("d_kvT", [LR, TB], BF16, kind="ExternalOutput").ap()
        dbg['qT'] = nc.dram_tensor("d_qT", [128, 16, TC], BF16, kind="ExternalOutput").ap()
        dbg['kT0'] = nc.dram_tensor("d_kT0", [128, 2, TB], BF16, kind="ExternalOutput").ap()
        dbg['oall'] = nc.dram_tensor("d_oall", [128, 16, TC], BF16, kind="ExternalOutput").ap()
        dbg['sums'] = nc.dram_tensor("d_sums", [NH, TC], F32, kind="ExternalOutput").ap()
        dbg['resid'] = nc.dram_tensor("d_resid", [D, TC], F32, kind="ExternalOutput").ap()
        dbg['xq'] = nc.dram_tensor("d_xq", [D, TC], BF16, kind="ExternalOutput").ap()
        dbg['hq0'] = nc.dram_tensor("d_hq0", [128, 11, 1024], BF16, kind="ExternalOutput").ap()
        dbg['sml'] = nc.dram_tensor("d_sml", [128, 16], F32, kind="ExternalOutput").ap()
        dbg['bnd'] = nc.dram_tensor("d_bnd", [128, 12], F32, kind="ExternalOutput").ap()

    es_sh = contextlib.ExitStack()

    def sb(name, shape, dt, es):
        return es.enter_context(nc.sbuf_tensor(name, shape, dt))

    def ps_(shape, dt, es, name):
        return es.enter_context(nc.psum_tensor(name, shape, dt))

    S = {}
    for s in ["x_in", "lnv", "lna", "lnd", "lnap", "xq", "tr", "trev",
              "xqs", "win", "wqd", "wqs", "wql", "kvev", "qpj", "wqt", "sco", "exp",
              "avq", "oev", "bnd", "b12", "b23", "b34", "b45", "wo16",
              "xt16", "rs", "rr", "out", "fw", "xqr", "kv_pe", "q_pe", "prep", "pbev",
              "ffh", "sil", "hbld", "hq", "fftr", "fftrev", "ffdn", "ssq", "srw", "wkl", "lad", "aab", "fw2"]:
        S[s] = Sem(nc, es_sh, "s_" + s)

    DS = {}
    for nm, nsl in [("x_in", 2), ("win", 2), ("wqs", 2), ("wql", 2), ("wkl", 4),
                    ("xt", 2), ("rr", 2), ("rsd", 2), ("ost", 2), ("ssq", 1)]:
        DS[nm] = DmaStream(nc, es_sh, "d_" + nm, nsl)

    pbm = sb("pbm", [128, 96], F32, es_sh)
    pscale = sb("pscale", [128, 96], F32, es_sh)
    prcp = sb("prcp", [128, 96], F32, es_sh)
    ident_t = sb("ident_t", [128, 128], F32, es_sh)
    ones_f1 = sb("ones_f1", [1, 128], F32, es_sh)
    kvT_b = sb("kvT_b", [LR, TB], BF16, es_sh)
    sums8 = sb("sums8", [NH, TC], F32, es_sh)
    sumc = sb("sumc", [128, 2], F32, es_sh)
    sqc = sb("sqc", [128, 2], F32, es_sh)
    mc = sb("mc", [128, 2], F32, es_sh)
    msqc = sb("msqc", [128, 2], F32, es_sh)
    varc = sb("varc", [128, 2], F32, es_sh)
    sdc = sb("sdc", [128, 2], F32, es_sh)
    istdc = sb("istdc", [128, 2], F32, es_sh)
    negmsc = sb("negmsc", [128, 2], F32, es_sh)
    bt1 = sb("bt1", [1, 8], F32, es_sh)
    bq2 = sb("bq2", [1, 8], F32, es_sh)
    ones_b128 = sb("ones_b128", [128, 1], BF16, es_sh)
    eps_col = sb("eps_col", [128, 1], F32, es_sh)
    negB_col = sb("negB_col", [128, 1], F32, es_sh)

    es_wq = contextlib.ExitStack()
    wbuf = sb("wbuf", [128, 2, 512], F32, es_wq)
    wqout = sb("wqout", [128, 2, 512], BF16, es_wq)
    srowq = sb("srowq", [1, 1, 512], F32, es_wq)

    es_ps = contextlib.ExitStack()
    psA = ps_([128, 512], F32, es_ps, "psA")
    psB = ps_([128, 512], F32, es_ps, "psB")
    psC0 = ps_([128, 512], F32, es_ps, "psC0")
    psC1 = ps_([128, 512], F32, es_ps, "psC1")
    psD0 = ps_([128, 512], F32, es_ps, "psD0")
    psD1 = ps_([128, 512], F32, es_ps, "psD1")
    psE = ps_([128, 512], F32, es_ps, "psE")
    psF = ps_([128, 512], F32, es_ps, "psF")
    SBK = [psA, psB]
    OAB = [(psC0, psC1, psE), (psD0, psD1, psF)]

    def emit_quant(eb, F, src, dst, PL, act_abs=False):
        NB = F // 16
        pA, pC, pD, pK = PL['pA'], PL['pC'], PL['pD'], PL['pK']
        v = 'vector'
        L = S['lad']
        if act_abs:
            AB = S['aab']
            AB.n += 1
            eb.em('scalar', lambda e: e.activation(out=pA[:, 0:F], in_=src, func=AF.Abs).then_inc(AB.h, 1))
            a_abs = AB.n
            AB.n += 1
            eb.em('scalar', lambda e: e.activation(out=pK[:, 0:F], in_=src, func=AF.Sign).then_inc(AB.h, 1))
            a_sg = AB.n

        def op(fn, inc=True):
            if inc:
                L.n += 1
                eb.em(v, lambda e, fn=fn: fn(e).then_inc(L.h, 1))
            else:
                eb.em(v, fn)
            return L.n

        def wt(tgt):
            eb.wait(v, L, tgt)

        src3 = src.rearrange("p (b w) -> p b w", w=16)
        A3 = pA[:, 0:F].rearrange("p (b w) -> p b w", w=16)
        K3 = pK[:, 0:F].rearrange("p (b w) -> p b w", w=16)
        sb_ = pscale[:, 0:NB].unsqueeze(2).broadcast_to([128, NB, 16])
        rb = prcp[:, 0:NB].unsqueeze(2).broadcast_to([128, NB, 16])
        t_bm = op(lambda e: e.tensor_reduce(out=pbm[:, 0:NB], in_=src3, axis=AX.X, op=ALU.max, apply_absolute_value=True))
        if not act_abs:
            t_abs = op(lambda e: e.scalar_tensor_tensor(out=pA[:, 0:F], in0=src, scalar=-1.0, in1=src, op0=ALU.mult, op1=ALU.max))
            t_sg = op(lambda e: e.tensor_scalar(out=pK[:, 0:F], in0=src, scalar1=0.0, scalar2=2.0, op0=ALU.is_ge, op1=ALU.mult))
        wt(t_bm)
        t_sc = op(lambda e: e.tensor_scalar(out=pscale[:, 0:NB], in0=pbm[:, 0:NB], scalar1=6e-12, scalar2=1.0 / 6.0, op0=ALU.max, op1=ALU.mult))
        if not act_abs:
            wt(t_sg)
            t_sg1 = op(lambda e: e.tensor_single_scalar(out=pK[:, 0:F], in_=pK[:, 0:F], scalar=1.0, op=ALU.subtract))
        wt(t_sc)
        t_rcp = op(lambda e: e.reciprocal(prcp[:, 0:NB], pscale[:, 0:NB]))
        wt(t_rcp)
        if act_abs:
            eb.wait(v, S['aab'], a_abs)
        else:
            wt(t_abs)
        t_A = op(lambda e: e.tensor_tensor(out=A3, in0=A3, in1=rb, op=ALU.mult))
        wt(t_A)
        tc = op(lambda e: e.tensor_single_scalar(out=pC[:, 0:F], in_=pA[:, 0:F], scalar=0.25, op=ALU.is_gt))
        td = op(lambda e: e.tensor_scalar(out=pD[:, 0:F], in0=pA[:, 0:F], scalar1=5.0, scalar2=2.0, op0=ALU.is_gt, op1=ALU.mult))
        wt(tc)
        tc = op(lambda e: e.scalar_tensor_tensor(out=pC[:, 0:F], in0=pA[:, 0:F], scalar=0.75, in1=pC[:, 0:F], op0=ALU.is_gt, op1=ALU.add))
        wt(td)
        td = op(lambda e: e.scalar_tensor_tensor(out=pD[:, 0:F], in0=pA[:, 0:F], scalar=3.5, in1=pD[:, 0:F], op0=ALU.is_gt, op1=ALU.add))
        wt(tc)
        tc = op(lambda e: e.scalar_tensor_tensor(out=pC[:, 0:F], in0=pA[:, 0:F], scalar=1.25, in1=pC[:, 0:F], op0=ALU.is_gt, op1=ALU.add))
        wt(td)
        td = op(lambda e: e.scalar_tensor_tensor(out=pD[:, 0:F], in0=pA[:, 0:F], scalar=2.5, in1=pD[:, 0:F], op0=ALU.is_gt, op1=ALU.add))
        wt(tc)
        tc = op(lambda e: e.scalar_tensor_tensor(out=pC[:, 0:F], in0=pA[:, 0:F], scalar=1.75, in1=pC[:, 0:F], op0=ALU.is_gt, op1=ALU.add))
        wt(tc)
        wt(td)
        t_q = op(lambda e: e.scalar_tensor_tensor(out=pC[:, 0:F], in0=pC[:, 0:F], scalar=0.5, in1=pD[:, 0:F], op0=ALU.mult, op1=ALU.add))
        wt(t_q)
        if act_abs:
            eb.wait(v, S['aab'], a_sg)
        else:
            wt(t_sg1)
        t_f1 = op(lambda e: e.tensor_tensor(out=pC[:, 0:F], in0=pC[:, 0:F], in1=pK[:, 0:F], op=ALU.mult))
        wt(t_f1)
        C3 = pC[:, 0:F].rearrange("p (b w) -> p b w", w=16)
        dst3 = dst.rearrange("p (b w) -> p b w", w=16)
        return lambda e: e.tensor_tensor(out=dst3, in0=C3, in1=sb_, op=ALU.mult)

    wq_units = []
    for W_, WN_ in ((Wg, WgqN), (Wu, WuqN)):
        for i in range(22):
            for j in range(2):
                wq_units.append((W_[i * 128:(i + 1) * 128, j * 512:(j + 1) * 512],
                                 WN_[i * 128:(i + 1) * 128, j * 512:(j + 1) * 512], 512))
    for i in range(8):
        for j in range(6):
            c0 = j * 512
            cl = 512 if j < 5 else 256
            wq_units.append((Wd[i * 128:(i + 1) * 128, c0:c0 + cl],
                             WdqN[i * 128:(i + 1) * 128, c0:c0 + cl], cl))
    NW_UNITS = len(wq_units)

    def emit_wq_units(eb, us, PL):
        for u in us:
            if u >= NW_UNITS:
                continue
            src, dst, F = wq_units[u]
            if u >= 2:
                eb.wait('sync', S['wqd'], u - 1)
            DS['win'].issue(eb, 'sync', wbuf[:, u % 2, 0:F], src)
            DS['win'].wait_for(eb, 'vector', u)
            if u >= 2:
                DS['wqs'].wait_for(eb, 'vector', u - 2)
            last = emit_quant(eb, F, wbuf[:, u % 2, 0:F], wqout[:, u % 2, 0:F], PL)
            S['wqd'].n += 1
            eb.em('vector', lambda e, last=last: last(e).then_inc(S['wqd'].h, 1))
            eb.wait('gpsimd', S['wqd'], S['wqd'].n)
            DS['wqs'].issue(eb, 'gpsimd', dst, wqout[:, u % 2, 0:F])

    def barrier(eb, name):
        eb.wait('sync', S['out'], S['out'].n)
        for eng in EB.ENGS:
            S[name].n += 1
            if eng == 'sync':
                eb.em(eng, lambda e: e.nop().then_inc(S[name].h, 1))
            else:
                eb.em(eng, lambda e: e.drain().then_inc(S[name].h, 1))

    def barrier_wait(eb, name):
        for eng in EB.ENGS:
            eb.wait(eng, S[name], S[name].n)

    es_b12 = contextlib.ExitStack()
    xnT = sb("xnT", [128, KC, TB], BF16, es_b12)
    es_b1 = contextlib.ExitStack()
    PL1 = {k: sb("PL1" + k, [128, 1024], F32, es_b1) for k in ('pA', 'pC', 'pD', 'pK')}
    xbuf = sb("xbuf", [128, 2, 2, 1024], F32, es_b1)
    xn_scr = sb("xn_scr", [128, 2, 1024], F32, es_b1)
    xq_f = sb("xq_f", [128, 2, 1024], F32, es_b1)
    stage_q = sb("stage_q", [128, KC, 512], BF16, es_b1)

    eb = EB(nc)
    eb.em('scalar', lambda e: e.preload_act_table(AF.Sqrt))
    S['xt16'].n += 16
    eb.em('sync', lambda e: e.dma_start(out=ident_t[:], in_=identD).then_inc(S['xt16'].h, 16))
    S['prep'].n += 4
    eb.em('gpsimd', lambda e: e.memset(bt1[:], 0.0).then_inc(S['prep'].h, 1))
    eb.em('gpsimd', lambda e: e.memset(ones_f1[:], 1.0).then_inc(S['prep'].h, 1))
    eb.em('gpsimd', lambda e: e.memset(eps_col[:], 1e-5).then_inc(S['prep'].h, 1))
    eb.em('gpsimd', lambda e: e.memset(ones_b128[:], 1.0).then_inc(S['prep'].h, 1))

    eb.wait('scalar', S['prep'], S['prep'].n)
    NCH_X = (B1_NT + 1) // 2

    def issue_x_chunk(ch):
        if ch >= 2:
            eb.wait('sync', S['lnap'], 2 * (ch - 1))
        DS['x_in'].issue(eb, 'sync', xbuf[:, ch % 2, :, :],
                         x_tok.rearrange("(c t p) d -> p c t d", p=128, t=2)[:, ch, :, :])

    for ch in range(min(2, NCH_X)):
        issue_x_chunk(ch)

    tr_cnt = 0
    eb.wait('tensor', S['xt16'], 16)

    def emit_tile_transposes(eb, src_tile, dst_fn):
        nonlocal tr_cnt
        for b_ in range(2):
            pbank = SBK[tr_cnt % 2]
            if tr_cnt >= 2:
                eb.wait('tensor', S['trev'], tr_cnt - 1)
            for j in range(4):
                c = b_ * 4 + j
                ins = (lambda e, c=c, j=j, pbank=pbank, src_tile=src_tile: e.transpose(
                    pbank[:, j * 128:(j + 1) * 128], src_tile[:, c * 128:(c + 1) * 128], ident_t[:]))
                if j == 3:
                    S['tr'].n += 1
                    eb.em('tensor', lambda e, ins=ins: ins(e).then_inc(S['tr'].h, 1))
                else:
                    eb.em('tensor', ins)
            eb.wait('scalar', S['tr'], S['tr'].n)
            dst = dst_fn(b_)
            src = pbank[:].rearrange("p (c n) -> p c n", c=4)
            S['trev'].n += 1
            eb.em('scalar', lambda e, dst=dst, src=src: e.copy(dst, src).then_inc(S['trev'].h, 1))
            tr_cnt += 1

    for t in range(B1_NT):
        ch = t // 2
        if t % 2 == 0 and 2 <= ch + 1 < NCH_X:
            issue_x_chunk(ch + 1)
        own = t < 16
        xt = xbuf[:, ch % 2, t % 2, :]
        c2 = slice(t % 2, t % 2 + 1)
        DS['x_in'].wait_for(eb, 'vector', ch)
        eb.em('vector', lambda e, xt=xt, c2=c2: e.tensor_reduce(out=sumc[:, c2], in_=xt, axis=AX.X, op=ALU.add))
        eb.em('vector', lambda e, xt=xt, c2=c2: e.scalar_tensor_tensor(
            out=PL1['pA'][:, 0:1024], in0=xt, scalar=1.0, in1=xt, op0=ALU.mult, op1=ALU.mult, accum_out=sqc[:, c2]))
        eb.em('vector', lambda e: e.drain())
        eb.em('vector', lambda e, c2=c2: e.tensor_single_scalar(out=mc[:, c2], in_=sumc[:, c2], scalar=1.0 / D, op=ALU.mult))
        eb.em('vector', lambda e: e.drain())
        eb.em('vector', lambda e, c2=c2: e.tensor_single_scalar(out=msqc[:, c2], in_=mc[:, c2], scalar=mc[:, c2], op=ALU.mult))
        eb.em('vector', lambda e, c2=c2: e.tensor_single_scalar(out=sqc[:, c2], in_=sqc[:, c2], scalar=1.0 / D, op=ALU.mult))
        eb.em('vector', lambda e: e.drain())
        eb.em('vector', lambda e, c2=c2: e.tensor_single_scalar(out=varc[:, c2], in_=sqc[:, c2], scalar=msqc[:, c2], op=ALU.subtract))
        S['lnv'].n += 1
        eb.em('vector', lambda e: e.drain().then_inc(S['lnv'].h, 1))
        eb.wait('scalar', S['lnv'], S['lnv'].n)
        S['lna'].n += 1
        eb.em('scalar', lambda e, c2=c2: e.activation(out=sdc[:, c2], in_=varc[:, c2], func=AF.Sqrt, bias=eps_col[:, 0:1], scale=1.0).then_inc(S['lna'].h, 1))
        eb.wait('vector', S['lna'], S['lna'].n)
        eb.em('vector', lambda e, c2=c2: e.reciprocal(istdc[:, c2], sdc[:, c2]))
        eb.em('vector', lambda e: e.drain())
        if t >= 2:
            eb.wait('vector', S['tr'], tr_cnt)
        S['lnap'].n += 1
        eb.em('vector', lambda e, xt=xt, c2=c2, t=t: e.tensor_scalar(
            out=xn_scr[:, t % 2, :], in0=xt, scalar1=mc[:, c2], scalar2=istdc[:, c2],
            op0=ALU.subtract, op1=ALU.mult).then_inc(S['lnap'].h, 1))
        eb.wait('tensor', S['lnap'], S['lnap'].n)
        if B1_TRANS:
            emit_tile_transposes(eb, xn_scr[:, t % 2, :],
                                 lambda b_, t=t: xnT[:, b_ * 4:(b_ + 1) * 4, t * 128:(t + 1) * 128])
        if own and B1_QUANT:
            eb.wait('vector', S['lnap'], S['lnap'].n)
            last = emit_quant(eb, 1024, xn_scr[:, t % 2, :], xq_f[:, t % 2, :], PL1)
            S['xq'].n += 1
            eb.em('vector', lambda e, last=last: last(e).then_inc(S['xq'].h, 1))
            eb.wait('tensor', S['xq'], S['xq'].n)
            g = t // 4
            if B1_TRANS:
                emit_tile_transposes(eb, xq_f[:, t % 2, :],
                                     lambda b_, t=t: stage_q[:, b_ * 4:(b_ + 1) * 4, (t % 4) * 128:(t % 4 + 1) * 128])
            if B1_TRANS and B1_SPILL and t % 4 == 3:
                eb.wait('sync', S['trev'], S['trev'].n)
                S['xqs'].n += 16
                eb.em('sync', lambda e, g=g: e.dma_start(
                    out=xqT_scr.rearrange("(c p) q -> p c q", p=128)[:, :, g * 512:(g + 1) * 512],
                    in_=stage_q[:]).then_inc(S['xqs'].h, 16))
                if g < 3:
                    eb.wait('vector', S['xqs'], 16 * (g + 1))
    if debug:
        eb.wait('sync', S['trev'], S['trev'].n)
        eb.wait('sync', S['lnap'], S['lnap'].n)
        S['out'].n += 16
        eb.em('sync', lambda e: e.dma_start(out=dbg['xnT'], in_=xnT[:]).then_inc(S['out'].h, 16))
        for i_, tns in enumerate([sumc, sqc, mc, msqc, varc, sdc, istdc, istdc]):
            S['out'].n += 16
            eb.em('sync', lambda e, i_=i_, tns=tns: e.dma_start(out=dbg['sml'][:, 2*i_:2*i_+2], in_=tns[:]).then_inc(S['out'].h, 16))
        S['out'].n += 16
        eb.em('sync', lambda e: e.dma_start(out=dbg['resid'][0:128, 0:2048], in_=xn_scr[:].rearrange("p c d -> p (c d)")).then_inc(S['out'].h, 16))


    if B1_WQ:
        pass
    eb.wait('sync', S['xqs'], S['xqs'].n)
    barrier(eb, 'b12')
    with nc.Block() as blk:
        eb.flush(blk)

    if stop_after <= 1:
        eb2 = EB(nc)
        eb2.wait('sync', S['out'], S['out'].n)
        with nc.Block() as blk:
            eb2.flush(blk)
        return nc
    es_b1.close()

    es_b2 = contextlib.ExitStack()
    PL2 = {k: sb("PL2" + k, [128, 512], F32, es_b2) for k in ('pA', 'pC', 'pD', 'pK')}
    wkv_b = sb("wkv_b", [128, KC, LR], BF16, es_b2)
    sqb2 = sb("sqb2", [128, 2, 512], BF16, es_b2)
    wqchunk = sb("wqchunk", [128, 2, KC, 512], BF16, es_b2)
    es_b23 = contextlib.ExitStack()
    qT_all = es_b23.enter_context(nc.sbuf_tensor("qT_all", [128, 16, TC], BF16, side="right"))

    eb = EB(nc)
    barrier_wait(eb, 'b12')
    eb.em('scalar', lambda e: e.preload_act_table(AF.Exp))
    S['wqt'].n += 16
    eb.em('gpsimd', lambda e: e.dma_start(out=wkv_b[:], in_=WkvT.rearrange("(c p) l -> p c l", p=128)).then_inc(S['wqt'].h, 16))
    eb.wait('tensor', S['wqt'], S['wqt'].n)
    for n in range(TB // 512):
        pb = SBK[n % 2]
        if n >= 2:
            eb.wait('tensor', S['kvev'], n - 1)
        for kc in range(KC):
            ins = (lambda e, n=n, kc=kc, pb=pb: e.matmul(pb[0:LR, 0:512], wkv_b[:, kc, :], xnT[:, kc, n * 512:(n + 1) * 512],
                                                         start=(kc == 0), stop=(kc == KC - 1)))
            if kc == KC - 1:
                S['kv_pe'].n += 1
                eb.em('tensor', lambda e, ins=ins: ins(e).then_inc(S['kv_pe'].h, 1))
            else:
                eb.em('tensor', ins)
        eb.wait('scalar', S['kv_pe'], S['kv_pe'].n)
        S['kvev'].n += 1
        eb.em('scalar', lambda e, n=n, pb=pb: e.copy(kvT_b[:, n * 512:(n + 1) * 512], pb[0:LR, 0:512]).then_inc(S['kvev'].h, 1))
    if debug:
        eb.wait('sync', S['kvev'], S['kvev'].n)
        S['out'].n += 16
        eb.em('sync', lambda e: e.dma_start(out=dbg['kvT'], in_=kvT_b[:]).then_inc(S['out'].h, 16))

    for j in range(4):
        if j >= 2:
            eb.wait('gpsimd', S['q_pe'], 16 * (j - 1))
        DS['wql'].issue(eb, 'gpsimd', wqchunk[:, j % 2, :, :],
                        WqT.rearrange("(c p) m -> p c m", p=128)[:, :, j * 512:(j + 1) * 512])
    eb.wait('tensor', S['kvev'], S['kvev'].n)
    qpj_base = S['qpj'].n
    for j in range(4):
        DS['wql'].wait_for(eb, 'tensor', j)
        for mrel in range(4):
            m = j * 4 + mrel
            for n in range(TC // 512):
                idx = m * 4 + n
                pb = SBK[idx % 2]
                if idx >= 2:
                    eb.wait('tensor', S['qpj'], qpj_base + idx - 1)
                for kc in range(KC):
                    ins = (lambda e, j=j, mrel=mrel, n=n, kc=kc, pb=pb: e.matmul(
                        pb[:, :], wqchunk[:, j % 2, kc, mrel * 128:(mrel + 1) * 128],
                        xnT[:, kc, n * 512:(n + 1) * 512], start=(kc == 0), stop=(kc == KC - 1)))
                    if kc == KC - 1:
                        S['q_pe'].n += 1
                        eb.em('tensor', lambda e, ins=ins: ins(e).then_inc(S['q_pe'].h, 1))
                    else:
                        eb.em('tensor', ins)
                eb.wait('scalar', S['q_pe'], S['q_pe'].n)
                S['qpj'].n += 1
                eb.em('scalar', lambda e, m=m, n=n, pb=pb: e.copy(
                    qT_all[:, m, n * 512:(n + 1) * 512], pb[:, :]).then_inc(S['qpj'].h, 1))
    if debug:
        eb.wait('sync', S['qpj'], S['qpj'].n)
        S['out'].n += 16
        eb.em('sync', lambda e: e.dma_start(out=dbg['qT'], in_=qT_all[:]).then_inc(S['out'].h, 16))

    eb.wait('vector', S['kvev'], S['kvev'].n)
    eb.wait('vector', S['qpj'], S['qpj'].n)
    eb.wait('tensor', S['prep'], S['prep'].n)
    PSR = [psE, psF]
    for sl in range(8):
        S['bnd'].n += 1
        eb.em('vector', lambda e, sl=sl: e.tensor_tensor(
            out=sqb2[0:LR, 0, :], in0=kvT_b[:, sl * 512:(sl + 1) * 512],
            in1=kvT_b[:, sl * 512:(sl + 1) * 512], op=ALU.mult))
        eb.em('vector', lambda e: e.drain().then_inc(S['bnd'].h, 1))
        eb.wait('tensor', S['bnd'], S['bnd'].n)
        if sl >= 2:
            eb.wait('tensor', S['pbev'], S['pbev'].n - 1)
        pse = PSR[sl % 2]
        S['bnd'].n += 1
        eb.em('tensor', lambda e, pse=pse: e.matmul(pse[0:1, 0:512], ones_b128[0:LR, 0:1], sqb2[0:LR, 0, :],
                                                    start=True, stop=True).then_inc(S['bnd'].h, 1))
        eb.wait('vector', S['bnd'], S['bnd'].n)
        eb.em('vector', lambda e, pse=pse: e.tensor_reduce(out=bt1[:, 0:1], in_=pse[0:1, 0:512], axis=AX.X, op=ALU.max))
        eb.em('vector', lambda e: e.drain())
        S['pbev'].n += 1
        if sl == 0:
            eb.em('vector', lambda e: e.tensor_single_scalar(out=bt1[:, 7:8], in_=bt1[:, 0:1], scalar=1.0, op=ALU.mult))
        else:
            eb.em('vector', lambda e: e.tensor_single_scalar(out=bt1[:, 7:8], in_=bt1[:, 0:1], scalar=bt1[:, 7:8], op=ALU.max))
        eb.em('vector', lambda e: e.drain().then_inc(S['pbev'].h, 1))
    for h2 in range(NH):
        for n in range(4):
            for m in range(2):
                qsl = qT_all[:, 2 * h2 + m, n * 512:(n + 1) * 512]
                if m == 1:
                    S['bnd'].n += 1
                    eb.em('vector', lambda e, qsl=qsl, m=m: e.tensor_tensor(out=sqb2[:, m, :], in0=qsl, in1=qsl, op=ALU.mult))
                    eb.em('vector', lambda e: e.drain().then_inc(S['bnd'].h, 1))
                else:
                    eb.em('vector', lambda e, qsl=qsl, m=m: e.tensor_tensor(out=sqb2[:, m, :], in0=qsl, in1=qsl, op=ALU.mult))
            eb.wait('tensor', S['bnd'], S['bnd'].n)
            eb.wait('tensor', S['pbev'], S['pbev'].n - 1)
            pse = PSR[n % 2]
            for m in range(2):
                ins = (lambda e, pse=pse, m=m: e.matmul(pse[0:1, 0:512], ones_b128[:, 0:1], sqb2[:, m, :],
                                                        start=(m == 0), stop=(m == 1)))
                if m == 1:
                    S['bnd'].n += 1
                    eb.em('tensor', lambda e, ins=ins: ins(e).then_inc(S['bnd'].h, 1))
                else:
                    eb.em('tensor', ins)
            eb.wait('vector', S['bnd'], S['bnd'].n)
            eb.em('vector', lambda e, pse=pse: e.tensor_reduce(out=bt1[:, 0:1], in_=pse[0:1, 0:512], axis=AX.X, op=ALU.max))
            eb.em('vector', lambda e: e.drain())
            S['pbev'].n += 1
            if n == 0:
                eb.em('vector', lambda e, h2=h2: e.tensor_single_scalar(out=bq2[:, h2:h2 + 1], in_=bt1[:, 0:1], scalar=1.0, op=ALU.mult))
            else:
                eb.em('vector', lambda e, h2=h2: e.tensor_single_scalar(out=bq2[:, h2:h2 + 1], in_=bt1[:, 0:1], scalar=bq2[:, h2:h2 + 1], op=ALU.max))
            eb.em('vector', lambda e: e.drain().then_inc(S['pbev'].h, 1))
    barrier(eb, 'b23')
    with nc.Block() as blk:
        eb.flush(blk)

    if stop_after <= 2:
        eb2 = EB(nc)
        eb2.wait('sync', S['out'], S['out'].n)
        with nc.Block() as blk:
            eb2.flush(blk)
        return nc
    es_b2.close()
    es_b12.close()

    es_b34 = contextlib.ExitStack()
    o_all = sb("o_all", [128, 16, TC], BF16, es_b34)
    es_b3 = contextlib.ExitStack()
    PL3 = {k: sb("PL3" + k, [128, 512], F32, es_b3) for k in ('pA', 'pC', 'pD', 'pK')}
    sqv_t = sb("sqv_t", [128, 1024], BF16, es_b3)
    wkuh = sb("wkuh", [LR, 2, 256], BF16, es_b3)
    wvuh = sb("wvuh", [LR, 2, 256], BF16, es_b3)
    kT_h = sb("kT_h", [128, 2, TB], BF16, es_b3)
    V_h = sb("V_h", [128, 32, 258], BF16, es_b3)
    pbuf = sb("pbuf", [128, 2, 512], BF16, es_b3)

    eb = EB(nc)
    barrier_wait(eb, 'b23')
    S['prep'].n += 1
    eb.em('gpsimd', lambda e: e.memset(V_h[:, :, 256:257], 1.0).then_inc(S['prep'].h, 1))
    prep_end = [None] * NH

    def emit_head_wload(h):
        if h >= 2 and prep_end[h - 2] is not None:
            eb.wait('gpsimd', S['prep'], prep_end[h - 2])
        DS['wkl'].issue(eb, 'gpsimd', wkuh[:, h % 2, :], WkuT[:, h * 256:(h + 1) * 256])
        DS['wkl'].issue(eb, 'gpsimd', wvuh[:, h % 2, :], WvuT[:, h * 256:(h + 1) * 256])

    emit_head_wload(0)
    emit_head_wload(1)

    wq_next = 0
    qglob = 0
    cglob = 0
    kt0_dump_tgt = [0]
    for h in range(NH):
        if debug and h == 1:
            eb.wait('vector', S['out'], kt0_dump_tgt[0])
        DS['wkl'].wait_for(eb, 'tensor', 2 * h)
        DS['wkl'].wait_for(eb, 'tensor', 2 * h + 1)
        eb.wait('tensor', S['prep'], S['prep'].n)
        if h > 0:
            eb.wait('tensor', S['exp'], cglob)
        kv_base_a = S['kvev'].n
        kv_base_d = S['qpj'].n
        if h > 0:
            eb.wait('tensor', S['oev'], 4 * h)
        PR4 = [psA, psB, psE, psF]

        def prep_done_wait(eng, upto):
            eb.wait(eng, S['kvev'], kv_base_a + (upto + 2) // 2)
            eb.wait(eng, S['qpj'], kv_base_d + (upto + 1) // 2)

        def emit_prep(rng):
            for idx in rng:
                pb = PR4[idx % 4]
                if idx >= 4:
                    pidx = idx - 4
                    if pidx % 2 == 0:
                        eb.wait('tensor', S['kvev'], kv_base_a + pidx // 2 + 1)
                    else:
                        eb.wait('tensor', S['qpj'], kv_base_d + pidx // 2 + 1)
                if idx < 16:
                    m, n = idx // 8, idx % 8
                    ins2 = [(lambda e, h=h, m=m, n=n, pb=pb: e.matmul(
                        pb[:, :], wkuh[:, h % 2, m * 128:(m + 1) * 128],
                        kvT_b[:, n * 512:(n + 1) * 512], start=True, stop=True))]
                    eva = (lambda e, m=m, n=n, pb=pb: e.copy(
                        kT_h[:, m, n * 512:(n + 1) * 512], pb[:, :]))
                    evd = (lambda e, m=m, n=n, pb=pb: e.tensor_copy(
                        kT_h[:, m, n * 512:(n + 1) * 512], pb[:, :]))
                else:
                    mt = (idx - 16) * 2
                    ins2 = [(lambda e, h=h, mt=mt, pb=pb: e.matmul(
                        pb[0:128, 0:256], kvT_b[:, mt * 128:(mt + 1) * 128],
                        wvuh[:, h % 2, :], start=True, stop=True)),
                            (lambda e, h=h, mt=mt, pb=pb: e.matmul(
                        pb[0:128, 256:512], kvT_b[:, (mt + 1) * 128:(mt + 2) * 128],
                        wvuh[:, h % 2, :], start=True, stop=True))]
                    dst = V_h[:, mt:mt + 2, 0:256]
                    srcp = pb[0:128, 0:512].rearrange("p (c n) -> p c n", c=2)
                    eva = (lambda e, dst=dst, srcp=srcp: e.copy(dst, srcp))
                    evd = (lambda e, dst=dst, srcp=srcp: e.tensor_copy(dst, srcp))
                for k2, ins in enumerate(ins2):
                    S['prep'].n += 1
                    eb.em('tensor', lambda e, ins=ins: ins(e).then_inc(S['prep'].h, 1))
                eb.wait('scalar', S['prep'], S['prep'].n)
                if idx % 2 == 0:
                    S['kvev'].n += 1
                    eb.em('scalar', lambda e, eva=eva: eva(e).then_inc(S['kvev'].h, 1))
                else:
                    S['qpj'].n += 1
                    eb.em('scalar', lambda e, eva=eva: eva(e).then_inc(S['qpj'].h, 1))

        emit_prep(range(16))
        NB4 = [psC0, psC1, psD0, psD1]
        eb.wait('scalar', S['qpj'], kv_base_d + 8)
        mmp_tgt = [0] * 8
        for grp in range(8):
            emit_prep(range(16 + 2 * grp, 16 + 2 * grp + 2))
            if grp >= 1:
                eb.wait('scalar', S['bnd'], mmp_tgt[grp - 1])
            sq_base = S['sco'].n
            for m in range(2):
                S['sco'].n += 1
                eb.em('scalar', lambda e, grp=grp, m=m: e.activation(
                    out=sqv_t[:, m * 512:(m + 1) * 512], in_=kT_h[:, m, grp * 512:(grp + 1) * 512],
                    func=AF.Square).then_inc(S['sco'].h, 1))
            pse = NB4[grp % 4]
            if grp >= 4:
                eb.wait('tensor', S['pbev'], S['pbev'].n - 3)
            eb.wait('tensor', S['sco'], S['sco'].n)
            for m in range(2):
                ins = (lambda e, pse=pse, m=m: e.matmul(
                    pse[0:1, 0:512], ones_b128[:, 0:1], sqv_t[:, m * 512:(m + 1) * 512],
                    start=(m == 0), stop=(m == 1)))
                if m == 1:
                    S['bnd'].n += 1
                    eb.em('tensor', lambda e, ins=ins: ins(e).then_inc(S['bnd'].h, 1))
                else:
                    eb.em('tensor', ins)
            mmp_tgt[grp] = S['bnd'].n
            eb.wait('vector', S['bnd'], S['bnd'].n)
            eb.em('vector', lambda e, pse=pse: e.tensor_reduce(out=bt1[:, 0:1], in_=pse[0:1, 0:512], axis=AX.X, op=ALU.max))
            eb.em('vector', lambda e: e.drain())
            S['pbev'].n += 1
            if grp == 0:
                eb.em('vector', lambda e: e.tensor_single_scalar(out=bt1[:, 2:3], in_=bt1[:, 0:1], scalar=1.0, op=ALU.mult))
            else:
                eb.em('vector', lambda e: e.tensor_single_scalar(out=bt1[:, 2:3], in_=bt1[:, 0:1], scalar=bt1[:, 2:3], op=ALU.max))
            eb.em('vector', lambda e: e.drain().then_inc(S['pbev'].h, 1))
        prep_end[h] = S['prep'].n
        if h + 2 < NH:
            emit_head_wload(h + 2)
        if debug and h == 0:
            eb.wait('sync', S['kvev'], S['kvev'].n)
            S['out'].n += 16
            eb.em('sync', lambda e: e.dma_start(out=dbg['kT0'], in_=kT_h[:]).then_inc(S['out'].h, 16))

        eb.wait('vector', S['exp'], 128 * h)
        eb.em('vector', lambda e, h=h: e.tensor_single_scalar(out=bt1[:, 3:4], in_=bq2[:, h:h + 1], scalar=bt1[:, 2:3], op=ALU.add))
        eb.em('vector', lambda e: e.drain())
        S['bnd'].n += 1
        eb.em('vector', lambda e: e.tensor_single_scalar(out=bt1[:, 6:7], in_=bt1[:, 3:4], scalar=-1.02 / 32.0, op=ALU.mult))
        eb.em('vector', lambda e: e.drain().then_inc(S['bnd'].h, 1))
        eb.wait('tensor', S['bnd'], S['bnd'].n)
        eb.wait('tensor', S['pbev'], S['pbev'].n)
        S['bnd'].n += 1
        eb.em('tensor', lambda e: e.matmul(psE[0:128, 0:1], ones_f1[:, :], bt1[0:1, 6:7], start=True, stop=True).then_inc(S['bnd'].h, 1))
        eb.wait('vector', S['bnd'], S['bnd'].n)
        S['pbev'].n += 1
        eb.em('vector', lambda e: e.tensor_single_scalar(out=negB_col[:], in_=psE[0:128, 0:1], scalar=1.0, op=ALU.mult))
        eb.em('vector', lambda e: e.drain().then_inc(S['pbev'].h, 1))
        S['bnd'].n += 1
        eb.em('vector', lambda e: e.nop().then_inc(S['bnd'].h, 1))
        bnd_ready = S['bnd'].n
        if debug and h == 0:
            eb.wait('sync', S['bnd'], bnd_ready)
            S['out'].n += 32
            eb.em('sync', lambda e: e.dma_start(out=dbg['bnd'][0:1, 0:8], in_=bt1[:]).then_inc(S['out'].h, 16))
            eb.em('sync', lambda e: e.dma_start(out=dbg['bnd'][:, 8:9], in_=negB_col[:], allow_slow_non_contiguous=True).then_inc(S['out'].h, 16))
            eb.wait('scalar', S['out'], S['out'].n)
            kt0_dump_tgt[0] = S['out'].n

        eb.wait('tensor', S['kvev'], S['kvev'].n)
        eb.wait('tensor', S['qpj'], S['qpj'].n)
        pbev_ready = S['pbev'].n
        for qq in range(4):
            par = qq % 2
            o0, o1, osum = OAB[par]
            exp_tgt = [0] * 32

            def emit_scores(kc):
                nonlocal cglob
                pb = SBK[cglob % 2]
                if cglob >= 2:
                    eb.wait('tensor', S['exp'], cglob - 1)
                if h == 0 and qq == 0 and kc == 0:
                    eb.wait('tensor', S['pbev'], S['pbev'].n)
                for m in range(2):
                    ins = (lambda e, h=h, qq=qq, kc=kc, m=m, pb=pb: e.matmul(
                        pb[:, :], kT_h[:, m, kc * 128:(kc + 1) * 128],
                        qT_all[:, 2 * h + m, qq * 512:(qq + 1) * 512],
                        start=(m == 0), stop=(m == 1)))
                    if m == 1:
                        S['sco'].n += 1
                        eb.em('tensor', lambda e, ins=ins: ins(e).then_inc(S['sco'].h, 1))
                    else:
                        eb.em('tensor', ins)
                eb.wait('scalar', S['sco'], S['sco'].n)
                if qq == 0 and kc == 0:
                    eb.wait('scalar', S['bnd'], bnd_ready)
                S['exp'].n += 1
                exp_tgt[kc] = S['exp'].n
                eb.em('scalar', lambda e, cglob=cglob, pb=pb: e.activation(
                    out=pbuf[:, cglob % 2, :], in_=pb[:, :], func=AF.Exp,
                    bias=negB_col[:], scale=SCALE).then_inc(S['exp'].h, 1))
                cglob += 1

            def emit_av(kc, cg):
                eb.wait('tensor', S['exp'], exp_tgt[kc])
                if kc == 0 and qglob >= 2:
                    eb.wait('tensor', S['oev'], qglob - 1)
                if kc == 0 and qq == 0:
                    eb.wait('tensor', S['pbev'], pbev_ready)
                for mt2 in range(3):
                    lo = [0, 128, 256][mt2]
                    hi = [128, 256, 257][mt2]
                    dstp = [o0, o1, osum][mt2]
                    dsl = (slice(0, 128), slice(0, 128), slice(0, 1))[mt2]
                    ins = (lambda e, kc=kc, cg=cg, lo=lo, hi=hi, dstp=dstp, dsl=dsl: e.matmul(
                        dstp[dsl, 0:512], V_h[:, kc, lo:hi], pbuf[:, cg % 2, :],
                        start=(kc == 0), stop=(kc == 31)))
                    if mt2 == 2 and kc == 31:
                        S['avq'].n += 1
                        eb.em('tensor', lambda e, ins=ins: ins(e).then_inc(S['avq'].h, 1))
                    else:
                        eb.em('tensor', ins)

            cg0 = cglob
            for kc in range(32):
                emit_scores(kc)
                if kc >= 1:
                    emit_av(kc - 1, cg0 + kc - 1)
            emit_av(31, cg0 + 31)
            eb.wait('scalar', S['avq'], S['avq'].n)
            if qglob >= 1:
                DS['ssq'].wait_for(eb, 'scalar', qglob - 1)
            for m in range(2):
                op = [o0, o1][m]
                eb.em('scalar', lambda e, h=h, qq=qq, m=m, op=op: e.copy(
                    o_all[:, 2 * h + m, qq * 512:(qq + 1) * 512], op[:, :]))
            S['oev'].n += 1
            eb.em('scalar', lambda e, qglob=qglob, osum=osum: e.copy(
                srowq[:, 0, :], osum[0:1, 0:512]).then_inc(S['oev'].h, 1))
            eb.wait('gpsimd', S['oev'], S['oev'].n)
            DS['ssq'].issue(eb, 'gpsimd', sums8[h:h + 1, qq * 512:(qq + 1) * 512], srowq[:, 0, :])
            qglob += 1
            kq = [6, 6, 5, 0][(qglob - 1) % 4]
            emit_wq_units(eb, range(wq_next, wq_next + kq), PL3)
            wq_next = min(wq_next + kq, NW_UNITS)

    emit_wq_units(eb, range(wq_next, NW_UNITS), PL3)
    wq_next = NW_UNITS
    if debug:
        eb.wait('sync', S['oev'], S['oev'].n)
        S['out'].n += 16
        eb.em('sync', lambda e: e.dma_start(out=dbg['oall'], in_=o_all[:]).then_inc(S['out'].h, 16))

    barrier(eb, 'b34')
    with nc.Block() as blk:
        eb.flush(blk)

    if stop_after <= 3:
        eb2 = EB(nc)
        eb2.wait('sync', S['out'], S['out'].n)
        with nc.Block() as blk:
            eb2.flush(blk)
        return nc
    es_b3.close()
    es_b23.close()

    es_w5 = contextlib.ExitStack()
    wgq_b = es_w5.enter_context(nc.sbuf_tensor("wgq_b", [128, KC, 1408], BF16, side="right"))
    wuq_b = es_w5.enter_context(nc.sbuf_tensor("wuq_b", [128, KC, 1408], BF16, side="right"))
    es_b4 = contextlib.ExitStack()
    wo_b = sb("wo_b", [128, 16, D], BF16, es_b4)
    xt_buf = sb("xt_buf", [128, 2, TC], F32, es_b4)
    rstage = sb("rstage", [128, 2, TC], F32, es_b4)
    srow = sb("srow", [1, TC], F32, es_b4)

    eb = EB(nc)
    barrier_wait(eb, 'b34')
    S['wo16'].n += 16
    eb.em('gpsimd', lambda e: e.dma_start(out=wo_b[:], in_=WoT.rearrange("(c p) d -> p c d", p=128)).then_inc(S['wo16'].h, 16))
    DS['wqs'].wait_all(eb, 'scalar')
    for kc2 in range(KC):
        S['fw'].n += 16
        eb.em('scalar', lambda e, kc2=kc2: e.dma_start(
            out=wgq_b[:, kc2, :], in_=WgqN[0:1408, kc2 * 128:(kc2 + 1) * 128], transpose=True).then_inc(S['fw'].h, 16))
    for kc2 in range(KC):
        S['fw'].n += 16
        eb.em('scalar', lambda e, kc2=kc2: e.dma_start(
            out=wuq_b[:, kc2, :], in_=WuqN[0:1408, kc2 * 128:(kc2 + 1) * 128], transpose=True).then_inc(S['fw'].h, 16))

    DS['ssq'].wait_all(eb, 'vector')
    rec_done = S['oev'].n + 1
    S['oev'].n += 1
    eb.em('vector', lambda e: e.reciprocal(sums8[:, :], sums8[:, :]))
    eb.em('vector', lambda e: e.drain().then_inc(S['oev'].h, 1))
    if debug:
        eb.wait('sync', S['oev'], rec_done)
        S['out'].n += 16
        eb.em('sync', lambda e: e.dma_start(out=dbg['sums'], in_=sums8[:, :]).then_inc(S['out'].h, 16))
    norm_base = S['oev'].n
    NBNK = [psC0, psC1, psD0, psD1]
    for h in range(NH):
        eb.wait('sync', S['oev'], rec_done if h == 0 else norm_base + 4 * h)
        S['srw'].n += 16
        eb.em('sync', lambda e, h=h: e.dma_start(out=srow[0:1, :], in_=sums8[h:h + 1, :]).then_inc(S['srw'].h, 16))
        eb.wait('tensor', S['srw'], S['srw'].n)
        for n in range(4):
            idx = h * 4 + n
            pb = NBNK[idx % 4]
            if idx >= 4:
                eb.wait('tensor', S['oev'], norm_base + idx - 3)
            S['avq'].n += 1
            eb.em('tensor', lambda e, n=n, pb=pb: e.matmul(
                pb[:, :], ones_f1[:, :], srow[0:1, n * 512:(n + 1) * 512],
                start=True, stop=True).then_inc(S['avq'].h, 1))
            eb.wait('vector', S['avq'], S['avq'].n)
            for m in range(2):
                sl = o_all[:, 2 * h + m, n * 512:(n + 1) * 512]
                ins = (lambda e, sl=sl, pb=pb: e.tensor_tensor(out=sl, in0=sl, in1=pb[:, :], op=ALU.mult))
                if m == 1:
                    S['oev'].n += 1
                    eb.em('vector', lambda e, ins=ins: ins(e).then_inc(S['oev'].h, 1))
                else:
                    eb.em('vector', ins)
    eb.wait('tensor', S['wo16'], S['wo16'].n)
    eb.wait('tensor', S['oev'], S['oev'].n)
    wo_base = S['oev'].n
    for m in range(2):
        DS['xt'].issue(eb, 'sync', xt_buf[:, m % 2, :], xT_own[m * 128:(m + 1) * 128, :])
    for m in range(8):
        for n in range(4):
            idx = 4 * m + n
            pb = [psC0, psC1][idx % 2]
            if idx >= 2:
                eb.wait('tensor', S['oev'], wo_base + idx - 1)
            for kc in range(16):
                ins = (lambda e, m=m, n=n, kc=kc, pb=pb: e.matmul(
                    pb[:, :], wo_b[:, kc, m * 128:(m + 1) * 128], o_all[:, kc, n * 512:(n + 1) * 512],
                    start=(kc == 0), stop=(kc == 15)))
                if kc == 15:
                    S['avq'].n += 1
                    eb.em('tensor', lambda e, ins=ins: ins(e).then_inc(S['avq'].h, 1))
                else:
                    eb.em('tensor', ins)
            eb.wait('vector', S['avq'], S['avq'].n)
            DS['xt'].wait_for(eb, 'vector', m)
            if m >= 2 and n == 0:
                DS['rsd'].wait_for(eb, 'vector', m - 2)
            S['oev'].n += 1
            eb.em('vector', lambda e, m=m, n=n, pb=pb: e.tensor_tensor(
                out=rstage[:, m % 2, n * 512:(n + 1) * 512], in0=xt_buf[:, m % 2, n * 512:(n + 1) * 512],
                in1=pb[:, :], op=ALU.add).then_inc(S['oev'].h, 1))
        eb.wait('sync', S['oev'], wo_base + 4 * (m + 1))
        DS['rsd'].issue(eb, 'sync', resid_scr[m * 128:(m + 1) * 128, :], rstage[:, m % 2, :])
        if m + 2 < 8:
            DS['xt'].issue(eb, 'sync', xt_buf[:, m % 2, :], xT_own[(m + 2) * 128:(m + 3) * 128, :])
    if debug:
        DS['rsd'].wait_all(eb, 'sync')

    DS['wqs'].wait_all(eb, 'sync')
    DS['rsd'].wait_all(eb, 'sync')
    barrier(eb, 'b45')
    with nc.Block() as blk:
        eb.flush(blk)

    if stop_after <= 4:
        eb2 = EB(nc)
        eb2.wait('sync', S['out'], S['out'].n)
        with nc.Block() as blk:
            eb2.flush(blk)
        return nc
    es_b4.close()
    es_b34.close()
    es_wq.close()

    es_b5 = contextlib.ExitStack()
    PL5 = {k: sb("PL5" + k, [128, 1408], F32, es_b5) for k in ('pA', 'pC', 'pD', 'pK')}
    xqT_sb = sb("xqT_sb", [128, KC, TC], BF16, es_b5)
    wdq_b = sb("wdq_b", [128, 11, 1024], BF16, es_b5)
    gs_c = sb("gs_c", [128, 512], F32, es_b5)
    h_t = sb("h_t", [128, 1408], F32, es_b5)
    hq_t = sb("hq_t", [128, 2, 1408], F32, es_b5)
    hqT_b = sb("hqT_b", [128, 11, 1024], BF16, es_b5)
    rbuf = sb("rbuf", [128, 2, 1024], F32, es_b5)
    obuf = sb("obuf", [128, 2, 1024], F32, es_b5)

    eb = EB(nc)
    barrier_wait(eb, 'b45')
    eb.em('scalar', lambda e: e.preload_act_table(AF.Silu))
    S['xqr'].n += 16
    eb.em('sync', lambda e: e.dma_start(out=xqT_sb[:], in_=xqT_scr.rearrange("(c p) q -> p c q", p=128)).then_inc(S['xqr'].h, 16))
    for it in range(11):
        S['fw2'].n += 16
        eb.em('scalar', lambda e, it=it: e.dma_start(
            out=wdq_b[:, it, :], in_=WdqN[0:1024, it * 128:(it + 1) * 128], transpose=True).then_inc(S['fw2'].h, 16))

    IG = [0, 1408]
    NCH = [(0, 512), (512, 512), (1024, 384)]
    ftr_glob = 0
    lgq = 0
    pend_tr = None

    def emit_tr(eb, lg, tt, hq_tgt):
        nonlocal ftr_glob
        if ftr_glob % 24 == 0 and ftr_glob > 0:
            eb.wait('scalar', S['avq'], S['avq'].n)
        eb.wait('tensor', S['hq'], hq_tgt)
        for b_ in range(3):
            js = [4, 4, 3][b_]
            pbk = SBK[ftr_glob % 2]
            if ftr_glob >= 2:
                eb.wait('tensor', S['fftrev'], ftr_glob - 1)
            for j in range(js):
                c = b_ * 4 + j
                ins = (lambda e, c=c, j=j, pbk=pbk, lg=lg: e.transpose(
                    pbk[:, j * 128:(j + 1) * 128], hq_t[:, lg % 2, c * 128:(c + 1) * 128], ident_t[:]))
                if j == js - 1:
                    S['fftr'].n += 1
                    eb.em('tensor', lambda e, ins=ins: ins(e).then_inc(S['fftr'].h, 1))
                else:
                    eb.em('tensor', ins)
            eb.wait('scalar', S['fftr'], S['fftr'].n)
            S['fftrev'].n += 1
            dst = hqT_b[:, b_ * 4:b_ * 4 + js, tt * 128:(tt + 1) * 128]
            src = pbk[:, 0:js * 128].rearrange("p (c n) -> p c n", c=js)
            eb.em('scalar', lambda e, dst=dst, src=src: e.copy(dst, src).then_inc(S['fftrev'].h, 1))
            ftr_glob += 1
    PAIR3 = [(psD0, psD1), (psE, psF), (psC0, psC1)]
    need_ffdn = [False]
    pend_down = None

    def mk_down(ig, half, toff, hbld_tgt):
        def go():
            dn_base = S['ffdn'].n
            eb.wait('tensor', S['fw2'], (11 if ig == 0 else 22) * 16)
            st_ds = DS['rsd'] if ig == 0 else DS['ost']
            rr_base_i = DS['rr'].cnt
            st_base_i = st_ds.cnt
            for m in range(2):
                DS['rr'].issue(eb, 'sync', rbuf[:, m % 2, :], resid_scr[m * 128:(m + 1) * 128, toff:toff + 1024])
            eb.wait('tensor', S['fftrev'], 24 * (2 * ig + half + 1))
            eb.wait('tensor', S['hbld'], hbld_tgt)
            DS['rsd'].wait_all(eb, 'vector')
            DS['ost'].wait_all(eb, 'vector')
            for m in range(8):
                for n2 in range(2):
                    idx = m * 2 + n2
                    pb = [psC0, psC1][idx % 2]
                    if idx >= 2:
                        eb.wait('tensor', S['ffdn'], dn_base + idx - 1)
                    for it in range(11):
                        ins = (lambda e, m=m, n2=n2, it=it, pb=pb: e.matmul(
                            pb[:, :], wdq_b[:, it, m * 128:(m + 1) * 128], hqT_b[:, it, n2 * 512:(n2 + 1) * 512],
                            start=(it == 0), stop=(it == 10)))
                        if it == 10:
                            S['avq'].n += 1
                            eb.em('tensor', lambda e, ins=ins: ins(e).then_inc(S['avq'].h, 1))
                        else:
                            eb.em('tensor', ins)
                    eb.wait('vector', S['avq'], S['avq'].n)
                    DS['rr'].wait_for(eb, 'vector', rr_base_i + m)
                    if m >= 2 and n2 == 0:
                        st_ds.wait_for(eb, 'vector', st_base_i + m - 2)
                    S['ffdn'].n += 1
                    eb.em('vector', lambda e, m=m, n2=n2, pb=pb: e.tensor_tensor(
                        out=obuf[:, m % 2, n2 * 512:(n2 + 1) * 512], in0=rbuf[:, m % 2, n2 * 512:(n2 + 1) * 512],
                        in1=pb[:, :], op=ALU.add).then_inc(S['ffdn'].h, 1))
                eb.wait('sync', S['ffdn'], dn_base + 2 * (m + 1))
                if ig == 0:
                    st_ds.issue(eb, 'sync', resid_scr[m * 128:(m + 1) * 128, toff:toff + 1024], obuf[:, m % 2, :])
                else:
                    st_ds.issue(eb, 'sync', o_out[m * 128:(m + 1) * 128, toff:toff + 1024], obuf[:, m % 2, :])
                if m + 2 < 8:
                    DS['rr'].issue(eb, 'sync', rbuf[:, m % 2, :], resid_scr[(m + 2) * 128:(m + 3) * 128, toff:toff + 1024])
            if ig == 0:
                DS['rsd'].wait_all(eb, 'sync')
                if half == 1:
                    eb.wait('scalar', S['avq'], S['avq'].n)
                    for it in range(11):
                        S['fw2'].n += 16
                        eb.em('scalar', lambda e, it=it: e.dma_start(
                            out=wdq_b[:, it, :], in_=WdqN[0:1024, 1408 + it * 128:1408 + (it + 1) * 128], transpose=True).then_inc(S['fw2'].h, 16))
            need_ffdn[0] = True
        return go

    for ig in range(2):
        i0 = IG[ig]
        eb.wait('tensor', S['fw'], S['fw'].n if ig == 1 else 16 * 16)
        eb.wait('tensor', S['xqr'], S['xqr'].n)
        for half in range(2):
            toff = half * 1024
            hoisted = (half == 1)
            for tt in range(8):
                eb.wait('vector', S['aab'], S['aab'].n)
                for ci, (c0, clen) in enumerate(NCH):
                    cg = S['hbld'].n
                    if hoisted and tt == 0:
                        pg, py = [PAIR3[0], PAIR3[1], PAIR3[0]][ci]
                        tgt = cg - 1
                    else:
                        pg, py = PAIR3[cg % 3]
                        tgt = cg - 1 if (hoisted and tt == 1) else cg - 2
                    if pg is psC0 and need_ffdn[0]:
                        eb.wait('tensor', S['ffdn'], S['ffdn'].n)
                        need_ffdn[0] = False
                    if tgt > 0:
                        eb.wait('tensor', S['hbld'], tgt)
                    for kc2 in range(KC):
                        eb.em('tensor', lambda e, tt=tt, c0=c0, clen=clen, kc2=kc2, pg=pg, toff=toff: e.matmul(
                            pg[:, 0:clen], xqT_sb[:, kc2, toff + tt * 128:toff + (tt + 1) * 128],
                            wgq_b[:, kc2, c0:c0 + clen], start=(kc2 == 0), stop=(kc2 == KC - 1)))
                    for kc2 in range(KC):
                        ins = (lambda e, tt=tt, c0=c0, clen=clen, kc2=kc2, py=py, toff=toff: e.matmul(
                            py[:, 0:clen], xqT_sb[:, kc2, toff + tt * 128:toff + (tt + 1) * 128],
                            wuq_b[:, kc2, c0:c0 + clen], start=(kc2 == 0), stop=(kc2 == KC - 1)))
                        if kc2 == KC - 1:
                            S['ffh'].n += 1
                            eb.em('tensor', lambda e, ins=ins: ins(e).then_inc(S['ffh'].h, 1))
                        else:
                            eb.em('tensor', ins)
                    eb.wait('scalar', S['ffh'], S['ffh'].n)
                    if S['hbld'].n > 0:
                        eb.wait('scalar', S['hbld'], S['hbld'].n)
                    S['sil'].n += 1
                    eb.em('scalar', lambda e, clen=clen, pg=pg: e.activation(
                        out=gs_c[:, 0:clen], in_=pg[:, 0:clen], func=AF.Silu).then_inc(S['sil'].h, 1))
                    eb.wait('vector', S['sil'], S['sil'].n)
                    S['hbld'].n += 1
                    eb.em('vector', lambda e, c0=c0, clen=clen, py=py: e.tensor_tensor(
                        out=h_t[:, c0:c0 + clen], in0=gs_c[:, 0:clen], in1=py[:, 0:clen],
                        op=ALU.mult).then_inc(S['hbld'].h, 1))
                lg = lgq
                eb.wait('scalar', S['hq'], S['hq'].n)
                eb.wait('scalar', S['hbld'], S['hbld'].n)
                eb.wait('vector', S['hbld'], S['hbld'].n)
                if lg >= 2:
                    eb.wait('vector', S['fftr'], 3 * (lg - 1))
                if pend_tr is not None:
                    pend_tr()
                    pend_tr = None
                last = emit_quant(eb, 1408, h_t[:, 0:1408], hq_t[:, lg % 2, 0:1408], PL5, act_abs=True)
                S['hq'].n += 1
                eb.em('vector', lambda e, last=last: last(e).then_inc(S['hq'].h, 1))
                pend_tr = (lambda lg=lg, tt=tt, hq_tgt=S['hq'].n: emit_tr(eb, lg, tt, hq_tgt))
                lgq += 1
                if hoisted and tt == 0 and pend_down is not None:
                    pend_down()
                    pend_down = None
            if not hoisted:
                pend_down = mk_down(ig, half, toff, S['hbld'].n)
                continue
            if pend_tr is not None:
                pend_tr()
                pend_tr = None
            if ig == 0:
                eb.wait('scalar', S['ffh'], S['ffh'].n)
                for kc2 in range(KC):
                    S['fw'].n += 16
                    eb.em('scalar', lambda e, kc2=kc2: e.dma_start(
                        out=wgq_b[:, kc2, :], in_=WgqN[1408:2816, kc2 * 128:(kc2 + 1) * 128], transpose=True).then_inc(S['fw'].h, 16))
                for kc2 in range(KC):
                    S['fw'].n += 16
                    eb.em('scalar', lambda e, kc2=kc2: e.dma_start(
                        out=wuq_b[:, kc2, :], in_=WuqN[1408:2816, kc2 * 128:(kc2 + 1) * 128], transpose=True).then_inc(S['fw'].h, 16))
            mk_down(ig, half, toff, S['hbld'].n)()
    DS['ost'].wait_all(eb, 'sync')
    eb.wait('sync', S['out'], S['out'].n)
    with nc.Block() as blk:
        eb.flush(blk)
    es_b5.close()
    es_w5.close()
    es_ps.close()
    es_sh.close()
    return nc


def make_inputs(inputs):
    x = np.asarray(inputs["L0_attn_x"], np.float32)
    Wq = np.asarray(inputs["L0_attn_Wq"], np.float32)
    Wkv = np.asarray(inputs["L0_attn_Wkv_down"], np.float32)
    Wku = np.asarray(inputs["L0_attn_Wk_up"], np.float32)
    Wvu = np.asarray(inputs["L0_attn_Wv_up"], np.float32)
    Wo = np.asarray(inputs["L0_attn_Wo"], np.float32)
    Wg = np.ascontiguousarray(np.asarray(inputs["L0_ffn_w_gate"], np.float32))
    Wu = np.ascontiguousarray(np.asarray(inputs["L0_ffn_w_up"], np.float32))
    Wd = np.ascontiguousarray(np.asarray(inputs["L0_ffn_w_down"], np.float32))
    shared = dict(
        WqT=np.ascontiguousarray(Wq.T), WkvT=np.ascontiguousarray(Wkv.T),
        WkuT=np.ascontiguousarray(Wku.T), WvuT=np.ascontiguousarray(Wvu.T),
        WoT=np.ascontiguousarray(Wo.T), Wg=Wg, Wu=Wu, Wd=Wd,
        identD=np.eye(128, dtype=np.float32),
    )
    in_maps = []
    for c in range(8):
        b, hf = c // 2, c % 2
        own = x[b, hf * TC:(hf + 1) * TC]
        oth = x[b, (1 - hf) * TC:(2 - hf) * TC]
        m = dict(shared)
        m["x_tok"] = np.ascontiguousarray(np.concatenate([own, oth], 0))
        m["xT_own"] = np.ascontiguousarray(own.T)
        in_maps.append(m)
    return in_maps


def kernel(**inputs):
    in_maps = make_inputs(inputs)
    nc = build_program(debug=False)
    res = run_bass_kernel_spmd(nc, in_maps, core_ids=list(range(8)))
    out = np.empty((4, TB, D), np.float32)
    for c in range(8):
        b, hf = c // 2, c % 2
        out[b, hf * TC:(hf + 1) * TC] = np.asarray(res.results[c]["o_out"], np.float32).T
    return out


# version 19 (final)
# speedup vs baseline: 1.0445x; 1.0166x over previous
import numpy as np
import contextlib
import concourse.bass as bass
import concourse.mybir as mybir
from concourse.bass_utils import run_bass_kernel_spmd

F32 = mybir.dt.float32
BF16 = mybir.dt.bfloat16
I32 = mybir.dt.int32
AF = mybir.ActivationFunctionType
ALU = mybir.AluOpType
AX = mybir.AxisListType

D = 1024
TC = 2048
TB = 4096
NH, HD = 8, 256
DQ = NH * HD
LR = 64
IF = 2816
KC = D // 128
SCALE = 1.0 / 16.0


class Sem:
    def __init__(self, nc, es, name):
        self.h = es.enter_context(nc.semaphore(name))
        self.n = 0


class DmaStream:

    def __init__(self, nc, es, name, nslots=2):
        self.sems = [Sem(nc, es, f"{name}{i}") for i in range(nslots)]
        self.cnt = 0
        self.nslots = nslots

    def issue(self, eb, qeng, out, in_):
        s = self.sems[self.cnt % self.nslots]
        s.n += 16
        eb.em(qeng, lambda e, out=out, in_=in_, s=s: e.dma_start(out=out, in_=in_).then_inc(s.h, 16))
        self.cnt += 1

    def wait_for(self, eb, eng, i):
        s = self.sems[i % self.nslots]
        eb.wait(eng, s, 16 * (i // self.nslots + 1))

    def wait_all(self, eb, eng):
        for s in self.sems:
            eb.wait(eng, s, s.n)


class EB:
    ENGS = ("sync", "tensor", "vector", "scalar", "gpsimd")

    def __init__(self, nc):
        self.nc = nc
        self.q = {e: [] for e in self.ENGS}

    def em(self, eng, fn):
        self.q[eng].append(fn)

    def wait(self, eng, sem, v):
        if v > 0:
            self.em(eng, lambda e: e.wait_ge(sem.h, v))

    def flush(self, blk):
        for eng in self.ENGS:
            lst = self.q[eng]
            if not lst:
                continue
            dec = getattr(blk, eng)

            def make(lst):
                def _(e):
                    for f in lst:
                        f(e)
                return _
            dec(make(lst))
        self.q = {e: [] for e in self.ENGS}


def build_program(debug=False, stop_after=9):
    import os
    B1_NT = int(os.environ.get('B1_NT', '32'))
    B1_QUANT = int(os.environ.get('B1_QUANT', '1'))
    B1_TRANS = int(os.environ.get('B1_TRANS', '1'))
    B1_WQ = int(os.environ.get('B1_WQ', '1'))
    B1_SPILL = int(os.environ.get('B1_SPILL', '1'))
    nc = bass.Bass("TRN2", target_bir_lowering=False)
    x_tok = nc.dram_tensor("x_tok", [TB, D], F32, kind="ExternalInput").ap()
    xT_own = nc.dram_tensor("xT_own", [D, TC], F32, kind="ExternalInput").ap()
    WqT = nc.dram_tensor("WqT", [D, DQ], F32, kind="ExternalInput").ap()
    WkvT = nc.dram_tensor("WkvT", [D, LR], F32, kind="ExternalInput").ap()
    WkuT = nc.dram_tensor("WkuT", [LR, DQ], F32, kind="ExternalInput").ap()
    WvuT = nc.dram_tensor("WvuT", [LR, DQ], F32, kind="ExternalInput").ap()
    WoT = nc.dram_tensor("WoT", [DQ, D], F32, kind="ExternalInput").ap()
    Wg = nc.dram_tensor("Wg", [IF, D], F32, kind="ExternalInput").ap()
    Wu = nc.dram_tensor("Wu", [IF, D], F32, kind="ExternalInput").ap()
    Wd = nc.dram_tensor("Wd", [D, IF], F32, kind="ExternalInput").ap()
    identD = nc.dram_tensor("identD", [128, 128], F32, kind="ExternalInput").ap()
    o_out = nc.dram_tensor("o_out", [D, TC], F32, kind="ExternalOutput").ap()
    WgqN = nc.dram_tensor("WgqN", [IF, D], BF16, kind="Internal").ap()
    WuqN = nc.dram_tensor("WuqN", [IF, D], BF16, kind="Internal").ap()
    WdqN = nc.dram_tensor("WdqN", [D, IF], BF16, kind="Internal").ap()
    xqT_scr = nc.dram_tensor("xqT_scr", [D, TC], BF16, kind="Internal").ap()
    resid_scr = nc.dram_tensor("resid_scr", [D, TC], F32, kind="Internal").ap()
    dbg = {}
    if debug:
        dbg['xnT'] = nc.dram_tensor("d_xnT", [128, KC, TB], BF16, kind="ExternalOutput").ap()
        dbg['kvT'] = nc.dram_tensor("d_kvT", [LR, TB], BF16, kind="ExternalOutput").ap()
        dbg['qT'] = nc.dram_tensor("d_qT", [128, 16, TC], BF16, kind="ExternalOutput").ap()
        dbg['kT0'] = nc.dram_tensor("d_kT0", [128, 2, TB], BF16, kind="ExternalOutput").ap()
        dbg['oall'] = nc.dram_tensor("d_oall", [128, 16, TC], BF16, kind="ExternalOutput").ap()
        dbg['sums'] = nc.dram_tensor("d_sums", [NH, TC], F32, kind="ExternalOutput").ap()
        dbg['resid'] = nc.dram_tensor("d_resid", [D, TC], F32, kind="ExternalOutput").ap()
        dbg['xq'] = nc.dram_tensor("d_xq", [D, TC], BF16, kind="ExternalOutput").ap()
        dbg['hq0'] = nc.dram_tensor("d_hq0", [128, 11, 1024], BF16, kind="ExternalOutput").ap()
        dbg['sml'] = nc.dram_tensor("d_sml", [128, 16], F32, kind="ExternalOutput").ap()
        dbg['bnd'] = nc.dram_tensor("d_bnd", [128, 12], F32, kind="ExternalOutput").ap()

    es_sh = contextlib.ExitStack()

    def sb(name, shape, dt, es):
        return es.enter_context(nc.sbuf_tensor(name, shape, dt))

    def ps_(shape, dt, es, name):
        return es.enter_context(nc.psum_tensor(name, shape, dt))

    S = {}
    for s in ["x_in", "lnv", "lna", "lnd", "lnap", "xq", "tr", "trev",
              "xqs", "win", "wqd", "wqs", "wql", "kvev", "qpj", "wqt", "sco", "exp",
              "avq", "oev", "bnd", "b12", "b23", "b34", "b45", "wo16",
              "xt16", "rs", "rr", "out", "fw", "xqr", "kv_pe", "q_pe", "prep", "pbev",
              "ffh", "sil", "hbld", "hq", "fftr", "fftrev", "ffdn", "ssq", "srw", "wkl", "lad", "aab", "fw2"]:
        S[s] = Sem(nc, es_sh, "s_" + s)

    DS = {}
    for nm, nsl in [("x_in", 2), ("win", 2), ("wqs", 2), ("wql", 2), ("wkl", 4),
                    ("xt", 2), ("rr", 2), ("rsd", 2), ("ost", 2), ("ssq", 1)]:
        DS[nm] = DmaStream(nc, es_sh, "d_" + nm, nsl)

    pbm = sb("pbm", [128, 96], F32, es_sh)
    pscale = sb("pscale", [128, 96], F32, es_sh)
    prcp = sb("prcp", [128, 96], F32, es_sh)
    ident_t = sb("ident_t", [128, 128], F32, es_sh)
    ones_f1 = sb("ones_f1", [1, 128], F32, es_sh)
    kvT_b = sb("kvT_b", [LR, TB], BF16, es_sh)
    sums8 = sb("sums8", [NH, TC], F32, es_sh)
    sumc = sb("sumc", [128, 2], F32, es_sh)
    sqc = sb("sqc", [128, 2], F32, es_sh)
    mc = sb("mc", [128, 2], F32, es_sh)
    msqc = sb("msqc", [128, 2], F32, es_sh)
    varc = sb("varc", [128, 2], F32, es_sh)
    sdc = sb("sdc", [128, 2], F32, es_sh)
    istdc = sb("istdc", [128, 2], F32, es_sh)
    negmsc = sb("negmsc", [128, 2], F32, es_sh)
    bt1 = sb("bt1", [1, 8], F32, es_sh)
    bq2 = sb("bq2", [1, 8], F32, es_sh)
    ones_b128 = sb("ones_b128", [128, 1], BF16, es_sh)
    eps_col = sb("eps_col", [128, 1], F32, es_sh)
    negB_col = sb("negB_col", [128, 1], F32, es_sh)

    es_wq = contextlib.ExitStack()
    wbuf = sb("wbuf", [128, 2, 512], F32, es_wq)
    wqout = sb("wqout", [128, 2, 512], BF16, es_wq)
    srowq = sb("srowq", [1, 1, 512], F32, es_wq)

    es_ps = contextlib.ExitStack()
    psA = ps_([128, 512], F32, es_ps, "psA")
    psB = ps_([128, 512], F32, es_ps, "psB")
    psC0 = ps_([128, 512], F32, es_ps, "psC0")
    psC1 = ps_([128, 512], F32, es_ps, "psC1")
    psD0 = ps_([128, 512], F32, es_ps, "psD0")
    psD1 = ps_([128, 512], F32, es_ps, "psD1")
    psE = ps_([128, 512], F32, es_ps, "psE")
    psF = ps_([128, 512], F32, es_ps, "psF")
    SBK = [psA, psB]
    OAB = [(psC0, psC1, psE), (psD0, psD1, psF)]

    def emit_quant(eb, F, src, dst, PL, act_abs=False):
        NB = F // 16
        pA, pC, pD, pK = PL['pA'], PL['pC'], PL['pD'], PL['pK']
        v = 'vector'
        L = S['lad']
        if act_abs:
            AB = S['aab']
            AB.n += 1
            eb.em('scalar', lambda e: e.activation(out=pA[:, 0:F], in_=src, func=AF.Abs).then_inc(AB.h, 1))
            a_abs = AB.n
            AB.n += 1
            eb.em('scalar', lambda e: e.activation(out=pK[:, 0:F], in_=src, func=AF.Sign).then_inc(AB.h, 1))
            a_sg = AB.n

        def op(fn, inc=True):
            if inc:
                L.n += 1
                eb.em(v, lambda e, fn=fn: fn(e).then_inc(L.h, 1))
            else:
                eb.em(v, fn)
            return L.n

        def wt(tgt):
            eb.wait(v, L, tgt)

        src3 = src.rearrange("p (b w) -> p b w", w=16)
        A3 = pA[:, 0:F].rearrange("p (b w) -> p b w", w=16)
        K3 = pK[:, 0:F].rearrange("p (b w) -> p b w", w=16)
        sb_ = pscale[:, 0:NB].unsqueeze(2).broadcast_to([128, NB, 16])
        rb = prcp[:, 0:NB].unsqueeze(2).broadcast_to([128, NB, 16])
        t_bm = op(lambda e: e.tensor_reduce(out=pbm[:, 0:NB], in_=src3, axis=AX.X, op=ALU.max, apply_absolute_value=True))
        if not act_abs:
            t_abs = op(lambda e: e.scalar_tensor_tensor(out=pA[:, 0:F], in0=src, scalar=-1.0, in1=src, op0=ALU.mult, op1=ALU.max))
            t_sg = op(lambda e: e.tensor_scalar(out=pK[:, 0:F], in0=src, scalar1=0.0, scalar2=2.0, op0=ALU.is_ge, op1=ALU.mult))
        wt(t_bm)
        t_sc = op(lambda e: e.tensor_scalar(out=pscale[:, 0:NB], in0=pbm[:, 0:NB], scalar1=6e-12, scalar2=1.0 / 6.0, op0=ALU.max, op1=ALU.mult))
        if not act_abs:
            wt(t_sg)
            t_sg1 = op(lambda e: e.tensor_single_scalar(out=pK[:, 0:F], in_=pK[:, 0:F], scalar=1.0, op=ALU.subtract))
        wt(t_sc)
        t_rcp = op(lambda e: e.reciprocal(prcp[:, 0:NB], pscale[:, 0:NB]))
        wt(t_rcp)
        if act_abs:
            eb.wait(v, S['aab'], a_abs)
        else:
            wt(t_abs)
        t_A = op(lambda e: e.tensor_tensor(out=A3, in0=A3, in1=rb, op=ALU.mult))
        wt(t_A)
        tc = op(lambda e: e.tensor_single_scalar(out=pC[:, 0:F], in_=pA[:, 0:F], scalar=0.25, op=ALU.is_gt))
        td = op(lambda e: e.tensor_scalar(out=pD[:, 0:F], in0=pA[:, 0:F], scalar1=5.0, scalar2=2.0, op0=ALU.is_gt, op1=ALU.mult))
        wt(tc)
        tc = op(lambda e: e.scalar_tensor_tensor(out=pC[:, 0:F], in0=pA[:, 0:F], scalar=0.75, in1=pC[:, 0:F], op0=ALU.is_gt, op1=ALU.add))
        wt(td)
        td = op(lambda e: e.scalar_tensor_tensor(out=pD[:, 0:F], in0=pA[:, 0:F], scalar=3.5, in1=pD[:, 0:F], op0=ALU.is_gt, op1=ALU.add))
        wt(tc)
        tc = op(lambda e: e.scalar_tensor_tensor(out=pC[:, 0:F], in0=pA[:, 0:F], scalar=1.25, in1=pC[:, 0:F], op0=ALU.is_gt, op1=ALU.add))
        wt(td)
        td = op(lambda e: e.scalar_tensor_tensor(out=pD[:, 0:F], in0=pA[:, 0:F], scalar=2.5, in1=pD[:, 0:F], op0=ALU.is_gt, op1=ALU.add))
        wt(tc)
        tc = op(lambda e: e.scalar_tensor_tensor(out=pC[:, 0:F], in0=pA[:, 0:F], scalar=1.75, in1=pC[:, 0:F], op0=ALU.is_gt, op1=ALU.add))
        wt(tc)
        wt(td)
        t_q = op(lambda e: e.scalar_tensor_tensor(out=pC[:, 0:F], in0=pC[:, 0:F], scalar=0.5, in1=pD[:, 0:F], op0=ALU.mult, op1=ALU.add))
        wt(t_q)
        if act_abs:
            eb.wait(v, S['aab'], a_sg)
        else:
            wt(t_sg1)
        t_f1 = op(lambda e: e.tensor_tensor(out=pC[:, 0:F], in0=pC[:, 0:F], in1=pK[:, 0:F], op=ALU.mult))
        wt(t_f1)
        C3 = pC[:, 0:F].rearrange("p (b w) -> p b w", w=16)
        dst3 = dst.rearrange("p (b w) -> p b w", w=16)
        return lambda e: e.tensor_tensor(out=dst3, in0=C3, in1=sb_, op=ALU.mult)

    wq_units = []
    for W_, WN_ in ((Wg, WgqN), (Wu, WuqN)):
        for i in range(22):
            for j in range(2):
                wq_units.append((W_[i * 128:(i + 1) * 128, j * 512:(j + 1) * 512],
                                 WN_[i * 128:(i + 1) * 128, j * 512:(j + 1) * 512], 512))
    for i in range(8):
        for j in range(6):
            c0 = j * 512
            cl = 512 if j < 5 else 256
            wq_units.append((Wd[i * 128:(i + 1) * 128, c0:c0 + cl],
                             WdqN[i * 128:(i + 1) * 128, c0:c0 + cl], cl))
    NW_UNITS = len(wq_units)

    def emit_wq_units(eb, us, PL):
        for u in us:
            if u >= NW_UNITS:
                continue
            src, dst, F = wq_units[u]
            if u >= 2:
                eb.wait('sync', S['wqd'], u - 1)
            DS['win'].issue(eb, 'sync', wbuf[:, u % 2, 0:F], src)
            DS['win'].wait_for(eb, 'vector', u)
            if u >= 2:
                DS['wqs'].wait_for(eb, 'vector', u - 2)
            last = emit_quant(eb, F, wbuf[:, u % 2, 0:F], wqout[:, u % 2, 0:F], PL)
            S['wqd'].n += 1
            eb.em('vector', lambda e, last=last: last(e).then_inc(S['wqd'].h, 1))
            eb.wait('gpsimd', S['wqd'], S['wqd'].n)
            DS['wqs'].issue(eb, 'gpsimd', dst, wqout[:, u % 2, 0:F])

    def barrier(eb, name):
        eb.wait('sync', S['out'], S['out'].n)
        for eng in EB.ENGS:
            S[name].n += 1
            if eng == 'sync':
                eb.em(eng, lambda e: e.nop().then_inc(S[name].h, 1))
            else:
                eb.em(eng, lambda e: e.drain().then_inc(S[name].h, 1))

    def barrier_wait(eb, name):
        for eng in EB.ENGS:
            eb.wait(eng, S[name], S[name].n)

    es_b12 = contextlib.ExitStack()
    xnT = sb("xnT", [128, KC, TB], BF16, es_b12)
    es_b1 = contextlib.ExitStack()
    PL1 = {k: sb("PL1" + k, [128, 1024], F32, es_b1) for k in ('pA', 'pC', 'pD', 'pK')}
    xbuf = sb("xbuf", [128, 2, 2, 1024], F32, es_b1)
    xn_scr = sb("xn_scr", [128, 2, 1024], F32, es_b1)
    xq_f = sb("xq_f", [128, 2, 1024], F32, es_b1)
    stage_q = sb("stage_q", [128, KC, 512], BF16, es_b1)

    eb = EB(nc)
    eb.em('scalar', lambda e: e.preload_act_table(AF.Sqrt))
    S['xt16'].n += 16
    eb.em('sync', lambda e: e.dma_start(out=ident_t[:], in_=identD).then_inc(S['xt16'].h, 16))
    S['prep'].n += 4
    eb.em('gpsimd', lambda e: e.memset(bt1[:], 0.0).then_inc(S['prep'].h, 1))
    eb.em('gpsimd', lambda e: e.memset(ones_f1[:], 1.0).then_inc(S['prep'].h, 1))
    eb.em('gpsimd', lambda e: e.memset(eps_col[:], 1e-5).then_inc(S['prep'].h, 1))
    eb.em('gpsimd', lambda e: e.memset(ones_b128[:], 1.0).then_inc(S['prep'].h, 1))

    eb.wait('scalar', S['prep'], S['prep'].n)
    NCH_X = (B1_NT + 1) // 2

    def issue_x_chunk(ch):
        if ch >= 2:
            eb.wait('sync', S['lnap'], 2 * (ch - 1))
        DS['x_in'].issue(eb, 'sync', xbuf[:, ch % 2, :, :],
                         x_tok.rearrange("(c t p) d -> p c t d", p=128, t=2)[:, ch, :, :])

    for ch in range(min(2, NCH_X)):
        issue_x_chunk(ch)

    tr_cnt = 0
    eb.wait('tensor', S['xt16'], 16)

    def emit_tile_transposes(eb, src_tile, dst_fn):
        nonlocal tr_cnt
        for b_ in range(2):
            pbank = SBK[tr_cnt % 2]
            if tr_cnt >= 2:
                eb.wait('tensor', S['trev'], tr_cnt - 1)
            for j in range(4):
                c = b_ * 4 + j
                ins = (lambda e, c=c, j=j, pbank=pbank, src_tile=src_tile: e.transpose(
                    pbank[:, j * 128:(j + 1) * 128], src_tile[:, c * 128:(c + 1) * 128], ident_t[:]))
                if j == 3:
                    S['tr'].n += 1
                    eb.em('tensor', lambda e, ins=ins: ins(e).then_inc(S['tr'].h, 1))
                else:
                    eb.em('tensor', ins)
            eb.wait('scalar', S['tr'], S['tr'].n)
            dst = dst_fn(b_)
            src = pbank[:].rearrange("p (c n) -> p c n", c=4)
            S['trev'].n += 1
            eb.em('scalar', lambda e, dst=dst, src=src: e.copy(dst, src).then_inc(S['trev'].h, 1))
            tr_cnt += 1

    for t in range(B1_NT):
        ch = t // 2
        if t % 2 == 0 and 2 <= ch + 1 < NCH_X:
            issue_x_chunk(ch + 1)
        own = t < 16
        xt = xbuf[:, ch % 2, t % 2, :]
        c2 = slice(t % 2, t % 2 + 1)
        DS['x_in'].wait_for(eb, 'vector', ch)
        eb.em('vector', lambda e, xt=xt, c2=c2: e.tensor_reduce(out=sumc[:, c2], in_=xt, axis=AX.X, op=ALU.add))
        eb.em('vector', lambda e, xt=xt, c2=c2: e.scalar_tensor_tensor(
            out=PL1['pA'][:, 0:1024], in0=xt, scalar=1.0, in1=xt, op0=ALU.mult, op1=ALU.mult, accum_out=sqc[:, c2]))
        eb.em('vector', lambda e: e.drain())
        eb.em('vector', lambda e, c2=c2: e.tensor_single_scalar(out=mc[:, c2], in_=sumc[:, c2], scalar=1.0 / D, op=ALU.mult))
        eb.em('vector', lambda e: e.drain())
        eb.em('vector', lambda e, c2=c2: e.tensor_single_scalar(out=msqc[:, c2], in_=mc[:, c2], scalar=mc[:, c2], op=ALU.mult))
        eb.em('vector', lambda e, c2=c2: e.tensor_single_scalar(out=sqc[:, c2], in_=sqc[:, c2], scalar=1.0 / D, op=ALU.mult))
        eb.em('vector', lambda e: e.drain())
        eb.em('vector', lambda e, c2=c2: e.tensor_single_scalar(out=varc[:, c2], in_=sqc[:, c2], scalar=msqc[:, c2], op=ALU.subtract))
        S['lnv'].n += 1
        eb.em('vector', lambda e: e.drain().then_inc(S['lnv'].h, 1))
        eb.wait('scalar', S['lnv'], S['lnv'].n)
        S['lna'].n += 1
        eb.em('scalar', lambda e, c2=c2: e.activation(out=sdc[:, c2], in_=varc[:, c2], func=AF.Sqrt, bias=eps_col[:, 0:1], scale=1.0).then_inc(S['lna'].h, 1))
        eb.wait('vector', S['lna'], S['lna'].n)
        eb.em('vector', lambda e, c2=c2: e.reciprocal(istdc[:, c2], sdc[:, c2]))
        eb.em('vector', lambda e: e.drain())
        if t >= 2:
            eb.wait('vector', S['tr'], tr_cnt)
        S['lnap'].n += 1
        eb.em('vector', lambda e, xt=xt, c2=c2, t=t: e.tensor_scalar(
            out=xn_scr[:, t % 2, :], in0=xt, scalar1=mc[:, c2], scalar2=istdc[:, c2],
            op0=ALU.subtract, op1=ALU.mult).then_inc(S['lnap'].h, 1))
        eb.wait('tensor', S['lnap'], S['lnap'].n)
        if B1_TRANS:
            emit_tile_transposes(eb, xn_scr[:, t % 2, :],
                                 lambda b_, t=t: xnT[:, b_ * 4:(b_ + 1) * 4, t * 128:(t + 1) * 128])
        if own and B1_QUANT:
            eb.wait('vector', S['lnap'], S['lnap'].n)
            last = emit_quant(eb, 1024, xn_scr[:, t % 2, :], xq_f[:, t % 2, :], PL1)
            S['xq'].n += 1
            eb.em('vector', lambda e, last=last: last(e).then_inc(S['xq'].h, 1))
            eb.wait('tensor', S['xq'], S['xq'].n)
            g = t // 4
            if B1_TRANS:
                emit_tile_transposes(eb, xq_f[:, t % 2, :],
                                     lambda b_, t=t: stage_q[:, b_ * 4:(b_ + 1) * 4, (t % 4) * 128:(t % 4 + 1) * 128])
            if B1_TRANS and B1_SPILL and t % 4 == 3:
                eb.wait('sync', S['trev'], S['trev'].n)
                S['xqs'].n += 16
                eb.em('sync', lambda e, g=g: e.dma_start(
                    out=xqT_scr.rearrange("(c p) q -> p c q", p=128)[:, :, g * 512:(g + 1) * 512],
                    in_=stage_q[:]).then_inc(S['xqs'].h, 16))
                if g < 3:
                    eb.wait('vector', S['xqs'], 16 * (g + 1))
    if debug:
        eb.wait('sync', S['trev'], S['trev'].n)
        eb.wait('sync', S['lnap'], S['lnap'].n)
        S['out'].n += 16
        eb.em('sync', lambda e: e.dma_start(out=dbg['xnT'], in_=xnT[:]).then_inc(S['out'].h, 16))
        for i_, tns in enumerate([sumc, sqc, mc, msqc, varc, sdc, istdc, istdc]):
            S['out'].n += 16
            eb.em('sync', lambda e, i_=i_, tns=tns: e.dma_start(out=dbg['sml'][:, 2*i_:2*i_+2], in_=tns[:]).then_inc(S['out'].h, 16))
        S['out'].n += 16
        eb.em('sync', lambda e: e.dma_start(out=dbg['resid'][0:128, 0:2048], in_=xn_scr[:].rearrange("p c d -> p (c d)")).then_inc(S['out'].h, 16))


    if B1_WQ:
        pass
    eb.wait('sync', S['xqs'], S['xqs'].n)
    barrier(eb, 'b12')
    with nc.Block() as blk:
        eb.flush(blk)

    if stop_after <= 1:
        eb2 = EB(nc)
        eb2.wait('sync', S['out'], S['out'].n)
        with nc.Block() as blk:
            eb2.flush(blk)
        return nc
    es_b1.close()

    es_b2 = contextlib.ExitStack()
    PL2 = {k: sb("PL2" + k, [128, 512], F32, es_b2) for k in ('pA', 'pC', 'pD', 'pK')}
    wkv_b = sb("wkv_b", [128, KC, LR], BF16, es_b2)
    sqb2 = sb("sqb2", [128, 2, 512], BF16, es_b2)
    wqchunk = sb("wqchunk", [128, 2, KC, 512], BF16, es_b2)
    es_b23 = contextlib.ExitStack()
    qT_all = es_b23.enter_context(nc.sbuf_tensor("qT_all", [128, 16, TC], BF16, side="right"))

    eb = EB(nc)
    barrier_wait(eb, 'b12')
    eb.em('scalar', lambda e: e.preload_act_table(AF.Exp))
    S['wqt'].n += 16
    eb.em('gpsimd', lambda e: e.dma_start(out=wkv_b[:], in_=WkvT.rearrange("(c p) l -> p c l", p=128)).then_inc(S['wqt'].h, 16))
    eb.wait('tensor', S['wqt'], S['wqt'].n)
    for n in range(TB // 512):
        pb = SBK[n % 2]
        if n >= 2:
            eb.wait('tensor', S['kvev'], n - 1)
        for kc in range(KC):
            ins = (lambda e, n=n, kc=kc, pb=pb: e.matmul(pb[0:LR, 0:512], wkv_b[:, kc, :], xnT[:, kc, n * 512:(n + 1) * 512],
                                                         start=(kc == 0), stop=(kc == KC - 1)))
            if kc == KC - 1:
                S['kv_pe'].n += 1
                eb.em('tensor', lambda e, ins=ins: ins(e).then_inc(S['kv_pe'].h, 1))
            else:
                eb.em('tensor', ins)
        eb.wait('scalar', S['kv_pe'], S['kv_pe'].n)
        S['kvev'].n += 1
        eb.em('scalar', lambda e, n=n, pb=pb: e.copy(kvT_b[:, n * 512:(n + 1) * 512], pb[0:LR, 0:512]).then_inc(S['kvev'].h, 1))
    if debug:
        eb.wait('sync', S['kvev'], S['kvev'].n)
        S['out'].n += 16
        eb.em('sync', lambda e: e.dma_start(out=dbg['kvT'], in_=kvT_b[:]).then_inc(S['out'].h, 16))

    for j in range(4):
        if j >= 2:
            eb.wait('gpsimd', S['q_pe'], 16 * (j - 1))
        DS['wql'].issue(eb, 'gpsimd', wqchunk[:, j % 2, :, :],
                        WqT.rearrange("(c p) m -> p c m", p=128)[:, :, j * 512:(j + 1) * 512])
    eb.wait('tensor', S['kvev'], S['kvev'].n)
    qpj_base = S['qpj'].n
    for j in range(4):
        DS['wql'].wait_for(eb, 'tensor', j)
        for mrel in range(4):
            m = j * 4 + mrel
            for n in range(TC // 512):
                idx = m * 4 + n
                pb = SBK[idx % 2]
                if idx >= 2:
                    eb.wait('tensor', S['qpj'], qpj_base + idx - 1)
                for kc in range(KC):
                    ins = (lambda e, j=j, mrel=mrel, n=n, kc=kc, pb=pb: e.matmul(
                        pb[:, :], wqchunk[:, j % 2, kc, mrel * 128:(mrel + 1) * 128],
                        xnT[:, kc, n * 512:(n + 1) * 512], start=(kc == 0), stop=(kc == KC - 1)))
                    if kc == KC - 1:
                        S['q_pe'].n += 1
                        eb.em('tensor', lambda e, ins=ins: ins(e).then_inc(S['q_pe'].h, 1))
                    else:
                        eb.em('tensor', ins)
                eb.wait('scalar', S['q_pe'], S['q_pe'].n)
                S['qpj'].n += 1
                eb.em('scalar', lambda e, m=m, n=n, pb=pb: e.copy(
                    qT_all[:, m, n * 512:(n + 1) * 512], pb[:, :]).then_inc(S['qpj'].h, 1))
    if debug:
        eb.wait('sync', S['qpj'], S['qpj'].n)
        S['out'].n += 16
        eb.em('sync', lambda e: e.dma_start(out=dbg['qT'], in_=qT_all[:]).then_inc(S['out'].h, 16))

    eb.wait('vector', S['kvev'], S['kvev'].n)
    eb.wait('vector', S['qpj'], S['qpj'].n)
    eb.wait('tensor', S['prep'], S['prep'].n)
    PSR = [psE, psF]
    for sl in range(8):
        S['bnd'].n += 1
        eb.em('vector', lambda e, sl=sl: e.tensor_tensor(
            out=sqb2[0:LR, 0, :], in0=kvT_b[:, sl * 512:(sl + 1) * 512],
            in1=kvT_b[:, sl * 512:(sl + 1) * 512], op=ALU.mult))
        eb.em('vector', lambda e: e.drain().then_inc(S['bnd'].h, 1))
        eb.wait('tensor', S['bnd'], S['bnd'].n)
        if sl >= 2:
            eb.wait('tensor', S['pbev'], S['pbev'].n - 1)
        pse = PSR[sl % 2]
        S['bnd'].n += 1
        eb.em('tensor', lambda e, pse=pse: e.matmul(pse[0:1, 0:512], ones_b128[0:LR, 0:1], sqb2[0:LR, 0, :],
                                                    start=True, stop=True).then_inc(S['bnd'].h, 1))
        eb.wait('vector', S['bnd'], S['bnd'].n)
        eb.em('vector', lambda e, pse=pse: e.tensor_reduce(out=bt1[:, 0:1], in_=pse[0:1, 0:512], axis=AX.X, op=ALU.max))
        eb.em('vector', lambda e: e.drain())
        S['pbev'].n += 1
        if sl == 0:
            eb.em('vector', lambda e: e.tensor_single_scalar(out=bt1[:, 7:8], in_=bt1[:, 0:1], scalar=1.0, op=ALU.mult))
        else:
            eb.em('vector', lambda e: e.tensor_single_scalar(out=bt1[:, 7:8], in_=bt1[:, 0:1], scalar=bt1[:, 7:8], op=ALU.max))
        eb.em('vector', lambda e: e.drain().then_inc(S['pbev'].h, 1))
    for h2 in range(NH):
        for n in range(4):
            for m in range(2):
                qsl = qT_all[:, 2 * h2 + m, n * 512:(n + 1) * 512]
                if m == 1:
                    S['bnd'].n += 1
                    eb.em('vector', lambda e, qsl=qsl, m=m: e.tensor_tensor(out=sqb2[:, m, :], in0=qsl, in1=qsl, op=ALU.mult))
                    eb.em('vector', lambda e: e.drain().then_inc(S['bnd'].h, 1))
                else:
                    eb.em('vector', lambda e, qsl=qsl, m=m: e.tensor_tensor(out=sqb2[:, m, :], in0=qsl, in1=qsl, op=ALU.mult))
            eb.wait('tensor', S['bnd'], S['bnd'].n)
            eb.wait('tensor', S['pbev'], S['pbev'].n - 1)
            pse = PSR[n % 2]
            for m in range(2):
                ins = (lambda e, pse=pse, m=m: e.matmul(pse[0:1, 0:512], ones_b128[:, 0:1], sqb2[:, m, :],
                                                        start=(m == 0), stop=(m == 1)))
                if m == 1:
                    S['bnd'].n += 1
                    eb.em('tensor', lambda e, ins=ins: ins(e).then_inc(S['bnd'].h, 1))
                else:
                    eb.em('tensor', ins)
            eb.wait('vector', S['bnd'], S['bnd'].n)
            eb.em('vector', lambda e, pse=pse: e.tensor_reduce(out=bt1[:, 0:1], in_=pse[0:1, 0:512], axis=AX.X, op=ALU.max))
            eb.em('vector', lambda e: e.drain())
            S['pbev'].n += 1
            if n == 0:
                eb.em('vector', lambda e, h2=h2: e.tensor_single_scalar(out=bq2[:, h2:h2 + 1], in_=bt1[:, 0:1], scalar=1.0, op=ALU.mult))
            else:
                eb.em('vector', lambda e, h2=h2: e.tensor_single_scalar(out=bq2[:, h2:h2 + 1], in_=bt1[:, 0:1], scalar=bq2[:, h2:h2 + 1], op=ALU.max))
            eb.em('vector', lambda e: e.drain().then_inc(S['pbev'].h, 1))
    barrier(eb, 'b23')
    with nc.Block() as blk:
        eb.flush(blk)

    if stop_after <= 2:
        eb2 = EB(nc)
        eb2.wait('sync', S['out'], S['out'].n)
        with nc.Block() as blk:
            eb2.flush(blk)
        return nc
    es_b2.close()
    es_b12.close()

    es_b34 = contextlib.ExitStack()
    o_all = sb("o_all", [128, 16, TC], BF16, es_b34)
    es_b3 = contextlib.ExitStack()
    PL3 = {k: sb("PL3" + k, [128, 512], F32, es_b3) for k in ('pA', 'pC', 'pD', 'pK')}
    sqv_t = sb("sqv_t", [128, 1024], BF16, es_b3)
    wkuh = sb("wkuh", [LR, 2, 256], BF16, es_b3)
    wvuh = sb("wvuh", [LR, 2, 256], BF16, es_b3)
    kT_h = sb("kT_h", [128, 2, TB], BF16, es_b3)
    V_h = sb("V_h", [128, 32, 258], BF16, es_b3)
    pbuf = sb("pbuf", [128, 2, 512], BF16, es_b3)

    eb = EB(nc)
    barrier_wait(eb, 'b23')
    S['prep'].n += 1
    eb.em('gpsimd', lambda e: e.memset(V_h[:, :, 256:257], 1.0).then_inc(S['prep'].h, 1))
    prep_end = [None] * NH

    def emit_head_wload(h):
        if h >= 2 and prep_end[h - 2] is not None:
            eb.wait('gpsimd', S['prep'], prep_end[h - 2])
        DS['wkl'].issue(eb, 'gpsimd', wkuh[:, h % 2, :], WkuT[:, h * 256:(h + 1) * 256])
        DS['wkl'].issue(eb, 'gpsimd', wvuh[:, h % 2, :], WvuT[:, h * 256:(h + 1) * 256])

    emit_head_wload(0)
    emit_head_wload(1)

    wq_next = 0
    qglob = 0
    cglob = 0
    kt0_dump_tgt = [0]
    for h in range(NH):
        if debug and h == 1:
            eb.wait('vector', S['out'], kt0_dump_tgt[0])
        DS['wkl'].wait_for(eb, 'tensor', 2 * h)
        DS['wkl'].wait_for(eb, 'tensor', 2 * h + 1)
        eb.wait('tensor', S['prep'], S['prep'].n)
        if h > 0:
            eb.wait('tensor', S['exp'], cglob)
        kv_base_a = S['kvev'].n
        kv_base_d = S['qpj'].n
        if h > 0:
            eb.wait('tensor', S['oev'], 4 * h)
        PR4 = [psA, psB, psE, psF]

        def prep_done_wait(eng, upto):
            eb.wait(eng, S['kvev'], kv_base_a + (upto + 2) // 2)
            eb.wait(eng, S['qpj'], kv_base_d + (upto + 1) // 2)

        def emit_prep(rng):
            for idx in rng:
                pb = PR4[idx % 4]
                if idx >= 4:
                    pidx = idx - 4
                    if pidx % 2 == 0:
                        eb.wait('tensor', S['kvev'], kv_base_a + pidx // 2 + 1)
                    else:
                        eb.wait('tensor', S['qpj'], kv_base_d + pidx // 2 + 1)
                if idx < 16:
                    m, n = idx // 8, idx % 8
                    ins2 = [(lambda e, h=h, m=m, n=n, pb=pb: e.matmul(
                        pb[:, :], wkuh[:, h % 2, m * 128:(m + 1) * 128],
                        kvT_b[:, n * 512:(n + 1) * 512], start=True, stop=True))]
                    eva = (lambda e, m=m, n=n, pb=pb: e.copy(
                        kT_h[:, m, n * 512:(n + 1) * 512], pb[:, :]))
                    evd = (lambda e, m=m, n=n, pb=pb: e.tensor_copy(
                        kT_h[:, m, n * 512:(n + 1) * 512], pb[:, :]))
                else:
                    mt = (idx - 16) * 2
                    ins2 = [(lambda e, h=h, mt=mt, pb=pb: e.matmul(
                        pb[0:128, 0:256], kvT_b[:, mt * 128:(mt + 1) * 128],
                        wvuh[:, h % 2, :], start=True, stop=True)),
                            (lambda e, h=h, mt=mt, pb=pb: e.matmul(
                        pb[0:128, 256:512], kvT_b[:, (mt + 1) * 128:(mt + 2) * 128],
                        wvuh[:, h % 2, :], start=True, stop=True))]
                    dst = V_h[:, mt:mt + 2, 0:256]
                    srcp = pb[0:128, 0:512].rearrange("p (c n) -> p c n", c=2)
                    eva = (lambda e, dst=dst, srcp=srcp: e.copy(dst, srcp))
                    evd = (lambda e, dst=dst, srcp=srcp: e.tensor_copy(dst, srcp))
                for k2, ins in enumerate(ins2):
                    S['prep'].n += 1
                    eb.em('tensor', lambda e, ins=ins: ins(e).then_inc(S['prep'].h, 1))
                eb.wait('scalar', S['prep'], S['prep'].n)
                if idx % 2 == 0:
                    S['kvev'].n += 1
                    eb.em('scalar', lambda e, eva=eva: eva(e).then_inc(S['kvev'].h, 1))
                else:
                    S['qpj'].n += 1
                    eb.em('scalar', lambda e, eva=eva: eva(e).then_inc(S['qpj'].h, 1))

        emit_prep(range(16))
        NB4 = [psC0, psC1, psD0, psD1]
        eb.wait('scalar', S['qpj'], kv_base_d + 8)
        mmp_tgt = [0] * 8
        for grp in range(8):
            emit_prep(range(16 + 2 * grp, 16 + 2 * grp + 2))
            if grp >= 1:
                eb.wait('scalar', S['bnd'], mmp_tgt[grp - 1])
            sq_base = S['sco'].n
            for m in range(2):
                S['sco'].n += 1
                eb.em('scalar', lambda e, grp=grp, m=m: e.activation(
                    out=sqv_t[:, m * 512:(m + 1) * 512], in_=kT_h[:, m, grp * 512:(grp + 1) * 512],
                    func=AF.Square).then_inc(S['sco'].h, 1))
            pse = NB4[grp % 4]
            if grp >= 4:
                eb.wait('tensor', S['pbev'], S['pbev'].n - 3)
            eb.wait('tensor', S['sco'], S['sco'].n)
            for m in range(2):
                ins = (lambda e, pse=pse, m=m: e.matmul(
                    pse[0:1, 0:512], ones_b128[:, 0:1], sqv_t[:, m * 512:(m + 1) * 512],
                    start=(m == 0), stop=(m == 1)))
                if m == 1:
                    S['bnd'].n += 1
                    eb.em('tensor', lambda e, ins=ins: ins(e).then_inc(S['bnd'].h, 1))
                else:
                    eb.em('tensor', ins)
            mmp_tgt[grp] = S['bnd'].n
            eb.wait('vector', S['bnd'], S['bnd'].n)
            eb.em('vector', lambda e, pse=pse: e.tensor_reduce(out=bt1[:, 0:1], in_=pse[0:1, 0:512], axis=AX.X, op=ALU.max))
            eb.em('vector', lambda e: e.drain())
            S['pbev'].n += 1
            if grp == 0:
                eb.em('vector', lambda e: e.tensor_single_scalar(out=bt1[:, 2:3], in_=bt1[:, 0:1], scalar=1.0, op=ALU.mult))
            else:
                eb.em('vector', lambda e: e.tensor_single_scalar(out=bt1[:, 2:3], in_=bt1[:, 0:1], scalar=bt1[:, 2:3], op=ALU.max))
            eb.em('vector', lambda e: e.drain().then_inc(S['pbev'].h, 1))
        prep_end[h] = S['prep'].n
        if h + 2 < NH:
            emit_head_wload(h + 2)
        if debug and h == 0:
            eb.wait('sync', S['kvev'], S['kvev'].n)
            S['out'].n += 16
            eb.em('sync', lambda e: e.dma_start(out=dbg['kT0'], in_=kT_h[:]).then_inc(S['out'].h, 16))

        eb.wait('vector', S['exp'], 128 * h)
        eb.em('vector', lambda e, h=h: e.tensor_single_scalar(out=bt1[:, 3:4], in_=bq2[:, h:h + 1], scalar=bt1[:, 2:3], op=ALU.add))
        eb.em('vector', lambda e: e.drain())
        S['bnd'].n += 1
        eb.em('vector', lambda e: e.tensor_single_scalar(out=bt1[:, 6:7], in_=bt1[:, 3:4], scalar=-1.02 / 32.0, op=ALU.mult))
        eb.em('vector', lambda e: e.drain().then_inc(S['bnd'].h, 1))
        eb.wait('tensor', S['bnd'], S['bnd'].n)
        eb.wait('tensor', S['pbev'], S['pbev'].n)
        S['bnd'].n += 1
        eb.em('tensor', lambda e: e.matmul(psE[0:128, 0:1], ones_f1[:, :], bt1[0:1, 6:7], start=True, stop=True).then_inc(S['bnd'].h, 1))
        eb.wait('vector', S['bnd'], S['bnd'].n)
        S['pbev'].n += 1
        eb.em('vector', lambda e: e.tensor_single_scalar(out=negB_col[:], in_=psE[0:128, 0:1], scalar=1.0, op=ALU.mult))
        eb.em('vector', lambda e: e.drain().then_inc(S['pbev'].h, 1))
        S['bnd'].n += 1
        eb.em('vector', lambda e: e.nop().then_inc(S['bnd'].h, 1))
        bnd_ready = S['bnd'].n
        if debug and h == 0:
            eb.wait('sync', S['bnd'], bnd_ready)
            S['out'].n += 32
            eb.em('sync', lambda e: e.dma_start(out=dbg['bnd'][0:1, 0:8], in_=bt1[:]).then_inc(S['out'].h, 16))
            eb.em('sync', lambda e: e.dma_start(out=dbg['bnd'][:, 8:9], in_=negB_col[:], allow_slow_non_contiguous=True).then_inc(S['out'].h, 16))
            eb.wait('scalar', S['out'], S['out'].n)
            kt0_dump_tgt[0] = S['out'].n

        eb.wait('tensor', S['kvev'], S['kvev'].n)
        eb.wait('tensor', S['qpj'], S['qpj'].n)
        pbev_ready = S['pbev'].n
        for qq in range(4):
            par = qq % 2
            o0, o1, osum = OAB[par]
            exp_tgt = [0] * 32

            def emit_scores(kc):
                nonlocal cglob
                pb = SBK[cglob % 2]
                if cglob >= 2:
                    eb.wait('tensor', S['exp'], cglob - 1)
                if h == 0 and qq == 0 and kc == 0:
                    eb.wait('tensor', S['pbev'], S['pbev'].n)
                for m in range(2):
                    ins = (lambda e, h=h, qq=qq, kc=kc, m=m, pb=pb: e.matmul(
                        pb[:, :], kT_h[:, m, kc * 128:(kc + 1) * 128],
                        qT_all[:, 2 * h + m, qq * 512:(qq + 1) * 512],
                        start=(m == 0), stop=(m == 1)))
                    if m == 1:
                        S['sco'].n += 1
                        eb.em('tensor', lambda e, ins=ins: ins(e).then_inc(S['sco'].h, 1))
                    else:
                        eb.em('tensor', ins)
                eb.wait('scalar', S['sco'], S['sco'].n)
                if qq == 0 and kc == 0:
                    eb.wait('scalar', S['bnd'], bnd_ready)
                S['exp'].n += 1
                exp_tgt[kc] = S['exp'].n
                eb.em('scalar', lambda e, cglob=cglob, pb=pb: e.activation(
                    out=pbuf[:, cglob % 2, :], in_=pb[:, :], func=AF.Exp,
                    bias=negB_col[:], scale=SCALE).then_inc(S['exp'].h, 1))
                cglob += 1

            def emit_av(kc, cg):
                eb.wait('tensor', S['exp'], exp_tgt[kc])
                if kc == 0 and qglob >= 2:
                    eb.wait('tensor', S['oev'], qglob - 1)
                if kc == 0 and qq == 0:
                    eb.wait('tensor', S['pbev'], pbev_ready)
                for mt2 in range(3):
                    lo = [0, 128, 256][mt2]
                    hi = [128, 256, 257][mt2]
                    dstp = [o0, o1, osum][mt2]
                    dsl = (slice(0, 128), slice(0, 128), slice(0, 1))[mt2]
                    ins = (lambda e, kc=kc, cg=cg, lo=lo, hi=hi, dstp=dstp, dsl=dsl: e.matmul(
                        dstp[dsl, 0:512], V_h[:, kc, lo:hi], pbuf[:, cg % 2, :],
                        start=(kc == 0), stop=(kc == 31)))
                    if mt2 == 2 and kc == 31:
                        S['avq'].n += 1
                        eb.em('tensor', lambda e, ins=ins: ins(e).then_inc(S['avq'].h, 1))
                    else:
                        eb.em('tensor', ins)

            cg0 = cglob
            for kc in range(32):
                emit_scores(kc)
                if kc >= 1:
                    emit_av(kc - 1, cg0 + kc - 1)
            emit_av(31, cg0 + 31)
            eb.wait('scalar', S['avq'], S['avq'].n)
            if qglob >= 1:
                DS['ssq'].wait_for(eb, 'scalar', qglob - 1)
            for m in range(2):
                op = [o0, o1][m]
                eb.em('scalar', lambda e, h=h, qq=qq, m=m, op=op: e.copy(
                    o_all[:, 2 * h + m, qq * 512:(qq + 1) * 512], op[:, :]))
            S['oev'].n += 1
            eb.em('scalar', lambda e, qglob=qglob, osum=osum: e.copy(
                srowq[:, 0, :], osum[0:1, 0:512]).then_inc(S['oev'].h, 1))
            eb.wait('gpsimd', S['oev'], S['oev'].n)
            DS['ssq'].issue(eb, 'gpsimd', sums8[h:h + 1, qq * 512:(qq + 1) * 512], srowq[:, 0, :])
            qglob += 1
            kq = [6, 6, 5, 0][(qglob - 1) % 4]
            emit_wq_units(eb, range(wq_next, wq_next + kq), PL3)
            wq_next = min(wq_next + kq, NW_UNITS)

    emit_wq_units(eb, range(wq_next, NW_UNITS), PL3)
    wq_next = NW_UNITS
    if debug:
        eb.wait('sync', S['oev'], S['oev'].n)
        S['out'].n += 16
        eb.em('sync', lambda e: e.dma_start(out=dbg['oall'], in_=o_all[:]).then_inc(S['out'].h, 16))

    barrier(eb, 'b34')
    with nc.Block() as blk:
        eb.flush(blk)

    if stop_after <= 3:
        eb2 = EB(nc)
        eb2.wait('sync', S['out'], S['out'].n)
        with nc.Block() as blk:
            eb2.flush(blk)
        return nc
    es_b3.close()
    es_b23.close()

    es_w5 = contextlib.ExitStack()
    wgq_b = es_w5.enter_context(nc.sbuf_tensor("wgq_b", [128, KC, 1408], BF16, side="right"))
    wuq_b = es_w5.enter_context(nc.sbuf_tensor("wuq_b", [128, KC, 1408], BF16, side="right"))
    es_b4 = contextlib.ExitStack()
    wo_b = sb("wo_b", [128, 16, D], BF16, es_b4)
    xt_buf = sb("xt_buf", [128, 2, TC], F32, es_b4)
    rstage = sb("rstage", [128, 2, TC], F32, es_b4)
    srow = sb("srow", [1, TC], F32, es_b4)

    eb = EB(nc)
    barrier_wait(eb, 'b34')
    S['wo16'].n += 16
    eb.em('gpsimd', lambda e: e.dma_start(out=wo_b[:], in_=WoT.rearrange("(c p) d -> p c d", p=128)).then_inc(S['wo16'].h, 16))
    DS['wqs'].wait_all(eb, 'scalar')
    for kc2 in range(KC):
        S['fw'].n += 16
        eb.em('scalar', lambda e, kc2=kc2: e.dma_start(
            out=wgq_b[:, kc2, :], in_=WgqN[0:1408, kc2 * 128:(kc2 + 1) * 128], transpose=True).then_inc(S['fw'].h, 16))
    for kc2 in range(KC):
        S['fw'].n += 16
        eb.em('scalar', lambda e, kc2=kc2: e.dma_start(
            out=wuq_b[:, kc2, :], in_=WuqN[0:1408, kc2 * 128:(kc2 + 1) * 128], transpose=True).then_inc(S['fw'].h, 16))

    DS['ssq'].wait_all(eb, 'vector')
    rec_done = S['oev'].n + 1
    S['oev'].n += 1
    eb.em('vector', lambda e: e.reciprocal(sums8[:, :], sums8[:, :]))
    eb.em('vector', lambda e: e.drain().then_inc(S['oev'].h, 1))
    if debug:
        eb.wait('sync', S['oev'], rec_done)
        S['out'].n += 16
        eb.em('sync', lambda e: e.dma_start(out=dbg['sums'], in_=sums8[:, :]).then_inc(S['out'].h, 16))
    norm_base = S['oev'].n
    NBNK = [psC0, psC1, psD0, psD1]
    for h in range(NH):
        eb.wait('sync', S['oev'], rec_done if h == 0 else norm_base + 4 * h)
        S['srw'].n += 16
        eb.em('sync', lambda e, h=h: e.dma_start(out=srow[0:1, :], in_=sums8[h:h + 1, :]).then_inc(S['srw'].h, 16))
        eb.wait('tensor', S['srw'], S['srw'].n)
        for n in range(4):
            idx = h * 4 + n
            pb = NBNK[idx % 4]
            if idx >= 4:
                eb.wait('tensor', S['oev'], norm_base + idx - 3)
            S['avq'].n += 1
            eb.em('tensor', lambda e, n=n, pb=pb: e.matmul(
                pb[:, :], ones_f1[:, :], srow[0:1, n * 512:(n + 1) * 512],
                start=True, stop=True).then_inc(S['avq'].h, 1))
            eb.wait('vector', S['avq'], S['avq'].n)
            for m in range(2):
                sl = o_all[:, 2 * h + m, n * 512:(n + 1) * 512]
                ins = (lambda e, sl=sl, pb=pb: e.tensor_tensor(out=sl, in0=sl, in1=pb[:, :], op=ALU.mult))
                if m == 1:
                    S['oev'].n += 1
                    eb.em('vector', lambda e, ins=ins: ins(e).then_inc(S['oev'].h, 1))
                else:
                    eb.em('vector', ins)
    eb.wait('tensor', S['wo16'], S['wo16'].n)
    eb.wait('tensor', S['oev'], S['oev'].n)
    wo_base = S['oev'].n
    for m in range(2):
        DS['xt'].issue(eb, 'sync', xt_buf[:, m % 2, :], xT_own[m * 128:(m + 1) * 128, :])
    for m in range(8):
        for n in range(4):
            idx = 4 * m + n
            pb = [psC0, psC1][idx % 2]
            if idx >= 2:
                eb.wait('tensor', S['oev'], wo_base + idx - 1)
            for kc in range(16):
                ins = (lambda e, m=m, n=n, kc=kc, pb=pb: e.matmul(
                    pb[:, :], wo_b[:, kc, m * 128:(m + 1) * 128], o_all[:, kc, n * 512:(n + 1) * 512],
                    start=(kc == 0), stop=(kc == 15)))
                if kc == 15:
                    S['avq'].n += 1
                    eb.em('tensor', lambda e, ins=ins: ins(e).then_inc(S['avq'].h, 1))
                else:
                    eb.em('tensor', ins)
            eb.wait('vector', S['avq'], S['avq'].n)
            DS['xt'].wait_for(eb, 'vector', m)
            if m >= 2 and n == 0:
                DS['rsd'].wait_for(eb, 'vector', m - 2)
            S['oev'].n += 1
            eb.em('vector', lambda e, m=m, n=n, pb=pb: e.tensor_tensor(
                out=rstage[:, m % 2, n * 512:(n + 1) * 512], in0=xt_buf[:, m % 2, n * 512:(n + 1) * 512],
                in1=pb[:, :], op=ALU.add).then_inc(S['oev'].h, 1))
        eb.wait('sync', S['oev'], wo_base + 4 * (m + 1))
        DS['rsd'].issue(eb, 'sync', resid_scr[m * 128:(m + 1) * 128, :], rstage[:, m % 2, :])
        if m + 2 < 8:
            DS['xt'].issue(eb, 'sync', xt_buf[:, m % 2, :], xT_own[(m + 2) * 128:(m + 3) * 128, :])
    if debug:
        DS['rsd'].wait_all(eb, 'sync')

    DS['wqs'].wait_all(eb, 'sync')
    DS['rsd'].wait_all(eb, 'sync')
    barrier(eb, 'b45')
    with nc.Block() as blk:
        eb.flush(blk)

    if stop_after <= 4:
        eb2 = EB(nc)
        eb2.wait('sync', S['out'], S['out'].n)
        with nc.Block() as blk:
            eb2.flush(blk)
        return nc
    es_b4.close()
    es_b34.close()
    es_wq.close()

    es_b5 = contextlib.ExitStack()
    PL5 = {k: sb("PL5" + k, [128, 1408], F32, es_b5) for k in ('pA', 'pC', 'pD', 'pK')}
    xqT_sb = sb("xqT_sb", [128, KC, TC], BF16, es_b5)
    wdq_b = sb("wdq_b", [128, 11, 1024], BF16, es_b5)
    gs_c = sb("gs_c", [128, 2, 512], F32, es_b5)
    h_t = sb("h_t", [128, 1408], F32, es_b5)
    hq_t = sb("hq_t", [128, 2, 1408], F32, es_b5)
    hqT_b = sb("hqT_b", [128, 11, 1024], BF16, es_b5)
    rbuf = sb("rbuf", [128, 2, 1024], F32, es_b5)
    obuf = sb("obuf", [128, 2, 1024], F32, es_b5)

    eb = EB(nc)
    barrier_wait(eb, 'b45')
    eb.em('scalar', lambda e: e.preload_act_table(AF.Silu))
    S['xqr'].n += 16
    eb.em('sync', lambda e: e.dma_start(out=xqT_sb[:], in_=xqT_scr.rearrange("(c p) q -> p c q", p=128)).then_inc(S['xqr'].h, 16))
    for it in range(11):
        S['fw2'].n += 16
        eb.em('scalar', lambda e, it=it: e.dma_start(
            out=wdq_b[:, it, :], in_=WdqN[0:1024, it * 128:(it + 1) * 128], transpose=True).then_inc(S['fw2'].h, 16))

    IG = [0, 1408]
    NCH = [(0, 512), (512, 512), (1024, 384)]
    ftr_glob = 0
    lgq = 0
    pend_tr = None

    def emit_tr(eb, lg, tt, hq_tgt):
        nonlocal ftr_glob
        if ftr_glob % 24 == 0 and ftr_glob > 0:
            eb.wait('scalar', S['avq'], S['avq'].n)
        eb.wait('tensor', S['hq'], hq_tgt)
        for b_ in range(3):
            js = [4, 4, 3][b_]
            pbk = SBK[ftr_glob % 2]
            if ftr_glob >= 2:
                eb.wait('tensor', S['fftrev'], ftr_glob - 1)
            for j in range(js):
                c = b_ * 4 + j
                ins = (lambda e, c=c, j=j, pbk=pbk, lg=lg: e.transpose(
                    pbk[:, j * 128:(j + 1) * 128], hq_t[:, lg % 2, c * 128:(c + 1) * 128], ident_t[:]))
                if j == js - 1:
                    S['fftr'].n += 1
                    eb.em('tensor', lambda e, ins=ins: ins(e).then_inc(S['fftr'].h, 1))
                else:
                    eb.em('tensor', ins)
            eb.wait('scalar', S['fftr'], S['fftr'].n)
            S['fftrev'].n += 1
            dst = hqT_b[:, b_ * 4:b_ * 4 + js, tt * 128:(tt + 1) * 128]
            src = pbk[:, 0:js * 128].rearrange("p (c n) -> p c n", c=js)
            eb.em('scalar', lambda e, dst=dst, src=src: e.copy(dst, src).then_inc(S['fftrev'].h, 1))
            ftr_glob += 1
    PAIR3 = [(psD0, psD1), (psE, psF), (psC0, psC1)]
    need_ffdn = [False]
    pend_down = None

    def mk_down(ig, half, toff, hbld_tgt):
        def go():
            dn_base = S['ffdn'].n
            eb.wait('tensor', S['fw2'], (11 if ig == 0 else 22) * 16)
            st_ds = DS['rsd'] if ig == 0 else DS['ost']
            rr_base_i = DS['rr'].cnt
            st_base_i = st_ds.cnt
            for m in range(2):
                DS['rr'].issue(eb, 'sync', rbuf[:, m % 2, :], resid_scr[m * 128:(m + 1) * 128, toff:toff + 1024])
            eb.wait('tensor', S['fftrev'], 24 * (2 * ig + half + 1))
            eb.wait('tensor', S['hbld'], hbld_tgt)
            DS['rsd'].wait_all(eb, 'vector')
            DS['ost'].wait_all(eb, 'vector')
            for m in range(8):
                for n2 in range(2):
                    idx = m * 2 + n2
                    pb = [psC0, psC1][idx % 2]
                    if idx >= 2:
                        eb.wait('tensor', S['ffdn'], dn_base + idx - 1)
                    for it in range(11):
                        ins = (lambda e, m=m, n2=n2, it=it, pb=pb: e.matmul(
                            pb[:, :], wdq_b[:, it, m * 128:(m + 1) * 128], hqT_b[:, it, n2 * 512:(n2 + 1) * 512],
                            start=(it == 0), stop=(it == 10)))
                        if it == 10:
                            S['avq'].n += 1
                            eb.em('tensor', lambda e, ins=ins: ins(e).then_inc(S['avq'].h, 1))
                        else:
                            eb.em('tensor', ins)
                    eb.wait('vector', S['avq'], S['avq'].n)
                    DS['rr'].wait_for(eb, 'vector', rr_base_i + m)
                    if m >= 2 and n2 == 0:
                        st_ds.wait_for(eb, 'vector', st_base_i + m - 2)
                    S['ffdn'].n += 1
                    eb.em('vector', lambda e, m=m, n2=n2, pb=pb: e.tensor_tensor(
                        out=obuf[:, m % 2, n2 * 512:(n2 + 1) * 512], in0=rbuf[:, m % 2, n2 * 512:(n2 + 1) * 512],
                        in1=pb[:, :], op=ALU.add).then_inc(S['ffdn'].h, 1))
                eb.wait('sync', S['ffdn'], dn_base + 2 * (m + 1))
                if ig == 0:
                    st_ds.issue(eb, 'sync', resid_scr[m * 128:(m + 1) * 128, toff:toff + 1024], obuf[:, m % 2, :])
                else:
                    st_ds.issue(eb, 'sync', o_out[m * 128:(m + 1) * 128, toff:toff + 1024], obuf[:, m % 2, :])
                if m + 2 < 8:
                    DS['rr'].issue(eb, 'sync', rbuf[:, m % 2, :], resid_scr[(m + 2) * 128:(m + 3) * 128, toff:toff + 1024])
            if ig == 0:
                DS['rsd'].wait_all(eb, 'sync')
                if half == 1:
                    eb.wait('scalar', S['avq'], S['avq'].n)
                    for it in range(11):
                        S['fw2'].n += 16
                        eb.em('scalar', lambda e, it=it: e.dma_start(
                            out=wdq_b[:, it, :], in_=WdqN[0:1024, 1408 + it * 128:1408 + (it + 1) * 128], transpose=True).then_inc(S['fw2'].h, 16))
            need_ffdn[0] = True
        return go

    for ig in range(2):
        i0 = IG[ig]
        eb.wait('tensor', S['fw'], S['fw'].n if ig == 1 else 16 * 16)
        eb.wait('tensor', S['xqr'], S['xqr'].n)
        for half in range(2):
            toff = half * 1024
            hoisted = (half == 1)
            for tt in range(8):
                eb.wait('vector', S['aab'], S['aab'].n)
                for ci, (c0, clen) in enumerate(NCH):
                    cg = S['hbld'].n
                    if hoisted and tt == 0:
                        pg, py = [PAIR3[0], PAIR3[1], PAIR3[0]][ci]
                        tgt = cg - 1
                    else:
                        pg, py = PAIR3[cg % 3]
                        tgt = cg - 1 if (hoisted and tt == 1) else cg - 2
                    if pg is psC0 and need_ffdn[0]:
                        eb.wait('tensor', S['ffdn'], S['ffdn'].n)
                        need_ffdn[0] = False
                    if tgt > 0:
                        eb.wait('tensor', S['hbld'], tgt)
                    for kc2 in range(KC):
                        eb.em('tensor', lambda e, tt=tt, c0=c0, clen=clen, kc2=kc2, pg=pg, toff=toff: e.matmul(
                            pg[:, 0:clen], xqT_sb[:, kc2, toff + tt * 128:toff + (tt + 1) * 128],
                            wgq_b[:, kc2, c0:c0 + clen], start=(kc2 == 0), stop=(kc2 == KC - 1)))
                    for kc2 in range(KC):
                        ins = (lambda e, tt=tt, c0=c0, clen=clen, kc2=kc2, py=py, toff=toff: e.matmul(
                            py[:, 0:clen], xqT_sb[:, kc2, toff + tt * 128:toff + (tt + 1) * 128],
                            wuq_b[:, kc2, c0:c0 + clen], start=(kc2 == 0), stop=(kc2 == KC - 1)))
                        if kc2 == KC - 1:
                            S['ffh'].n += 1
                            eb.em('tensor', lambda e, ins=ins: ins(e).then_inc(S['ffh'].h, 1))
                        else:
                            eb.em('tensor', ins)
                    gb = S['hbld'].n % 2
                    eb.wait('scalar', S['ffh'], S['ffh'].n)
                    if S['hbld'].n > 1:
                        eb.wait('scalar', S['hbld'], S['hbld'].n - 1)
                    S['sil'].n += 1
                    eb.em('scalar', lambda e, clen=clen, pg=pg, gb=gb: e.activation(
                        out=gs_c[:, gb, 0:clen], in_=pg[:, 0:clen], func=AF.Silu).then_inc(S['sil'].h, 1))
                    eb.wait('vector', S['sil'], S['sil'].n)
                    S['hbld'].n += 1
                    eb.em('vector', lambda e, c0=c0, clen=clen, py=py, gb=gb: e.tensor_tensor(
                        out=h_t[:, c0:c0 + clen], in0=gs_c[:, gb, 0:clen], in1=py[:, 0:clen],
                        op=ALU.mult).then_inc(S['hbld'].h, 1))
                lg = lgq
                eb.wait('scalar', S['hq'], S['hq'].n)
                eb.wait('scalar', S['hbld'], S['hbld'].n)
                eb.wait('vector', S['hbld'], S['hbld'].n)
                if lg >= 2:
                    eb.wait('vector', S['fftr'], 3 * (lg - 1))
                if pend_tr is not None:
                    pend_tr()
                    pend_tr = None
                last = emit_quant(eb, 1408, h_t[:, 0:1408], hq_t[:, lg % 2, 0:1408], PL5, act_abs=True)
                S['hq'].n += 1
                eb.em('vector', lambda e, last=last: last(e).then_inc(S['hq'].h, 1))
                pend_tr = (lambda lg=lg, tt=tt, hq_tgt=S['hq'].n: emit_tr(eb, lg, tt, hq_tgt))
                lgq += 1
                if hoisted and tt == 0 and pend_down is not None:
                    pend_down()
                    pend_down = None
            if not hoisted:
                pend_down = mk_down(ig, half, toff, S['hbld'].n)
                continue
            if pend_tr is not None:
                pend_tr()
                pend_tr = None
            if ig == 0:
                eb.wait('scalar', S['ffh'], S['ffh'].n)
                for kc2 in range(KC):
                    S['fw'].n += 16
                    eb.em('scalar', lambda e, kc2=kc2: e.dma_start(
                        out=wgq_b[:, kc2, :], in_=WgqN[1408:2816, kc2 * 128:(kc2 + 1) * 128], transpose=True).then_inc(S['fw'].h, 16))
                for kc2 in range(KC):
                    S['fw'].n += 16
                    eb.em('scalar', lambda e, kc2=kc2: e.dma_start(
                        out=wuq_b[:, kc2, :], in_=WuqN[1408:2816, kc2 * 128:(kc2 + 1) * 128], transpose=True).then_inc(S['fw'].h, 16))
            mk_down(ig, half, toff, S['hbld'].n)()
    DS['ost'].wait_all(eb, 'sync')
    eb.wait('sync', S['out'], S['out'].n)
    with nc.Block() as blk:
        eb.flush(blk)
    es_b5.close()
    es_w5.close()
    es_ps.close()
    es_sh.close()
    return nc


def make_inputs(inputs):
    x = np.asarray(inputs["L0_attn_x"], np.float32)
    Wq = np.asarray(inputs["L0_attn_Wq"], np.float32)
    Wkv = np.asarray(inputs["L0_attn_Wkv_down"], np.float32)
    Wku = np.asarray(inputs["L0_attn_Wk_up"], np.float32)
    Wvu = np.asarray(inputs["L0_attn_Wv_up"], np.float32)
    Wo = np.asarray(inputs["L0_attn_Wo"], np.float32)
    Wg = np.ascontiguousarray(np.asarray(inputs["L0_ffn_w_gate"], np.float32))
    Wu = np.ascontiguousarray(np.asarray(inputs["L0_ffn_w_up"], np.float32))
    Wd = np.ascontiguousarray(np.asarray(inputs["L0_ffn_w_down"], np.float32))
    shared = dict(
        WqT=np.ascontiguousarray(Wq.T), WkvT=np.ascontiguousarray(Wkv.T),
        WkuT=np.ascontiguousarray(Wku.T), WvuT=np.ascontiguousarray(Wvu.T),
        WoT=np.ascontiguousarray(Wo.T), Wg=Wg, Wu=Wu, Wd=Wd,
        identD=np.eye(128, dtype=np.float32),
    )
    in_maps = []
    for c in range(8):
        b, hf = c // 2, c % 2
        own = x[b, hf * TC:(hf + 1) * TC]
        oth = x[b, (1 - hf) * TC:(2 - hf) * TC]
        m = dict(shared)
        m["x_tok"] = np.ascontiguousarray(np.concatenate([own, oth], 0))
        m["xT_own"] = np.ascontiguousarray(own.T)
        in_maps.append(m)
    return in_maps


def kernel(**inputs):
    in_maps = make_inputs(inputs)
    nc = build_program(debug=False)
    res = run_bass_kernel_spmd(nc, in_maps, core_ids=list(range(8)))
    out = np.empty((4, TB, D), np.float32)
    for c in range(8):
        b, hf = c // 2, c % 2
        out[b, hf * TC:(hf + 1) * TC] = np.asarray(res.results[c]["o_out"], np.float32).T
    return out
